# version 25
# baseline (speedup 1.0000x reference)
import math
import contextlib
import numpy as np
import ml_dtypes

import concourse.bass as bass
import concourse.mybir as mybir

F32 = mybir.dt.float32
BF16 = mybir.dt.bfloat16
AF = mybir.ActivationFunctionType
ALU = mybir.AluOpType
AX = mybir.AxisListType

D = 1024
DC = 32
DHR = 32
H = 8
DK = 128
E = 8
FF = 2048
EPS = 1e-5
SC = 1.0 / math.sqrt(DK + DHR)

BF = ml_dtypes.bfloat16

SEM_NAMES = ["qc", "qk", "x0", "x1", "qo",
             "wg0", "wg1", "wg2", "wp0", "wp1", "wp2",
             "wo0", "wo1", "wo2", "wo3", "wo4", "wo5", "wo6", "wo7",
             "tS", "vS", "sS"]


class Walk:

    def __init__(self, en, px, sems):
        self.en = en
        self.px = px
        self.sems = sems
        self.c = {k: 0 for k in sems}
        self.seen = {}

    _SELF = {"vector": "vS", "scalar": "sS"}

    def do(self, eng, fn, inc=None, k=1):
        if eng == self.en:
            sw = self._SELF.get(eng)
            if sw is not None and self.c[sw] > 0:
                self.px.wait_ge(self.sems[sw], self.c[sw])
            inst = fn(self.px)
            if inc is not None:
                inst.then_inc(self.sems[inc], k)
        if inc is not None:
            self.c[inc] += k
            return self.c[inc]
        return None

    def tok(self, sem):
        return self.c[sem]

    def wait(self, eng, sem, v):
        if v is None or v <= 0:
            return
        if eng == self.en:
            key = sem
            if self.seen.get(key, -1) >= v:
                return
            self.seen[key] = v
            self.px.wait_ge(self.sems[sem], v)


def build_nc(T):
    assert T % 512 == 0
    NT = T // 128
    NTB = max(1, T // 1024)
    TB = T // NTB
    NTT = TB // 128
    NHALF = NTT // 4
    assert NTT % 4 == 0
    N5 = T // 512

    nc = bass.Bass("TRN2", target_bir_lowering=False)

    dx = nc.dram_tensor("x", [T, D], F32, kind="ExternalInput").ap()
    dy = nc.dram_tensor("y", [T, D], F32, kind="ExternalOutput").ap()
    dw = {}
    for nm, shp in [("wdq", [D, DC]), ("wdkv", [D, DC]), ("wkr", [D, DHR]),
                    ("wuq", [DC, H * DK]), ("wuk", [DC, H * DK]), ("wuv", [DC, H * DK]),
                    ("wqr", [DC, H * DHR]), ("wo_attn", [H * DK, D]),
                    ("rw", [E, D]), ("wg", [E, D, FF]), ("wp", [E, D, FF]), ("wexp", [E, FF, D])]:
        dw[nm] = nc.dram_tensor(nm, shp, F32, kind="ExternalInput").ap()
    for nm, shp in [("bdq", [DC]), ("bdkv", [DC]), ("bkr", [DHR]),
                    ("buq", [H * DK]), ("buk", [H * DK]), ("buv", [H * DK]),
                    ("bqr", [H * DHR]), ("bo", [D]), ("rb", [E]), ("temp", [1])]:
        dw[nm] = nc.dram_tensor(nm, shp, F32, kind="ExternalInput").ap()
    for nm, shp, dt in [("eyeb", [128, 128], BF16), ("eyef", [8, 8], F32),
                        ("onesr", [1, 128], F32), ("onescb", [128, 1], BF16),
                        ("onescf", [128, 1], F32), ("onesm", [128, 128], BF16)]:
        dw[nm] = nc.dram_tensor(nm, shp, dt, kind="ExternalInput").ap()

    with contextlib.ExitStack() as stack:
        en = stack.enter_context
        S = {}

        def sb(nm, shp, dt):
            S[nm] = en(nc.sbuf_tensor("s_" + nm, shp, dt))
            return S[nm]

        sb("eyeb", [128, 128], BF16); sb("eyef8", [8, 8], F32)
        sb("onesr", [1, 128], F32); sb("onescb", [128, 1], BF16); sb("onescf", [128, 1], F32)
        sb("onesm", [128, 128], BF16)
        sb("wcat", [128, 8, 96], BF16)
        sb("bdq", [DC, 1], F32); sb("bdkv", [DC, 1], F32); sb("bkr", [DHR, 1], F32)
        sb("wuq", [DC, H * DK], BF16); sb("wuk", [DC, H * DK], BF16)
        sb("wuv", [DC, H * DK], BF16); sb("wqr", [DC, H * DHR], BF16)
        sb("buq", [128, H], F32); sb("buk", [128, H], F32); sb("bqr", [DHR, H], F32)
        sb("bUVb", [128, D], BF16); sb("borow", [1, D], BF16)
        sb("rwT", [128, 8, E], BF16)
        sb("rbc", [E, 1], F32); sb("tempc", [1, 4], F32)
        sb("rcol", [E, 1], F32); sb("rbs", [E, 1], F32)
        sb("resid", [128, NT, D], F32)
        sb("xnT", [128, 8, T], BF16)
        sb("areaA", [128, max(8 * T, 16 * TB)], BF16)
        sb("areaC", [128, 8 * 1024], BF16)
        sb("wgbuf", [128, 3, 8, 128], BF16)
        sb("wpbuf", [128, 3, 8, 128], BF16)
        sb("areaB", [128, max(7168, 4 * T + 1536)], BF16)
        sb("statc", [128, 2, 4, 7], F32)
        sb("cq", [DC, T], BF16); sb("ckv", [DC, T], BF16); sb("kr", [DHR, T], BF16)
        sb("r_sb", [E, 2, 512], F32)
        sb("la", [128, NT * 8], F32)
        sb("gex", [128, NT * 8], F32)
        sb("gt", [128, NT * 8], F32)
        sb("gt2", [128, NT * 8], F32)
        sb("w_tm", [128, NT * 8], F32)
        sb("grow", [1, NT * 8], F32)
        sb("lse8", [1, 2 * E], F32)
        sb("epsc", [128, 1], F32)
        gl = nc.sbuf_bytes_remaining
        psum8 = en(nc.psum_tensor("psum8", [128, 4096], F32))
        sems = {s: en(nc.semaphore("sem_" + s)) for s in SEM_NAMES}
        block = en(nc.Block())

        oT = S["areaA"].ap()[:, 0:8 * T].rearrange("p (c t) -> p c t", c=8)
        hbuf = S["areaA"].ap()[:, 0:16 * TB].rearrange("p (f t) -> p f t", f=16)
        WO = S["areaC"].ap().rearrange("p (c d) -> p c d", c=8)
        wobuf = S["areaC"].ap().rearrange("p (s d) -> p s d", s=8)
        pb = psum8.ap()
        pbf = psum8.ap().bitcast(BF16)

        def bank(b, w=512):
            return pb[:, b * 512: b * 512 + w]

        def bankb(b, w=1024):
            return pbf[:, b * 1024: b * 1024 + w]

        rzv = S["r_sb"].ap()[0:1, 0, :]
        rzbv = S["wcat"].ap().rearrange("p a b -> p (a b)")[:, 0:512]
        aB = S["areaB"].ap()
        rowbuf = aB[0:1, 2048:4096].bitcast(F32)
        xnbf = aB[:, 4096:6144].rearrange("p (s d) -> p s d", s=2)
        sqout = aB[:, 6144:7168]
        qc_h = aB[:, 0:T]
        kc_h = aB[:, T:2 * T]
        v_h = aB[:, 2 * T:3 * T].rearrange("p (a b) -> p a b", b=DK)
        qr_h = aB[0:DHR, 3 * T:4 * T]
        Ebuf = aB[:, 4 * T:4 * T + 1536].rearrange("p (s d) -> p s d", s=3)
        rw8 = aB[0:E, 0:1024]
        lav = S["la"].ap().rearrange("p (a b) -> p a b", b=8)
        gexv = S["gex"].ap().rearrange("p (a b) -> p a b", b=8)
        gtv = S["gt"].ap().rearrange("p (a b) -> p a b", b=8)
        wv = S["w_tm"].ap().rearrange("p (a b) -> p a b", b=8)

        def walk(w):
            tok = {}
            wg_uses = {0: 0, 1: 0, 2: 0}
            wp_uses = {0: 0, 1: 0, 2: 0}
            wgp_ci = {"i": 0}
            wgp_emitted = set()

            def emit_wgp(tb_, e_, ft_):
                if (tb_, e_, ft_) in wgp_emitted:
                    return
                wgp_emitted.add((tb_, e_, ft_))
                gs = wgp_ci["i"] % 3
                wgp_ci["i"] += 1
                w.wait("gpsimd", "tS", tok.get(("wg_slot_prev", gs)))
                w.do("gpsimd", lambda q, e2=e_, ft2=ft_, gs2=gs: q.dma_start(
                    out=S["wgbuf"].ap()[:, gs2, :, :],
                    in_=dw["wg"][e2, :, :].rearrange("(c p) f -> p c f", p=128)
                        [:, :, ft2 * 128:(ft2 + 1) * 128]),
                    inc="wg%d" % gs, k=16)
                wg_uses[gs] += 1
                w.wait("gpsimd", "tS", tok.get(("wp_slot_prev", gs)))
                w.do("gpsimd", lambda q, e2=e_, ft2=ft_, gs2=gs: q.dma_start(
                    out=S["wpbuf"].ap()[:, gs2, :, :],
                    in_=dw["wp"][e2, :, :].rearrange("(c p) f -> p c f", p=128)
                        [:, :, ft2 * 128:(ft2 + 1) * 128]),
                    inc="wp%d" % gs, k=16)
                wp_uses[gs] += 1
                tok[("wgp_slot_of", tb_, e_, ft_)] = (gs, wg_uses[gs], wp_uses[gs])
            wo_uses = {s: 0 for s in range(8)}

            gp_emitted = set()
            gp_state = {"gp": 0}
            rot_state = {"n": 0}

            def emit_gp_group(tb, e, ft, tt2, fill=False):
                if (tb, e, ft, tt2) in gp_emitted:
                    return
                gp_emitted.add((tb, e, ft, tt2))
                emit_wgp(tb, e, ft)
                gs, wgu, wpu = tok[("wgp_slot_of", tb, e, ft)]
                gp_count = gp_state["gp"]
                if fill:
                    gb, pbk = 6, 7
                    w.wait("tensor", "vS", max(tok.get("fill_prev_v", 0),
                                               tok.get(("bc_v", 6), 0),
                                               tok.get(("bc_v", 7), 0),
                                               tok[("trev_v", (tt2 * 4 + 3) * 8 + 7)]))
                    w.wait("tensor", "sS", tok.get("fill_prev_s", 0))
                else:
                    par = rot_state["n"] % 2
                    gb, pbk = 2 * par, 2 * par + 1
                    first2 = rot_state["n"] - rot_state.get("tb0", 0) < 2
                    rot_state["last_first2"] = first2
                    needv = max(tok.get(("h_v", gp_count - 2), 0),
                                tok.get(("sttb", gb), 0),
                                tok.get(("sttb", pbk), 0),
                                tok.get("psum_vfree", 0) if first2 else 0)
                    w.wait("tensor", "vS", needv)
                    w.wait("tensor", "sS", max(tok.get(("relu_s", gp_count - 2), 0),
                                               tok.get("psum_sfree", 0) if first2 else 0))
                    rot_state["n"] += 1
                w.wait("tensor", "wg%d" % gs, 16 * wgu)
                for c in range(8):
                    w.do("tensor", lambda t, c_=c, tb_=tb, tt2_=tt2, gs_=gs, b_=gb: t.matmul(
                        bank(b_)[:, :], S["wgbuf"].ap()[:, gs_, c_, :],
                        S["xnT"].ap()[:, c_, tb_ * TB + tt2_ * 512: tb_ * TB + (tt2_ + 1) * 512],
                        start=(c_ == 0), stop=(c_ == 7)),
                        inc="tS" if c == 7 else None)
                tok[("g_t", gp_count)] = w.tok("tS")
                w.wait("tensor", "wp%d" % gs, 16 * wpu)
                for c in range(8):
                    w.do("tensor", lambda t, c_=c, tb_=tb, tt2_=tt2, gs_=gs, b_=pbk: t.matmul(
                        bank(b_)[:, :], S["wpbuf"].ap()[:, gs_, c_, :],
                        S["xnT"].ap()[:, c_, tb_ * TB + tt2_ * 512: tb_ * TB + (tt2_ + 1) * 512],
                        start=(c_ == 0), stop=(c_ == 7)),
                        inc="tS" if c == 7 else None)
                tok[("p_t", gp_count)] = w.tok("tS")
                if tt2 == TB // 512 - 1:
                    tok[("wg_slot_prev", gs)] = tok[("g_t", gp_count)]
                    tok[("wp_slot_prev", gs)] = tok[("p_t", gp_count)]
                w.wait("scalar", "tS", tok[("g_t", gp_count)])
                w.wait("scalar", "vS", tok.get(("h_v", gp_count - 2), 0))
                w.do("scalar", lambda s, b_=gb, sl_=gp_count % 2: s.activation(
                    out=Ebuf[:, sl_, :], in_=bank(b_)[:, :], func=AF.Relu), inc="sS")
                tok[("relu_s", gp_count)] = w.tok("sS")
                w.wait("vector", "sS", tok[("relu_s", gp_count)])
                w.wait("vector", "tS", tok[("p_t", gp_count)])
                if not fill and tok.get("area_a_tfree") and rot_state.get("last_first2"):
                    w.wait("vector", "tS", tok["area_a_tfree"])
                w.do("vector", lambda v, b_=pbk, sl_=gp_count % 2, ft_=ft, tt2_=tt2, tb_=tb: v.tensor_tensor(
                    hbuf[:, ft_, tt2_ * 512:(tt2_ + 1) * 512],
                    Ebuf[:, sl_, :], bank(b_)[:, :], ALU.mult), inc="vS")
                tok[("h_v", gp_count)] = w.tok("vS")
                tok[("h_ft_v", tb, e, ft)] = w.tok("vS")
                tok[("gpbank_v", pbk)] = w.tok("vS")
                tok[("gpbank_s", gb)] = tok[("relu_s", gp_count)]
                if fill:
                    tok["fill_prev_v"] = w.tok("vS")
                    tok["fill_prev_s"] = tok[("relu_s", gp_count)]
                gp_state["gp"] = gp_count + 1

            for i0 in range(2):
                w.do("sync", lambda q, i_=i0: q.dma_start(
                    out=S["resid"].ap()[:, i_, :], in_=dx[i_ * 128:(i_ + 1) * 128, :]),
                    inc="x0" if i0 % 2 == 0 else "x1", k=16)
            nconst = 0
            for dst, src in [
                (S["eyeb"][:], dw["eyeb"]), (S["eyef8"][:], dw["eyef"]),
                (S["onesr"][:], dw["onesr"]), (S["onescb"][:], dw["onescb"]),
                (S["onescf"][:], dw["onescf"]), (S["onesm"][:], dw["onesm"]),
                (rowbuf, dw["buv"].rearrange("(o d) -> o d", o=1)),
            ]:
                w.do("sync", lambda q, d_=dst, s_=src: q.dma_start(out=d_, in_=s_), inc="qc", k=16)
                nconst += 1
            for dst, src in [
                (S["bdq"][:], dw["bdq"].rearrange("(p o) -> p o", o=1)),
                (S["bdkv"][:], dw["bdkv"].rearrange("(p o) -> p o", o=1)),
                (S["bkr"][:], dw["bkr"].rearrange("(p o) -> p o", o=1)),
                (S["rbc"][:], dw["rb"].rearrange("(p o) -> p o", o=1)),
                (S["tempc"][:, 0:1], dw["temp"].rearrange("(p o) -> p o", o=1)),
                (S["buq"][:], dw["buq"].rearrange("(h p) -> p h", p=128)),
                (S["buk"][:], dw["buk"].rearrange("(h p) -> p h", p=128)),
                (S["bqr"][:], dw["bqr"].rearrange("(h p) -> p h", p=DHR)),
            ]:
                w.do("sync", lambda q, d_=dst, s_=src: q.dma_start(
                    out=d_, in_=s_, allow_slow_non_contiguous=True), inc="qc", k=16)
                nconst += 1
            tok["consts"] = w.tok("qc")

            ncast = 0
            cast_loads = [
                (S["wcat"].ap()[:, :, 0:32], dw["wdq"].rearrange("(c p) m -> p c m", p=128)),
                (S["wcat"].ap()[:, :, 32:64], dw["wdkv"].rearrange("(c p) m -> p c m", p=128)),
                (S["wcat"].ap()[:, :, 64:96], dw["wkr"].rearrange("(c p) m -> p c m", p=128)),
                (S["wuq"][:], dw["wuq"]), (S["wuk"][:], dw["wuk"]),
                (S["wuv"][:], dw["wuv"]), (S["wqr"][:], dw["wqr"]),
                (rw8, dw["rw"]),
                (S["borow"][:], dw["bo"].rearrange("(o d) -> o d", o=1)),
            ]
            for dst, src in cast_loads:
                w.do("gpsimd", lambda q, d_=dst, s_=src: q.dma_start(out=d_, in_=s_),
                     inc="qk", k=16)
                ncast += 1
            tok["casts"] = w.tok("qk")

            w.wait("tensor", "qc", tok["consts"])
            for dh in range(2):
                b_use = 6 + dh % 2
                w.do("tensor", lambda t, d_=dh, b_=b_use: t.matmul(
                    bank(b_)[:, :], S["onesr"][:], rowbuf[:, d_ * 512:(d_ + 1) * 512],
                    start=True, stop=True), inc="tS")
                tk = w.tok("tS")
                w.wait("vector", "tS", tk)
                w.do("vector", lambda v, d_=dh, b_=b_use: v.tensor_copy(
                    S["bUVb"][:, d_ * 512:(d_ + 1) * 512], bank(b_)[:, :]), inc="vS")
                tok[("bc_v", b_use)] = w.tok("vS")
            tok["buvb_t"] = w.tok("tS")
            tok["bcasts_v"] = w.tok("vS")

            w.do("vector", lambda v: v.memset(S["epsc"][:], EPS), inc="vS")
            tok["eps_v"] = w.tok("vS")
            w.wait("vector", "qc", tok["consts"])
            w.do("vector", lambda v: v.tensor_scalar_max(S["tempc"][:, 1:2], S["tempc"][:, 0:1], 0.1), inc="vS")
            w.do("vector", lambda v: v.reciprocal(S["tempc"][:, 2:3], S["tempc"][:, 1:2]), inc="vS")
            tok["temp_v"] = w.tok("vS")
            w.wait("tensor", "vS", tok["temp_v"])
            w.do("tensor", lambda t: t.matmul(pb[0:E, 5 * 512:5 * 512 + 1],
                                              S["onesr"][0:1, 0:E], S["tempc"][0:1, 2:3],
                                              start=True, stop=True), inc="tS")
            tok["rcol_t"] = w.tok("tS")
            w.wait("vector", "tS", tok["rcol_t"])
            w.do("vector", lambda v: v.tensor_copy(S["rcol"][:], pb[0:E, 5 * 512:5 * 512 + 1]), inc="vS")
            w.do("vector", lambda v: v.tensor_tensor(S["rbs"][:], S["rbc"][:], S["rcol"][:], ALU.mult), inc="vS")
            tok["rbs_v"] = w.tok("vS")

            w.wait("tensor", "qk", tok["casts"])
            for c in range(8):
                b = c % 2
                if c >= 2:
                    w.wait("tensor", "vS", tok[("rwT_v", c - 2)])
                w.do("tensor", lambda t, c_=c, b_=b: t.transpose(
                    bankb(b_)[:, 0:E], rw8[:, c_ * 128:(c_ + 1) * 128], S["eyeb"][0:E, 0:E]),
                    inc="tS")
                tk = w.tok("tS")
                w.wait("vector", "tS", tk)
                w.do("vector", lambda v, c_=c, b_=b: v.tensor_copy(
                    S["rwT"].ap()[:, c_, :], bankb(b_)[:, 0:E]), inc="vS")
                tok[("rwT_v", c)] = w.tok("vS")
            tok["rwT_done"] = w.tok("vS")

            bank01 = {0: 0, 1: 0}
            gidx = 0
            fill_state = {"k": 0}
            NB = NT // 4
            stw = S["statc"].ap()
            for B in range(NB):
                bp = B % 2
                if B >= 2:
                    w.wait("vector", "sS", tok[("resid_v", (B - 2) * 4 + 3)])
                for j in range(4):
                    i = B * 4 + j
                    sl = i % 2
                    xs = "x0" if sl == 0 else "x1"
                    if i >= 2:
                        w.wait("sync", "vS", tok[("s1_v", i - 2)])
                        w.do("sync", lambda q, i_=i: q.dma_start(
                            out=S["resid"].ap()[:, i_, :], in_=dx[i_ * 128:(i_ + 1) * 128, :]),
                            inc=xs, k=16)
                    xtile = S["resid"].ap()[:, i, :]
                    w.wait("vector", xs, 16 * (i // 2 + 1))
                    w.do("vector", lambda v, x_=xtile, j_=j: v.reduce_sum(
                        stw[:, bp, j_, 0:1], x_, axis=AX.X), inc="vS")
                    tok[("s1_v", i)] = w.tok("vS")
                    w.wait("scalar", xs, 16 * (i // 2 + 1))
                    if B >= 2 and j == 0:
                        w.wait("scalar", "vS", tok[("s1_v", i)])
                    w.do("scalar", lambda s, x_=xtile, j_=j: s.activation(
                        out=sqout, in_=x_, func=AF.Square, accum_out=stw[:, bp, j_, 1:2]), inc="sS")
                tok[("sq_s", B)] = w.tok("sS")
                w.wait("vector", "sS", tok[("sq_s", B)])
                w.do("vector", lambda v: v.tensor_scalar_mul(stw[:, bp, :, 2:3], stw[:, bp, :, 0:1], 1.0 / D), inc="vS")
                w.do("vector", lambda v: v.tensor_scalar_mul(stw[:, bp, :, 3:4], stw[:, bp, :, 1:2], 1.0 / D), inc="vS")
                w.do("vector", lambda v: v.tensor_tensor(stw[:, bp, :, 4:5], stw[:, bp, :, 2:3], stw[:, bp, :, 2:3], ALU.mult), inc="vS")
                w.do("vector", lambda v: v.tensor_tensor(stw[:, bp, :, 3:4], stw[:, bp, :, 3:4], stw[:, bp, :, 4:5], ALU.subtract), inc="vS")
                tok[("var_v", B)] = w.tok("vS")
                w.wait("scalar", "vS", tok[("var_v", B)])
                w.do("scalar", lambda s: s.activation(
                    out=stw[:, bp, :, 4:5], in_=stw[:, bp, :, 3:4], func=AF.Sqrt,
                    bias=S["epsc"][:]), inc="sS")
                tok[("sqrt_s", B)] = w.tok("sS")
                w.wait("vector", "sS", tok[("sqrt_s", B)])
                w.do("vector", lambda v: v.reciprocal(stw[:, bp, :, 5:6], stw[:, bp, :, 4:5]), inc="vS")
                w.do("vector", lambda v: v.tensor_tensor(stw[:, bp, :, 6:7], stw[:, bp, :, 2:3], stw[:, bp, :, 5:6], ALU.mult), inc="vS")
                w.do("vector", lambda v: v.tensor_scalar_mul(stw[:, bp, :, 6:7], stw[:, bp, :, 6:7], -1.0), inc="vS")
                w.do("vector", lambda v: v.tensor_scalar_add(stw[:, bp, :, 0:1], stw[:, bp, :, 5:6], 1.0), inc="vS")
                tok[("cols_v", B)] = w.tok("vS")
                w.wait("scalar", "vS", tok[("cols_v", B)])
                for j in range(4):
                    i = B * 4 + j
                    sl = i % 2
                    xtile = S["resid"].ap()[:, i, :]
                    if i >= 2:
                        w.wait("scalar", "tS", tok[("tr_t", i - 2)])
                    w.do("scalar", lambda s, x_=xtile, j_=j, sl_=sl: s.activation(
                        out=xnbf[:, sl_, :], in_=x_, func=AF.Identity,
                        bias=stw[:, bp, j_, 6:7], scale=stw[:, bp, j_, 5:6]), inc="sS")
                    tok[("xn_s", i)] = w.tok("sS")
                    if j < 2:
                        w.wait("vector", "sS", tok[("xn_s", i)])
                        w.do("vector", lambda v, x_=xtile, j_=j, i_=i: v.tensor_scalar(
                            S["resid"].ap()[:, i_, :], x_, stw[:, bp, j_, 0:1],
                            stw[:, bp, j_, 6:7], ALU.mult, ALU.add), inc="vS")
                        tok[("resid_v", i)] = ("vS", w.tok("vS"))
                        tok[("resid_v", i)] = w.tok("vS")
                        tok[("residv_is_v", i)] = True
                    else:
                        w.do("scalar", lambda s, x_=xtile, j_=j, i_=i: s.activation(
                            out=S["resid"].ap()[:, i_, :], in_=x_, func=AF.Identity,
                            bias=stw[:, bp, j_, 6:7], scale=stw[:, bp, j_, 0:1]), inc="sS")
                        tok[("resid_v", i)] = w.tok("sS")
                    w.wait("tensor", "sS", tok[("xn_s", i)])
                    for c in range(8):
                        b = (i * 8 + c) % 2
                        w.wait("tensor", "vS", bank01[b])
                        w.do("tensor", lambda t, c_=c, b_=b, sl_=sl: t.transpose(
                            bankb(b_)[:, 0:128], xnbf[:, sl_, c_ * 128:(c_ + 1) * 128],
                            S["eyeb"][:]), inc="tS")
                        tk = w.tok("tS")
                        w.wait("vector", "tS", tk)
                        w.do("vector", lambda v, c_=c, b_=b, i_=i: v.tensor_copy(
                            S["xnT"].ap()[:, c_, i_ * 128:(i_ + 1) * 128], bankb(b_)[:, 0:128]), inc="vS")
                        bank01[b] = w.tok("vS")
                        tok[("trev_v", i * 8 + c)] = w.tok("vS")
                    tok[("tr_t", i)] = w.tok("tS")
                i = B * 4 + 3
                tn = B
                xready = tok[("trev_v", i * 8 + 7)]
                if tn == 0:
                    w.wait("tensor", "qk", tok["casts"])
                    w.wait("tensor", "vS", tok["rwT_done"])
                w.wait("tensor", "vS", xready)
                for kind, off, bias, dst in [("cq", 0, "bdq", "cq"), ("ckv", 32, "bdkv", "ckv"),
                                             ("kr", 64, "bkr", "kr")]:
                    P = DC if kind != "kr" else DHR
                    b = 2 + gidx % 2
                    prev = tok.get(("cev_s", gidx - 2))
                    w.wait("tensor", "sS", prev)
                    for c in range(8):
                        w.do("tensor", lambda t, c_=c, b_=b, o_=off, tn_=tn, P_=P: t.matmul(
                            pb[0:P_, b_ * 512:(b_ + 1) * 512],
                            S["wcat"].ap()[:, c_, o_:o_ + P_],
                            S["xnT"].ap()[:, c_, tn_ * 512:(tn_ + 1) * 512],
                            start=(c_ == 0), stop=(c_ == 7)),
                            inc="tS" if c == 7 else None)
                    tk = w.tok("tS")
                    w.wait("scalar", "tS", tk)
                    w.do("scalar", lambda s, b_=b, tn_=tn, P_=P, bi=bias, d_=dst: s.activation(
                        out=S[d_][:, tn_ * 512:(tn_ + 1) * 512],
                        in_=pb[0:P_, b_ * 512:(b_ + 1) * 512],
                        func=AF.Identity, bias=S[bi][:]), inc="sS")
                    tok[("cev_s", gidx)] = w.tok("sS")
                    gidx += 1
                b = 4 + tn % 2
                prev = tok.get(("rev_s", tn - 2))
                w.wait("tensor", "sS", prev)
                if tn < 2:
                    w.wait("tensor", "vS", tok["rbs_v"])
                for c in range(8):
                    w.do("tensor", lambda t, c_=c, b_=b, tn_=tn: t.matmul(
                        pb[0:E, b_ * 512:(b_ + 1) * 512],
                        S["rwT"].ap()[:, c_, :],
                        S["xnT"].ap()[:, c_, tn_ * 512:(tn_ + 1) * 512],
                        start=(c_ == 0), stop=(c_ == 7)),
                        inc="tS" if c == 7 else None)
                tk = w.tok("tS")
                w.wait("scalar", "tS", tk)
                w.wait("scalar", "vS", tok["rbs_v"])
                w.do("scalar", lambda s, b_=b, tn_=tn: s.activation(
                    out=S["r_sb"].ap()[:, tn_ % 2, :],
                    in_=pb[0:E, b_ * 512:(b_ + 1) * 512],
                    func=AF.Identity, bias=S["rbs"][:], scale=S["rcol"][:]), inc="sS")
                tok[("rev_s", tn)] = w.tok("sS")
                w.wait("tensor", "sS", tok[("rev_s", tn)])
                for j in range(4):
                    ii = tn * 4 + j
                    b2 = ii % 2
                    w.wait("tensor", "vS", bank01[b2])
                    w.do("tensor", lambda t, tn_=tn, j_=j, b_=b2: t.transpose(
                        bank(b_, 8)[:, :], S["r_sb"].ap()[:, tn_ % 2, j_ * 128:(j_ + 1) * 128],
                        S["eyef8"][:]), inc="tS")
                    tk2 = w.tok("tS")
                    w.wait("vector", "tS", tk2)
                    w.do("vector", lambda v, i_=ii, b_=b2: v.tensor_copy(
                        lav[:, i_, :], bank(b_, 8)[:, :]), inc="vS")
                    bank01[b2] = w.tok("vS")
                    tok[("laev_v", ii)] = w.tok("vS")
                tok[("rtr_t", tn)] = w.tok("tS")
                for _ in range(2):
                    k = fill_state["k"]
                    ft_f, tt2_f = k // (TB // 512), k % (TB // 512)
                    if ft_f >= 16 or tt2_f > B or NTB < 2:
                        break
                    emit_gp_group(0, 0, ft_f, tt2_f, fill=True)
                    fill_state["k"] += 1
            tok["xnT_v"] = w.tok("vS")
            tok["c_done_s"] = w.tok("sS")
            tok["r_done_s"] = w.tok("sS")
            tok["laev_last"] = w.tok("vS")

            NE = NT * 8
            gate_steps = []
            def _gs(fn):
                gate_steps.append(fn)
            for it in range(3):
                def stepA(it_=it):
                    w.wait("scalar", "vS", w.tok("vS"))
                    w.do("scalar", lambda s: s.activation(out=S["gex"][:], in_=S["la"][:], func=AF.Exp), inc="sS")
                    sk = w.tok("sS")
                    w.wait("vector", "sS", sk)
                    w.do("vector", lambda v: v.reduce_sum(gtv[:, :, 0:1], gexv, axis=AX.X), inc="vS")
                    vk = w.tok("vS")
                    w.wait("scalar", "vS", vk)
                    w.do("scalar", lambda s: s.activation(out=S["gt2"][:, 0:NT], in_=gtv[:, :, 0:1].squeeze(2),
                                                          func=AF.Ln), inc="sS")
                    sk = w.tok("sS")
                    w.wait("vector", "sS", sk)
                    w.do("vector", lambda v: v.tensor_tensor(
                        lav, lav, S["gt2"].ap()[:, 0:NT].unsqueeze(2).broadcast_to([128, NT, 8]),
                        ALU.subtract), inc="vS")
                    vk = w.tok("vS")
                    w.wait("scalar", "vS", vk)
                    w.do("scalar", lambda s: s.activation(out=S["gex"][:], in_=S["la"][:], func=AF.Exp), inc="sS")
                    tok[("sink_exp2", it_)] = w.tok("sS")
                _gs(stepA)

                def stepB(it_=it):
                    w.wait("tensor", "sS", tok[("sink_exp2", it_)])
                    w.wait("tensor", "vS", tok.get(("lse_v", it_ - 1)))
                    w.do("tensor", lambda t: t.matmul(pb[0:1, 6 * 512:6 * 512 + NE],
                                                      S["onescf"][:], S["gex"][:],
                                                      start=True, stop=True), inc="tS")
                    tk = w.tok("tS")
                    w.wait("vector", "tS", tk)
                    w.do("vector", lambda v: v.reduce_sum(
                        S["lse8"][0:1, 0:E],
                        pb[0:1, 6 * 512:6 * 512 + NE].rearrange("o (a b) -> o b a", b=8),
                        axis=AX.X), inc="vS")
                    vk = w.tok("vS")
                    w.wait("scalar", "vS", vk)
                    w.do("scalar", lambda s: s.activation(out=S["lse8"][0:1, E:2 * E], in_=S["lse8"][0:1, 0:E], func=AF.Ln), inc="sS")
                    sk = w.tok("sS")
                    w.wait("vector", "sS", sk)
                    w.do("vector", lambda v: v.tensor_copy(
                        S["grow"].ap().rearrange("o (a b) -> o a b", b=8),
                        S["lse8"][0:1, E:2 * E].unsqueeze(1).broadcast_to([1, NT, 8])), inc="vS")
                    tok[("sink_rep", it_)] = w.tok("vS")
                _gs(stepB)

                def stepC(it_=it):
                    w.wait("tensor", "vS", tok[("sink_rep", it_)])
                    w.do("tensor", lambda t: t.matmul(pb[:, 7 * 512:7 * 512 + NE],
                                                      S["onesr"][:], S["grow"][:],
                                                      start=True, stop=True), inc="tS")
                    tk = w.tok("tS")
                    w.wait("vector", "tS", tk)
                    w.do("vector", lambda v: v.tensor_tensor(
                        S["la"][:], S["la"][:], pb[:, 7 * 512:7 * 512 + NE], ALU.subtract), inc="vS")
                    tok[("lse_v", it_)] = w.tok("vS")
                _gs(stepC)

            def stepFin():
              w.wait("scalar", "vS", w.tok("vS"))
              w.do("scalar", lambda s: s.activation(out=S["gex"][:], in_=S["la"][:], func=AF.Exp), inc="sS")
              sk = w.tok("sS")
              w.wait("vector", "sS", sk)
              w.do("vector", lambda v: v.reduce_sum(gtv[:, :, 0:1], gexv, axis=AX.X), inc="vS")
              w.do("vector", lambda v: v.tensor_scalar_add(gtv[:, :, 1:2], gtv[:, :, 0:1], 1e-8), inc="vS")
              w.do("vector", lambda v: v.reciprocal(gtv[:, :, 2:3], gtv[:, :, 1:2]), inc="vS")
              w.do("vector", lambda v: v.tensor_tensor(
                  gexv, gexv, gtv[:, :, 2:3].broadcast_to([128, NT, 8]), ALU.mult), inc="vS")
              g4 = S["gex"].ap().rearrange("p (a b c) -> p a b c", b=4, c=2)
              hi = S["gt"].ap().rearrange("p (a b) -> p a b", b=8)[:, :, 0:4]
              lo = S["gt"].ap().rearrange("p (a b) -> p a b", b=8)[:, :, 4:8]
              w.do("vector", lambda v: v.reduce_max(hi, g4, axis=AX.X), inc="vS")
              w.do("vector", lambda v: v.tensor_reduce(lo, g4, axis=AX.X, op=ALU.min), inc="vS")
              t2 = S["gt2"].ap().rearrange("p (a b) -> p a b", b=8)
              w.do("vector", lambda v: v.tensor_tensor(t2[:, :, 0:2], hi[:, :, 0:4:2], hi[:, :, 1:4:2], ALU.max), inc="vS")
              w.do("vector", lambda v: v.tensor_tensor(t2[:, :, 2:4], lo[:, :, 0:4:2], hi[:, :, 1:4:2], ALU.max), inc="vS")
              w.do("vector", lambda v: v.tensor_tensor(t2[:, :, 4:6], lo[:, :, 1:4:2], hi[:, :, 0:4:2], ALU.max), inc="vS")
              w.do("vector", lambda v: v.tensor_tensor(t2[:, :, 6:8], t2[:, :, 2:4], t2[:, :, 4:6], ALU.min), inc="vS")
              w.do("vector", lambda v: v.tensor_tensor(gtv[:, :, 0:1], t2[:, :, 0:1], t2[:, :, 1:2], ALU.max), inc="vS")
              w.do("vector", lambda v: v.tensor_tensor(gtv[:, :, 3:4], t2[:, :, 6:7], t2[:, :, 1:2], ALU.max), inc="vS")
              w.do("vector", lambda v: v.tensor_tensor(gtv[:, :, 4:5], t2[:, :, 7:8], t2[:, :, 0:1], ALU.max), inc="vS")
              w.do("vector", lambda v: v.tensor_tensor(gtv[:, :, 1:2], gtv[:, :, 3:4], gtv[:, :, 4:5], ALU.min), inc="vS")
              w.do("vector", lambda v: v.tensor_tensor(gtv[:, :, 5:6], gtv[:, :, 0:1], gtv[:, :, 1:2], ALU.add), inc="vS")
              w.do("vector", lambda v: v.tensor_scalar_add(gtv[:, :, 5:6], gtv[:, :, 5:6], 1e-8), inc="vS")
              w.do("vector", lambda v: v.reciprocal(gtv[:, :, 6:7], gtv[:, :, 5:6]), inc="vS")
              w.do("vector", lambda v: v.tensor_tensor(
                  wv, gexv, gtv[:, :, 1:2].broadcast_to([128, NT, 8]), ALU.is_ge), inc="vS")
              w.do("vector", lambda v: v.tensor_tensor(wv, wv, gexv, ALU.mult), inc="vS")
              w.do("vector", lambda v: v.tensor_tensor(
                  wv, wv, gtv[:, :, 6:7].broadcast_to([128, NT, 8]), ALU.mult), inc="vS")
              tok["w_v"] = w.tok("vS")
            _gs(stepFin)


            gp_state = {"gp": 0}

            def do_ffn_tb(tb, filler=None):
                rot_state["tb0"] = rot_state["n"]
                for e in range(E):
                    for ft in range(16):
                        for tt2 in range(TB // 512):
                            emit_gp_group(tb, e, ft, tt2)
                        if filler:
                            filler.pop(0)()
                    tok[("gp_done_v", tb, e)] = w.tok("vS")

                    for half in range(NHALF):
                        for f in range(16):
                            if half == 0 and f < 2:
                                if e + 1 < E:
                                    emit_wgp(tb, e + 1, f)
                                elif tb + 1 < NTB:
                                    emit_wgp(tb + 1, 0, f)
                            ws = f % 8
                            prevuse = max(tok.get(("wo_slot_prev", ws), 0),
                                          tok.get("areaC_tfree", 0))
                            w.wait("gpsimd", "tS", prevuse)
                            w.do("gpsimd", lambda q, e_=e, f_=f, ws_=ws: q.dma_start(
                                out=wobuf[:, ws_, :],
                                in_=dw["wexp"][e_, f_ * 128:(f_ + 1) * 128, :]),
                                inc="wo%d" % ws, k=16)
                            wo_uses[ws] += 1
                            w.wait("tensor", "wo%d" % ws, 16 * wo_uses[ws])
                            w.wait("tensor", "vS", tok.get(("h_ft_v", tb, e, f), 0))
                            if f == 0 and tb == 0 and e == 0:
                                w.wait("tensor", "vS", tok["w_v"])
                            dh_order = (0, 1) if f == 15 else (1, 0)
                            for dh in dh_order:
                                for ttl in range(4):
                                    bk = dh * 4 + ttl
                                    if f == 0:
                                        w.wait("tensor", "vS", tok.get(("sttb", bk), 0))
                                        if dh == 0:
                                            w.wait("tensor", "vS", tok.get(("gpbank_v", bk), 0))
                                            w.wait("tensor", "sS", tok.get(("gpbank_s", bk), 0))
                                    tt = half * 4 + ttl
                                    w.do("tensor", lambda t, f_=f, dh_=dh, tt_=tt, bk_=bk, ws_=ws, tb_=tb: t.matmul(
                                        bank(bk_)[:, :],
                                        hbuf[:, f_, tt_ * 128:(tt_ + 1) * 128],
                                        wobuf[:, ws_, dh_ * 512:(dh_ + 1) * 512],
                                        start=(f_ == 0), stop=(f_ == 15)),
                                        inc="tS" if (f == 15 and ttl == 3) else (
                                            "tS" if (f != 15 and dh == 0 and ttl == 3) else None))
                                if f == 15 and dh == 0:
                                    tok[("wo_mid_t", tb, e, half)] = w.tok("tS")
                            tok[("wo_slot_prev", ws)] = w.tok("tS")
                        tok[("woacc_t", tb, e, half)] = w.tok("tS")
                        w.wait("vector", "tS", tok[("wo_mid_t", tb, e, half)])
                        if e == 0 and tb == 0 and half == 0:
                            w.wait("vector", "vS", tok["w_v"])
                            w.wait("vector", "sS", tok[("resid_v", NT - 1)])
                        for phase2 in range(2):
                            if phase2 == 1:
                                w.wait("vector", "tS", tok[("woacc_t", tb, e, half)])
                            for ttl in range(4):
                                dh = phase2
                                bk = dh * 4 + ttl
                                gt_ = tb * NTT + half * 4 + ttl
                                w.do("vector", lambda v, bk_=bk, gt__=gt_, dh_=dh, e_=e: v.scalar_tensor_tensor(
                                    S["resid"].ap()[:, gt__, dh_ * 512:(dh_ + 1) * 512],
                                    bank(bk_)[:, :],
                                    S["w_tm"][:, gt__ * 8 + e_: gt__ * 8 + e_ + 1],
                                    S["resid"].ap()[:, gt__, dh_ * 512:(dh_ + 1) * 512],
                                    ALU.mult, ALU.add), inc="vS")
                                tok[("sttb", bk)] = w.tok("vS")
                        tok[("stt_v", tb, e, half)] = w.tok("vS")
                    tok["stt_prev_v"] = tok[("stt_v", tb, e, NHALF - 1)]
                    tok["bankfree_v"] = tok["stt_prev_v"]

            tok["psum_vfree"] = tok["laev_last"]
            tok["psum_sfree"] = tok["r_done_s"]
            do_ffn_tb(0, filler=list(gate_steps))

            w.wait("gpsimd", "tS", max(tok.get(("wo_slot_prev", s_), 0) for s_ in range(8)))
            w.do("gpsimd", lambda q: q.dma_start(
                out=WO, in_=dw["wo_attn"].rearrange("(c p) d -> p c d", p=128)),
                inc="qk", k=16)
            tok["wo_attn_qk"] = w.tok("qk")

            def flush_przb():
                pend = tok.get("pend")
                if pend is None:
                    return
                h_, qb_, qg_, ob_, rectok = pend
                w.wait("tensor", "vS", rectok)
                w.wait("tensor", "vS", tok.get("przb_cp"))
                w.do("tensor", lambda t: t.matmul(bank(4)[:, :], S["onesr"][:], rzv,
                                                  start=True, stop=True), inc="tS")
                tkp = w.tok("tS")
                w.wait("vector", "tS", tkp)
                w.do("vector", lambda v: v.tensor_copy(rzbv, bank(4)[:, :]), inc="vS")
                tok["przb_cp"] = w.tok("vS")
                w.do("vector", lambda v, h2=h_, qb2=qb_, ob2=ob_: v.tensor_tensor(
                    oT[:, h2, qb2 * 512:(qb2 + 1) * 512], bank(ob2)[:, :], rzbv,
                    ALU.mult), inc="vS")
                tok[("oev_v", qg_)] = w.tok("vS")
                tok["oev_last"] = w.tok("vS")
                tok["pend"] = None

            for h in range(H):
                pidx = 0
                w.wait("tensor", "sS", tok["r_done_s"])
                flushed = tok.get("pend") is not None and tok["pend"][2] % 2 == 0
                if flushed:
                    flush_przb()
                w.wait("tensor", "vS", max(tok.get(("oev_v", h * N5 - 2), 0),
                                           tok.get("stt_prev_v", 0),
                                           tok.get("oev_last", 0) if flushed else 0))

                def projchain(lhs_name, lhs_off, lhs_p, rhs_name, rhs_rows, bias_kind, bias_sl,
                              dst_fn, ev_eng, P_out):
                    nonlocal pidx
                    for tn in range(N5):
                        b = 2 + pidx % 2
                        prev = tok.get(("pj", h, pidx - 2))
                        w.wait("tensor", prev[0] if prev else "tS", prev[1] if prev else 0)
                        w.do("tensor", lambda t, tn_=tn, b_=b: t.matmul(
                            pb[0:P_out, b_ * 512:(b_ + 1) * 512],
                            S[lhs_name][:, lhs_off:lhs_off + lhs_p],
                            S[rhs_name][:, tn_ * 512:(tn_ + 1) * 512],
                            start=True, stop=True), inc="tS")
                        tk_ = w.tok("tS")
                        w.wait(ev_eng, "tS", tk_)
                        dst_fn(tn, b)
                        semn = "sS" if ev_eng == "scalar" else "vS"
                        tok[("pj", h, pidx)] = (semn, w.tok(semn))
                        pidx += 1

                def ev_kc(tn, b):
                    w.do("scalar", lambda s: s.activation(
                        out=kc_h[:, tn * 512:(tn + 1) * 512],
                        in_=pb[0:128, b * 512:(b + 1) * 512],
                        func=AF.Identity, bias=S["buk"][:, h:h + 1]), inc="sS")
                projchain("wuk", h * DK, DK, "ckv", DC, None, None, ev_kc, "scalar", 128)

                def ev_qc(tn, b):
                    w.do("scalar", lambda s: s.activation(
                        out=qc_h[:, tn * 512:(tn + 1) * 512],
                        in_=pb[0:128, b * 512:(b + 1) * 512],
                        func=AF.Identity, bias=S["buq"][:, h:h + 1]), inc="sS")
                projchain("wuq", h * DK, DK, "cq", DC, None, None, ev_qc, "scalar", 128)

                def ev_qr(tn, b):
                    w.do("scalar", lambda s: s.activation(
                        out=qr_h[:, tn * 512:(tn + 1) * 512],
                        in_=pb[0:DHR, b * 512:(b + 1) * 512],
                        func=AF.Identity, bias=S["bqr"][:, h:h + 1]), inc="sS")
                projchain("wqr", h * DHR, DHR, "cq", DC, None, None, ev_qr, "scalar", DHR)

                for tt in range(NT):
                    b = 2 + pidx % 2
                    prev = tok.get(("pj", h, pidx - 2))
                    w.wait("tensor", prev[0] if prev else "tS", prev[1] if prev else 0)
                    w.do("tensor", lambda t, tt_=tt, b_=b: t.matmul(
                        pb[:, b_ * 512:b_ * 512 + DK],
                        S["ckv"][:, tt_ * 128:(tt_ + 1) * 128],
                        S["wuv"][:, h * DK:(h + 1) * DK],
                        start=True, stop=True), inc="tS")
                    tk = w.tok("tS")
                    w.wait("vector", "tS", tk)
                    w.wait("vector", "vS", tok["bcasts_v"])
                    w.do("vector", lambda v, tt_=tt, b_=b: v.tensor_tensor(
                        v_h[:, tt_, :], pb[:, b_ * 512:b_ * 512 + DK],
                        S["bUVb"][:, h * DK:(h + 1) * DK], ALU.add), inc="vS")
                    tok[("pj", h, pidx)] = ("vS", w.tok("vS"))
                    pidx += 1
                tok[("proj_done", h)] = ("vS", w.tok("vS"), "sS", w.tok("sS"))

                for qb in range(N5):
                    qg = h * N5 + qb
                    p2 = qg % 2
                    ob = 2 if p2 == 0 else 6
                    zb = 3 if p2 == 0 else 7
                    w.wait("tensor", "vS", tok[("proj_done", h)][1])
                    w.wait("tensor", "sS", tok[("proj_done", h)][3])
                    for kt in range(NT + 2):
                        ko = kt - 2
                        if ko >= 0:
                            w.wait("tensor", "sS", tok[("exp_s", h, qb, ko)])
                        if kt == 0:
                            w.wait("tensor", "vS", tok.get(("oev_v", qg - 2),
                                                            tok.get("stt_prev_v", 0)))
                            w.wait("tensor", "sS", tok.get("exp_prev_last"))
                        if ko >= 0:
                            w.do("tensor", lambda t, kt_=ko, ob_=ob: t.matmul(
                                bank(ob_)[:, :], v_h[:, kt_, :], Ebuf[:, kt_ % 3, :],
                                start=(kt_ == 0), stop=(kt_ == NT - 1)),
                                inc="tS" if ko == NT - 1 else None)
                            w.do("tensor", lambda t, kt_=ko, zb_=zb: t.matmul(
                                bank(zb_)[:, :], S["onesm"][:], Ebuf[:, kt_ % 3, :],
                                start=(kt_ == 0), stop=(kt_ == NT - 1)),
                                inc="tS" if ko == NT - 1 else None)
                        if kt < NT:
                            b = (0, 1, 5)[kt % 3]
                            w.do("tensor", lambda t, kt_=kt, b_=b, qb_=qb: t.matmul(
                                bank(b_)[:, :], S["kr"][:, kt_ * 128:(kt_ + 1) * 128],
                                qr_h[:, qb_ * 512:(qb_ + 1) * 512],
                                start=True, stop=False))
                            w.do("tensor", lambda t, kt_=kt, b_=b, qb_=qb: t.matmul(
                                bank(b_)[:, :], kc_h[:, kt_ * 128:(kt_ + 1) * 128],
                                qc_h[:, qb_ * 512:(qb_ + 1) * 512],
                                start=False, stop=True), inc="tS")
                            tk = w.tok("tS")
                            w.wait("scalar", "tS", tk)
                            w.do("scalar", lambda s, kt_=kt, b_=b: s.activation(
                                out=Ebuf[:, kt_ % 3, :], in_=bank(b_)[:, :],
                                func=AF.Exp, scale=SC), inc="sS")
                            tok[("exp_s", h, qb, kt)] = w.tok("sS")
                        if kt == 5:
                            flush_przb()
                    tok[("oz_t", qg)] = w.tok("tS")
                    tok["exp_prev_last"] = tok[("exp_s", h, qb, NT - 1)]
                    w.wait("vector", "tS", tok[("oz_t", qg)])
                    w.do("vector", lambda v, zb_=zb: v.reciprocal(
                        rzv, pb[0:1, zb_ * 512:(zb_ + 1) * 512]), inc="vS")
                    tok["pend"] = (h, qb, qg, ob, w.tok("vS"))
            flush_przb()
            tok["attn_ov"] = w.tok("vS")
            tok["exp_all_s"] = w.tok("sS")

            widx = 0
            w.wait("tensor", "qk", tok["wo_attn_qk"])
            w.wait("vector", "sS", tok[("resid_v", NT - 1)])
            for qt in range(NT):
                for dh in range(2):
                    b = 5 + widx % 2
                    prev = tok.get(("woev_v", widx - 2))
                    w.wait("tensor", "vS", prev if prev else tok["attn_ov"])
                    for c in range(8):
                        w.do("tensor", lambda t, qt_=qt, dh_=dh, c_=c, b_=b: t.matmul(
                            bank(b_)[:, :], oT[:, c_, qt_ * 128:(qt_ + 1) * 128],
                            WO[:, c_, dh_ * 512:(dh_ + 1) * 512],
                            start=(c_ == 0), stop=False))
                    w.do("tensor", lambda t, dh_=dh, b_=b: t.matmul(
                        bank(b_)[:, :], S["onesm"][0:1, :],
                        S["borow"][0:1, dh_ * 512:(dh_ + 1) * 512],
                        start=False, stop=True), inc="tS")
                    tk = w.tok("tS")
                    w.wait("vector", "tS", tk)
                    w.do("vector", lambda v, qt_=qt, dh_=dh, b_=b: v.tensor_tensor(
                        S["resid"].ap()[:, qt_, dh_ * 512:(dh_ + 1) * 512],
                        S["resid"].ap()[:, qt_, dh_ * 512:(dh_ + 1) * 512],
                        bank(b_)[:, :], ALU.add), inc="vS")
                    tok[("woev_v", widx)] = w.tok("vS")
                    if widx == 2 * NTT - 1:
                        tok["woev_tb0_v"] = w.tok("vS")
                    widx += 1
            tok["WO_t"] = w.tok("tS")
            tok["WO_v"] = w.tok("vS")


            tok["psum_vfree"] = tok["WO_v"]
            tok["psum_sfree"] = tok["exp_all_s"]
            tok["area_a_tfree"] = tok["WO_t"]
            tok["areaC_tfree"] = tok["WO_t"]
            for tb2 in range(1, NTB):
                do_ffn_tb(tb2)
            w.wait("sync", "vS", max(tok["woev_tb0_v"], tok[("stt_v", 0, E - 1, NHALF - 1)]))
            w.do("sync", lambda q: q.dma_start(
                out=dy.rearrange("(i p) d -> p i d", p=128)[:, 0:NTT, :],
                in_=S["resid"].ap()[:, 0:NTT, :]),
                inc="qo", k=16)
            for tb2 in range(1, NTB):
                for hf in range(NHALF):
                    w.wait("sync", "vS", max(tok[("stt_v", tb2, E - 1, hf)], tok["WO_v"]))
                    w.do("sync", lambda q, tb_=tb2, hf_=hf: q.dma_start(
                        out=dy.rearrange("(i p) d -> p i d", p=128)
                            [:, tb_ * NTT + hf_ * 4:tb_ * NTT + hf_ * 4 + 4, :],
                        in_=S["resid"].ap()[:, tb_ * NTT + hf_ * 4:tb_ * NTT + hf_ * 4 + 4, :]),
                        inc="qo", k=16)
            w.wait("sync", "qo", 16 * (1 + (NTB - 1) * NHALF))

        finals = {}
        for name in ["sync", "gpsimd", "tensor", "vector", "scalar"]:
            deco = getattr(block, name)

            def mk(nm):
                def fn(px):
                    ww = Walk(nm, px, sems)
                    walk(ww)
                    finals[nm] = dict(ww.c)
                return fn
            deco(mk(name))
        ref_c = finals["sync"]
        for nm, c in finals.items():
            assert c == ref_c, (nm, {k: (c[k], ref_c[k]) for k in c if c[k] != ref_c[k]})

    return nc


def make_in_map(x_core, inputs):
    f32 = np.float32
    m = {"x": np.ascontiguousarray(np.asarray(x_core, f32))}
    key = {
        "wdq": "L0_attn_W_DQ", "bdq": "L0_attn_b_DQ", "wuq": "L0_attn_W_UQ", "buq": "L0_attn_b_UQ",
        "wdkv": "L0_attn_W_DKV", "bdkv": "L0_attn_b_DKV", "wuk": "L0_attn_W_UK", "buk": "L0_attn_b_UK",
        "wuv": "L0_attn_W_UV", "buv": "L0_attn_b_UV", "wqr": "L0_attn_W_QR", "bqr": "L0_attn_b_QR",
        "wkr": "L0_attn_W_KR", "bkr": "L0_attn_b_KR", "wo_attn": "L0_attn_W_O", "bo": "L0_attn_b_O",
        "rw": "L0_ffn_router_w", "rb": "L0_ffn_router_b", "temp": "L0_ffn_temperature",
        "wg": "L0_ffn_wg", "wp": "L0_ffn_wp", "wexp": "L0_ffn_wo",
    }
    for k, src in key.items():
        m[k] = np.ascontiguousarray(np.asarray(inputs[src], f32))
    m["eyeb"] = np.eye(128, dtype=BF)
    m["eyef"] = np.eye(8, dtype=f32)
    m["onesr"] = np.ones((1, 128), f32)
    m["onescb"] = np.ones((128, 1), BF)
    m["onescf"] = np.ones((128, 1), f32)
    m["onesm"] = np.ones((128, 128), BF)
    return m


def kernel(**inputs):
    from concourse.bass_utils import run_bass_kernel_spmd
    x = np.asarray(inputs["L0_attn_h_t"], np.float32)
    B, T, _ = x.shape
    nc = build_nc(T)
    in_maps = [make_in_map(x[b], inputs) for b in range(B)]
    res = run_bass_kernel_spmd(nc, in_maps, core_ids=list(range(B)))
    return np.stack([np.asarray(r["y"], np.float32) for r in res.results], axis=0)


# version 26
# speedup vs baseline: 1.0666x; 1.0666x over previous
import math
import contextlib
import numpy as np
import ml_dtypes

import concourse.bass as bass
import concourse.mybir as mybir

F32 = mybir.dt.float32
BF16 = mybir.dt.bfloat16
AF = mybir.ActivationFunctionType
ALU = mybir.AluOpType
AX = mybir.AxisListType

D = 1024
DC = 32
DHR = 32
H = 8
DK = 128
E = 8
FF = 2048
EPS = 1e-5
SC = 1.0 / math.sqrt(DK + DHR)

BF = ml_dtypes.bfloat16

SEM_NAMES = ["qc", "qk", "x0", "x1", "qo", "qtr",
             "wg0", "wg1", "wg2", "wp0", "wp1", "wp2",
             "wo0", "wo1", "wo2", "wo3", "wo4", "wo5", "wo6", "wo7",
             "tS", "vS", "sS"]


class Walk:

    def __init__(self, en, px, sems):
        self.en = en
        self.px = px
        self.sems = sems
        self.c = {k: 0 for k in sems}
        self.seen = {}

    _SELF = {"vector": "vS", "scalar": "sS"}

    def do(self, eng, fn, inc=None, k=1):
        if eng == self.en:
            sw = self._SELF.get(eng)
            if sw is not None and self.c[sw] > 0:
                self.px.wait_ge(self.sems[sw], self.c[sw])
            inst = fn(self.px)
            if inc is not None:
                inst.then_inc(self.sems[inc], k)
        if inc is not None:
            self.c[inc] += k
            return self.c[inc]
        return None

    def tok(self, sem):
        return self.c[sem]

    def wait(self, eng, sem, v):
        if v is None or v <= 0:
            return
        if eng == self.en:
            key = sem
            if self.seen.get(key, -1) >= v:
                return
            self.seen[key] = v
            self.px.wait_ge(self.sems[sem], v)


def build_nc(T):
    assert T % 512 == 0
    NT = T // 128
    NTB = max(1, T // 1024)
    TB = T // NTB
    NTT = TB // 128
    NHALF = NTT // 4
    assert NTT % 4 == 0
    N5 = T // 512

    nc = bass.Bass("TRN2", target_bir_lowering=False)

    dx = nc.dram_tensor("x", [T, D], F32, kind="ExternalInput").ap()
    dy = nc.dram_tensor("y", [T, D], F32, kind="ExternalOutput").ap()
    dw = {}
    for nm, shp in [("wdq", [D, DC]), ("wdkv", [D, DC]), ("wkr", [D, DHR]),
                    ("wuq", [DC, H * DK]), ("wuk", [DC, H * DK]), ("wuv", [DC, H * DK]),
                    ("wqr", [DC, H * DHR]), ("wo_attn", [H * DK, D]),
                    ("rw", [E, D]), ("wg", [E, D, FF]), ("wp", [E, D, FF]), ("wexp", [E, FF, D])]:
        dw[nm] = nc.dram_tensor(nm, shp, F32, kind="ExternalInput").ap()
    for nm, shp in [("bdq", [DC]), ("bdkv", [DC]), ("bkr", [DHR]),
                    ("buq", [H * DK]), ("buk", [H * DK]), ("buv", [H * DK]),
                    ("bqr", [H * DHR]), ("bo", [D]), ("rb", [E]), ("temp", [1])]:
        dw[nm] = nc.dram_tensor(nm, shp, F32, kind="ExternalInput").ap()
    for nm, shp, dt in [("eyeb", [128, 128], BF16), ("eyef", [8, 8], F32),
                        ("onesr", [1, 128], F32), ("onescb", [128, 1], BF16),
                        ("onescf", [128, 1], F32), ("onesm", [128, 128], BF16)]:
        dw[nm] = nc.dram_tensor(nm, shp, dt, kind="ExternalInput").ap()

    with contextlib.ExitStack() as stack:
        en = stack.enter_context
        S = {}

        def sb(nm, shp, dt):
            S[nm] = en(nc.sbuf_tensor("s_" + nm, shp, dt))
            return S[nm]

        sb("eyeb", [128, 128], BF16); sb("eyef8", [8, 8], F32)
        sb("onesr", [1, 128], F32); sb("onescb", [128, 1], BF16); sb("onescf", [128, 1], F32)
        sb("onesm", [128, 128], BF16)
        sb("wcat", [128, 8, 96], BF16)
        sb("bdq", [DC, 1], F32); sb("bdkv", [DC, 1], F32); sb("bkr", [DHR, 1], F32)
        sb("wuq", [DC, H * DK], BF16); sb("wuk", [DC, H * DK], BF16)
        sb("wuv", [DC, H * DK], BF16); sb("wqr", [DC, H * DHR], BF16)
        sb("buq", [128, H], F32); sb("buk", [128, H], F32); sb("bqr", [DHR, H], F32)
        sb("bUVb", [128, D], BF16); sb("borow", [1, D], BF16)
        sb("rwT", [128, 8, E], BF16)
        sb("rbc", [E, 1], F32); sb("tempc", [1, 4], F32)
        sb("rcol", [E, 1], F32); sb("rbs", [E, 1], F32)
        sb("resid", [128, NT, D], F32)
        sb("xnT", [128, 8, T], BF16)
        sb("areaA", [128, max(8 * T, 16 * TB)], BF16)
        sb("areaC", [128, 8 * 1024], BF16)
        sb("wgbuf", [128, 3, 8, 128], BF16)
        sb("wpbuf", [128, 3, 8, 128], BF16)
        sb("areaB", [128, max(7168, 4 * T + 1536)], BF16)
        sb("statc", [128, 2, 4, 7], F32)
        sb("cq", [DC, T], BF16); sb("ckv", [DC, T], BF16); sb("kr", [DHR, T], BF16)
        sb("r_sb", [E, 2, 512], F32)
        sb("la", [128, NT * 8], F32)
        sb("gex", [128, NT * 8], F32)
        sb("gt", [128, NT * 8], F32)
        sb("gt2", [128, NT * 8], F32)
        sb("w_tm", [128, NT * 8], F32)
        sb("grow", [1, NT * 8], F32)
        sb("lse8", [1, 2 * E], F32)
        sb("epsc", [128, 1], F32)
        gl = nc.sbuf_bytes_remaining
        psum8 = en(nc.psum_tensor("psum8", [128, 4096], F32))
        sems = {s: en(nc.semaphore("sem_" + s)) for s in SEM_NAMES}
        block = en(nc.Block())

        oT = S["areaA"].ap()[:, 0:8 * T].rearrange("p (c t) -> p c t", c=8)
        hbuf = S["areaA"].ap()[:, 0:16 * TB].rearrange("p (f t) -> p f t", f=16)
        WO = S["areaC"].ap().rearrange("p (c d) -> p c d", c=8)
        wobuf = S["areaC"].ap().rearrange("p (s d) -> p s d", s=8)
        pb = psum8.ap()
        pbf = psum8.ap().bitcast(BF16)

        def bank(b, w=512):
            return pb[:, b * 512: b * 512 + w]

        def bankb(b, w=1024):
            return pbf[:, b * 1024: b * 1024 + w]

        rzv = S["r_sb"].ap()[0:1, 0, :]
        rzbv = S["wcat"].ap().rearrange("p a b -> p (a b)")[:, 0:512]
        aB = S["areaB"].ap()
        rowbuf = aB[0:1, 2048:4096].bitcast(F32)
        xnbf = aB[:, 4096:6144].rearrange("p (s d) -> p s d", s=2)
        sqout = aB[:, 6144:7168]
        qc_h = aB[:, 0:T]
        kc_h = aB[:, T:2 * T]
        v_h = aB[:, 2 * T:3 * T].rearrange("p (a b) -> p a b", b=DK)
        qr_h = aB[0:DHR, 3 * T:4 * T]
        Ebuf = aB[:, 4 * T:4 * T + 1536].rearrange("p (s d) -> p s d", s=3)
        rw8 = aB[0:E, 0:1024]
        lav = S["la"].ap().rearrange("p (a b) -> p a b", b=8)
        gexv = S["gex"].ap().rearrange("p (a b) -> p a b", b=8)
        gtv = S["gt"].ap().rearrange("p (a b) -> p a b", b=8)
        wv = S["w_tm"].ap().rearrange("p (a b) -> p a b", b=8)

        def walk(w):
            tok = {}

            for i0 in range(2):
                w.do("sync", lambda q, i_=i0: q.dma_start(
                    out=S["resid"].ap()[:, i_, :], in_=dx[i_ * 128:(i_ + 1) * 128, :]),
                    inc="x0" if i0 % 2 == 0 else "x1", k=16)
            nconst = 0
            for dst, src in [
                (S["eyeb"][:], dw["eyeb"]), (S["eyef8"][:], dw["eyef"]),
                (S["onesr"][:], dw["onesr"]), (S["onescb"][:], dw["onescb"]),
                (S["onescf"][:], dw["onescf"]), (S["onesm"][:], dw["onesm"]),
                (rowbuf, dw["buv"].rearrange("(o d) -> o d", o=1)),
            ]:
                w.do("sync", lambda q, d_=dst, s_=src: q.dma_start(out=d_, in_=s_), inc="qc", k=16)
                nconst += 1
            for dst, src in [
                (S["bdq"][:], dw["bdq"].rearrange("(p o) -> p o", o=1)),
                (S["bdkv"][:], dw["bdkv"].rearrange("(p o) -> p o", o=1)),
                (S["bkr"][:], dw["bkr"].rearrange("(p o) -> p o", o=1)),
                (S["rbc"][:], dw["rb"].rearrange("(p o) -> p o", o=1)),
                (S["tempc"][:, 0:1], dw["temp"].rearrange("(p o) -> p o", o=1)),
                (S["buq"][:], dw["buq"].rearrange("(h p) -> p h", p=128)),
                (S["buk"][:], dw["buk"].rearrange("(h p) -> p h", p=128)),
                (S["bqr"][:], dw["bqr"].rearrange("(h p) -> p h", p=DHR)),
            ]:
                w.do("sync", lambda q, d_=dst, s_=src: q.dma_start(
                    out=d_, in_=s_, allow_slow_non_contiguous=True), inc="qc", k=16)
                nconst += 1
            tok["consts"] = w.tok("qc")

            ncast = 0
            cast_loads = [
                (S["wcat"].ap()[:, :, 0:32], dw["wdq"].rearrange("(c p) m -> p c m", p=128)),
                (S["wcat"].ap()[:, :, 32:64], dw["wdkv"].rearrange("(c p) m -> p c m", p=128)),
                (S["wcat"].ap()[:, :, 64:96], dw["wkr"].rearrange("(c p) m -> p c m", p=128)),
                (S["wuq"][:], dw["wuq"]), (S["wuk"][:], dw["wuk"]),
                (S["wuv"][:], dw["wuv"]), (S["wqr"][:], dw["wqr"]),
                (rw8, dw["rw"]),
                (S["borow"][:], dw["bo"].rearrange("(o d) -> o d", o=1)),
            ]
            for dst, src in cast_loads:
                w.do("gpsimd", lambda q, d_=dst, s_=src: q.dma_start(out=d_, in_=s_),
                     inc="qk", k=16)
                ncast += 1
            tok["casts"] = w.tok("qk")

            w.wait("tensor", "qc", tok["consts"])
            for dh in range(2):
                b_use = 6 + dh % 2
                w.do("tensor", lambda t, d_=dh, b_=b_use: t.matmul(
                    bank(b_)[:, :], S["onesr"][:], rowbuf[:, d_ * 512:(d_ + 1) * 512],
                    start=True, stop=True), inc="tS")
                tk = w.tok("tS")
                w.wait("vector", "tS", tk)
                w.do("vector", lambda v, d_=dh, b_=b_use: v.tensor_copy(
                    S["bUVb"][:, d_ * 512:(d_ + 1) * 512], bank(b_)[:, :]), inc="vS")
                tok[("bc_v", b_use)] = w.tok("vS")
            tok["buvb_t"] = w.tok("tS")
            tok["bcasts_v"] = w.tok("vS")

            w.do("vector", lambda v: v.memset(S["epsc"][:], EPS), inc="vS")
            tok["eps_v"] = w.tok("vS")
            w.wait("vector", "qc", tok["consts"])
            w.do("vector", lambda v: v.tensor_scalar_max(S["tempc"][:, 1:2], S["tempc"][:, 0:1], 0.1), inc="vS")
            w.do("vector", lambda v: v.reciprocal(S["tempc"][:, 2:3], S["tempc"][:, 1:2]), inc="vS")
            tok["temp_v"] = w.tok("vS")
            w.wait("tensor", "vS", tok["temp_v"])
            w.do("tensor", lambda t: t.matmul(pb[0:E, 5 * 512:5 * 512 + 1],
                                              S["onesr"][0:1, 0:E], S["tempc"][0:1, 2:3],
                                              start=True, stop=True), inc="tS")
            tok["rcol_t"] = w.tok("tS")
            w.wait("vector", "tS", tok["rcol_t"])
            w.do("vector", lambda v: v.tensor_copy(S["rcol"][:], pb[0:E, 5 * 512:5 * 512 + 1]), inc="vS")
            w.do("vector", lambda v: v.tensor_tensor(S["rbs"][:], S["rbc"][:], S["rcol"][:], ALU.mult), inc="vS")
            tok["rbs_v"] = w.tok("vS")

            w.wait("tensor", "qk", tok["casts"])
            for c in range(8):
                b = c % 2
                if c >= 2:
                    w.wait("tensor", "vS", tok[("rwT_v", c - 2)])
                w.do("tensor", lambda t, c_=c, b_=b: t.transpose(
                    bankb(b_)[:, 0:E], rw8[:, c_ * 128:(c_ + 1) * 128], S["eyeb"][0:E, 0:E]),
                    inc="tS")
                tk = w.tok("tS")
                w.wait("vector", "tS", tk)
                w.do("vector", lambda v, c_=c, b_=b: v.tensor_copy(
                    S["rwT"].ap()[:, c_, :], bankb(b_)[:, 0:E]), inc="vS")
                tok[("rwT_v", c)] = w.tok("vS")
            tok["rwT_done"] = w.tok("vS")

            bank01 = {0: 0, 1: 0}
            gidx = 0
            NB = NT // 4
            stw = S["statc"].ap()
            for B in range(NB):
                bp = B % 2
                if B >= 2:
                    w.wait("vector", "sS", tok[("resid_v", (B - 2) * 4 + 3)])
                for j in range(4):
                    i = B * 4 + j
                    sl = i % 2
                    xs = "x0" if sl == 0 else "x1"
                    if i >= 2:
                        w.wait("sync", "vS", tok[("s1_v", i - 2)])
                        w.do("sync", lambda q, i_=i: q.dma_start(
                            out=S["resid"].ap()[:, i_, :], in_=dx[i_ * 128:(i_ + 1) * 128, :]),
                            inc=xs, k=16)
                    xtile = S["resid"].ap()[:, i, :]
                    w.wait("vector", xs, 16 * (i // 2 + 1))
                    w.do("vector", lambda v, x_=xtile, j_=j: v.reduce_sum(
                        stw[:, bp, j_, 0:1], x_, axis=AX.X), inc="vS")
                    tok[("s1_v", i)] = w.tok("vS")
                    w.wait("scalar", xs, 16 * (i // 2 + 1))
                    if B >= 2 and j == 0:
                        w.wait("scalar", "vS", tok[("s1_v", i)])
                    w.do("scalar", lambda s, x_=xtile, j_=j: s.activation(
                        out=sqout, in_=x_, func=AF.Square, accum_out=stw[:, bp, j_, 1:2]), inc="sS")
                tok[("sq_s", B)] = w.tok("sS")
                w.wait("vector", "sS", tok[("sq_s", B)])
                w.do("vector", lambda v: v.tensor_scalar_mul(stw[:, bp, :, 2:3], stw[:, bp, :, 0:1], 1.0 / D), inc="vS")
                w.do("vector", lambda v: v.tensor_scalar_mul(stw[:, bp, :, 3:4], stw[:, bp, :, 1:2], 1.0 / D), inc="vS")
                w.do("vector", lambda v: v.tensor_tensor(stw[:, bp, :, 4:5], stw[:, bp, :, 2:3], stw[:, bp, :, 2:3], ALU.mult), inc="vS")
                w.do("vector", lambda v: v.tensor_tensor(stw[:, bp, :, 3:4], stw[:, bp, :, 3:4], stw[:, bp, :, 4:5], ALU.subtract), inc="vS")
                tok[("var_v", B)] = w.tok("vS")
                w.wait("scalar", "vS", tok[("var_v", B)])
                w.do("scalar", lambda s: s.activation(
                    out=stw[:, bp, :, 4:5], in_=stw[:, bp, :, 3:4], func=AF.Sqrt,
                    bias=S["epsc"][:]), inc="sS")
                tok[("sqrt_s", B)] = w.tok("sS")
                w.wait("vector", "sS", tok[("sqrt_s", B)])
                w.do("vector", lambda v: v.reciprocal(stw[:, bp, :, 5:6], stw[:, bp, :, 4:5]), inc="vS")
                w.do("vector", lambda v: v.tensor_tensor(stw[:, bp, :, 6:7], stw[:, bp, :, 2:3], stw[:, bp, :, 5:6], ALU.mult), inc="vS")
                w.do("vector", lambda v: v.tensor_scalar_mul(stw[:, bp, :, 6:7], stw[:, bp, :, 6:7], -1.0), inc="vS")
                w.do("vector", lambda v: v.tensor_scalar_add(stw[:, bp, :, 0:1], stw[:, bp, :, 5:6], 1.0), inc="vS")
                tok[("cols_v", B)] = w.tok("vS")
                w.wait("scalar", "vS", tok[("cols_v", B)])
                for j in range(4):
                    i = B * 4 + j
                    sl = i % 2
                    xtile = S["resid"].ap()[:, i, :]
                    if i >= 2:
                        w.wait("scalar", "qtr", 128 * (i - 1))
                    w.do("scalar", lambda s, x_=xtile, j_=j, sl_=sl: s.activation(
                        out=xnbf[:, sl_, :], in_=x_, func=AF.Identity,
                        bias=stw[:, bp, j_, 6:7], scale=stw[:, bp, j_, 5:6]), inc="sS")
                    tok[("xn_s", i)] = w.tok("sS")
                    if j < 2:
                        w.wait("vector", "sS", tok[("xn_s", i)])
                        w.do("vector", lambda v, x_=xtile, j_=j, i_=i: v.tensor_scalar(
                            S["resid"].ap()[:, i_, :], x_, stw[:, bp, j_, 0:1],
                            stw[:, bp, j_, 6:7], ALU.mult, ALU.add), inc="vS")
                        tok[("resid_v", i)] = ("vS", w.tok("vS"))
                        tok[("resid_v", i)] = w.tok("vS")
                        tok[("residv_is_v", i)] = True
                    else:
                        w.do("scalar", lambda s, x_=xtile, j_=j, i_=i: s.activation(
                            out=S["resid"].ap()[:, i_, :], in_=x_, func=AF.Identity,
                            bias=stw[:, bp, j_, 6:7], scale=stw[:, bp, j_, 0:1]), inc="sS")
                        tok[("resid_v", i)] = w.tok("sS")
                    w.wait("sync", "sS", tok[("xn_s", i)])
                    w.wait("sync", "qtr", 128 * i)
                    for c in range(8):
                        w.do("sync", lambda q, c_=c, i_=i, sl_=sl: q.dma_start_transpose(
                            S["xnT"].ap()[:, c_, i_ * 128:(i_ + 1) * 128],
                            xnbf[:, sl_, c_ * 128:(c_ + 1) * 128]), inc="qtr", k=16)
                i = B * 4 + 3
                tn = B
                xready = None
                if tn == 0:
                    w.wait("tensor", "qk", tok["casts"])
                    w.wait("tensor", "vS", tok["rwT_done"])
                w.wait("tensor", "qtr", 128 * (4 * (tn + 1)))
                for kind, off, bias, dst in [("cq", 0, "bdq", "cq"), ("ckv", 32, "bdkv", "ckv"),
                                             ("kr", 64, "bkr", "kr")]:
                    P = DC if kind != "kr" else DHR
                    b = 2 + gidx % 2
                    prev = tok.get(("cev_s", gidx - 2))
                    w.wait("tensor", "sS", prev)
                    for c in range(8):
                        w.do("tensor", lambda t, c_=c, b_=b, o_=off, tn_=tn, P_=P: t.matmul(
                            pb[0:P_, b_ * 512:(b_ + 1) * 512],
                            S["wcat"].ap()[:, c_, o_:o_ + P_],
                            S["xnT"].ap()[:, c_, tn_ * 512:(tn_ + 1) * 512],
                            start=(c_ == 0), stop=(c_ == 7)),
                            inc="tS" if c == 7 else None)
                    tk = w.tok("tS")
                    w.wait("scalar", "tS", tk)
                    w.do("scalar", lambda s, b_=b, tn_=tn, P_=P, bi=bias, d_=dst: s.activation(
                        out=S[d_][:, tn_ * 512:(tn_ + 1) * 512],
                        in_=pb[0:P_, b_ * 512:(b_ + 1) * 512],
                        func=AF.Identity, bias=S[bi][:]), inc="sS")
                    tok[("cev_s", gidx)] = w.tok("sS")
                    gidx += 1
                b = 4 + tn % 2
                prev = tok.get(("rev_s", tn - 2))
                w.wait("tensor", "sS", prev)
                if tn < 2:
                    w.wait("tensor", "vS", tok["rbs_v"])
                for c in range(8):
                    w.do("tensor", lambda t, c_=c, b_=b, tn_=tn: t.matmul(
                        pb[0:E, b_ * 512:(b_ + 1) * 512],
                        S["rwT"].ap()[:, c_, :],
                        S["xnT"].ap()[:, c_, tn_ * 512:(tn_ + 1) * 512],
                        start=(c_ == 0), stop=(c_ == 7)),
                        inc="tS" if c == 7 else None)
                tk = w.tok("tS")
                w.wait("scalar", "tS", tk)
                w.wait("scalar", "vS", tok["rbs_v"])
                w.do("scalar", lambda s, b_=b, tn_=tn: s.activation(
                    out=S["r_sb"].ap()[:, tn_ % 2, :],
                    in_=pb[0:E, b_ * 512:(b_ + 1) * 512],
                    func=AF.Identity, bias=S["rbs"][:], scale=S["rcol"][:]), inc="sS")
                tok[("rev_s", tn)] = w.tok("sS")
                w.wait("tensor", "sS", tok[("rev_s", tn)])
                for j in range(4):
                    ii = tn * 4 + j
                    b2 = ii % 2
                    w.wait("tensor", "vS", bank01[b2])
                    w.do("tensor", lambda t, tn_=tn, j_=j, b_=b2: t.transpose(
                        bank(b_, 8)[:, :], S["r_sb"].ap()[:, tn_ % 2, j_ * 128:(j_ + 1) * 128],
                        S["eyef8"][:]), inc="tS")
                    tk2 = w.tok("tS")
                    w.wait("vector", "tS", tk2)
                    w.do("vector", lambda v, i_=ii, b_=b2: v.tensor_copy(
                        lav[:, i_, :], bank(b_, 8)[:, :]), inc="vS")
                    bank01[b2] = w.tok("vS")
                    tok[("laev_v", ii)] = w.tok("vS")
                tok[("rtr_t", tn)] = w.tok("tS")
            tok["xnT_v"] = w.tok("vS")
            tok["c_done_s"] = w.tok("sS")
            tok["r_done_s"] = w.tok("sS")
            tok["laev_last"] = w.tok("vS")

            NE = NT * 8
            gate_steps = []
            def _gs(fn):
                gate_steps.append(fn)
            for it in range(3):
                def stepA(it_=it):
                    w.wait("scalar", "vS", w.tok("vS"))
                    w.do("scalar", lambda s: s.activation(out=S["gex"][:], in_=S["la"][:], func=AF.Exp), inc="sS")
                    sk = w.tok("sS")
                    w.wait("vector", "sS", sk)
                    w.do("vector", lambda v: v.reduce_sum(gtv[:, :, 0:1], gexv, axis=AX.X), inc="vS")
                    vk = w.tok("vS")
                    w.wait("scalar", "vS", vk)
                    w.do("scalar", lambda s: s.activation(out=S["gt2"][:, 0:NT], in_=gtv[:, :, 0:1].squeeze(2),
                                                          func=AF.Ln), inc="sS")
                    sk = w.tok("sS")
                    w.wait("vector", "sS", sk)
                    w.do("vector", lambda v: v.tensor_tensor(
                        lav, lav, S["gt2"].ap()[:, 0:NT].unsqueeze(2).broadcast_to([128, NT, 8]),
                        ALU.subtract), inc="vS")
                    vk = w.tok("vS")
                    w.wait("scalar", "vS", vk)
                    w.do("scalar", lambda s: s.activation(out=S["gex"][:], in_=S["la"][:], func=AF.Exp), inc="sS")
                    tok[("sink_exp2", it_)] = w.tok("sS")
                _gs(stepA)

                def stepB(it_=it):
                    w.wait("tensor", "sS", tok[("sink_exp2", it_)])
                    w.wait("tensor", "vS", tok.get(("lse_v", it_ - 1)))
                    w.do("tensor", lambda t: t.matmul(pb[0:1, 6 * 512:6 * 512 + NE],
                                                      S["onescf"][:], S["gex"][:],
                                                      start=True, stop=True), inc="tS")
                    tk = w.tok("tS")
                    w.wait("vector", "tS", tk)
                    w.do("vector", lambda v: v.reduce_sum(
                        S["lse8"][0:1, 0:E],
                        pb[0:1, 6 * 512:6 * 512 + NE].rearrange("o (a b) -> o b a", b=8),
                        axis=AX.X), inc="vS")
                    vk = w.tok("vS")
                    w.wait("scalar", "vS", vk)
                    w.do("scalar", lambda s: s.activation(out=S["lse8"][0:1, E:2 * E], in_=S["lse8"][0:1, 0:E], func=AF.Ln), inc="sS")
                    sk = w.tok("sS")
                    w.wait("vector", "sS", sk)
                    w.do("vector", lambda v: v.tensor_copy(
                        S["grow"].ap().rearrange("o (a b) -> o a b", b=8),
                        S["lse8"][0:1, E:2 * E].unsqueeze(1).broadcast_to([1, NT, 8])), inc="vS")
                    tok[("sink_rep", it_)] = w.tok("vS")
                _gs(stepB)

                def stepC(it_=it):
                    w.wait("tensor", "vS", tok[("sink_rep", it_)])
                    w.do("tensor", lambda t: t.matmul(pb[:, 7 * 512:7 * 512 + NE],
                                                      S["onesr"][:], S["grow"][:],
                                                      start=True, stop=True), inc="tS")
                    tk = w.tok("tS")
                    w.wait("vector", "tS", tk)
                    w.do("vector", lambda v: v.tensor_tensor(
                        S["la"][:], S["la"][:], pb[:, 7 * 512:7 * 512 + NE], ALU.subtract), inc="vS")
                    tok[("lse_v", it_)] = w.tok("vS")
                _gs(stepC)

            def stepFin():
              w.wait("scalar", "vS", w.tok("vS"))
              w.do("scalar", lambda s: s.activation(out=S["gex"][:], in_=S["la"][:], func=AF.Exp), inc="sS")
              sk = w.tok("sS")
              w.wait("vector", "sS", sk)
              w.do("vector", lambda v: v.reduce_sum(gtv[:, :, 0:1], gexv, axis=AX.X), inc="vS")
              w.do("vector", lambda v: v.tensor_scalar_add(gtv[:, :, 1:2], gtv[:, :, 0:1], 1e-8), inc="vS")
              w.do("vector", lambda v: v.reciprocal(gtv[:, :, 2:3], gtv[:, :, 1:2]), inc="vS")
              w.do("vector", lambda v: v.tensor_tensor(
                  gexv, gexv, gtv[:, :, 2:3].broadcast_to([128, NT, 8]), ALU.mult), inc="vS")
              g4 = S["gex"].ap().rearrange("p (a b c) -> p a b c", b=4, c=2)
              hi = S["gt"].ap().rearrange("p (a b) -> p a b", b=8)[:, :, 0:4]
              lo = S["gt"].ap().rearrange("p (a b) -> p a b", b=8)[:, :, 4:8]
              w.do("vector", lambda v: v.reduce_max(hi, g4, axis=AX.X), inc="vS")
              w.do("vector", lambda v: v.tensor_reduce(lo, g4, axis=AX.X, op=ALU.min), inc="vS")
              t2 = S["gt2"].ap().rearrange("p (a b) -> p a b", b=8)
              w.do("vector", lambda v: v.tensor_tensor(t2[:, :, 0:2], hi[:, :, 0:4:2], hi[:, :, 1:4:2], ALU.max), inc="vS")
              w.do("vector", lambda v: v.tensor_tensor(t2[:, :, 2:4], lo[:, :, 0:4:2], hi[:, :, 1:4:2], ALU.max), inc="vS")
              w.do("vector", lambda v: v.tensor_tensor(t2[:, :, 4:6], lo[:, :, 1:4:2], hi[:, :, 0:4:2], ALU.max), inc="vS")
              w.do("vector", lambda v: v.tensor_tensor(t2[:, :, 6:8], t2[:, :, 2:4], t2[:, :, 4:6], ALU.min), inc="vS")
              w.do("vector", lambda v: v.tensor_tensor(gtv[:, :, 0:1], t2[:, :, 0:1], t2[:, :, 1:2], ALU.max), inc="vS")
              w.do("vector", lambda v: v.tensor_tensor(gtv[:, :, 3:4], t2[:, :, 6:7], t2[:, :, 1:2], ALU.max), inc="vS")
              w.do("vector", lambda v: v.tensor_tensor(gtv[:, :, 4:5], t2[:, :, 7:8], t2[:, :, 0:1], ALU.max), inc="vS")
              w.do("vector", lambda v: v.tensor_tensor(gtv[:, :, 1:2], gtv[:, :, 3:4], gtv[:, :, 4:5], ALU.min), inc="vS")
              w.do("vector", lambda v: v.tensor_tensor(gtv[:, :, 5:6], gtv[:, :, 0:1], gtv[:, :, 1:2], ALU.add), inc="vS")
              w.do("vector", lambda v: v.tensor_scalar_add(gtv[:, :, 5:6], gtv[:, :, 5:6], 1e-8), inc="vS")
              w.do("vector", lambda v: v.reciprocal(gtv[:, :, 6:7], gtv[:, :, 5:6]), inc="vS")
              w.do("vector", lambda v: v.tensor_tensor(
                  wv, gexv, gtv[:, :, 1:2].broadcast_to([128, NT, 8]), ALU.is_ge), inc="vS")
              w.do("vector", lambda v: v.tensor_tensor(wv, wv, gexv, ALU.mult), inc="vS")
              w.do("vector", lambda v: v.tensor_tensor(
                  wv, wv, gtv[:, :, 6:7].broadcast_to([128, NT, 8]), ALU.mult), inc="vS")
              tok["w_v"] = w.tok("vS")
            _gs(stepFin)


            wg_uses = {0: 0, 1: 0, 2: 0}
            wp_uses = {0: 0, 1: 0, 2: 0}
            wgp_ci = {"i": 0}
            wgp_emitted = set()

            def emit_wgp(tb_, e_, ft_):
                if (tb_, e_, ft_) in wgp_emitted:
                    return
                wgp_emitted.add((tb_, e_, ft_))
                gs = wgp_ci["i"] % 3
                wgp_ci["i"] += 1
                w.wait("gpsimd", "tS", tok.get(("wg_slot_prev", gs)))
                w.do("gpsimd", lambda q, e2=e_, ft2=ft_, gs2=gs: q.dma_start(
                    out=S["wgbuf"].ap()[:, gs2, :, :],
                    in_=dw["wg"][e2, :, :].rearrange("(c p) f -> p c f", p=128)
                        [:, :, ft2 * 128:(ft2 + 1) * 128]),
                    inc="wg%d" % gs, k=16)
                wg_uses[gs] += 1
                w.wait("gpsimd", "tS", tok.get(("wp_slot_prev", gs)))
                w.do("gpsimd", lambda q, e2=e_, ft2=ft_, gs2=gs: q.dma_start(
                    out=S["wpbuf"].ap()[:, gs2, :, :],
                    in_=dw["wp"][e2, :, :].rearrange("(c p) f -> p c f", p=128)
                        [:, :, ft2 * 128:(ft2 + 1) * 128]),
                    inc="wp%d" % gs, k=16)
                wp_uses[gs] += 1
                tok[("wgp_slot_of", tb_, e_, ft_)] = (gs, wg_uses[gs], wp_uses[gs])
            wo_uses = {s: 0 for s in range(8)}
            gp_state = {"gp": 0}

            def do_ffn_tb(tb, filler=None):
                gp_count = gp_state["gp"]
                tb_gp0 = gp_count
                w.wait("tensor", "qtr", 128 * NT)
                for e in range(E):
                    for ft in range(16):
                        emit_wgp(tb, e, ft)
                        gs, wgu, wpu = tok[("wgp_slot_of", tb, e, ft)]
                        for tt2 in range(TB // 512):
                            par = gp_count % 2
                            gb, pbk = 2 * par, 2 * par + 1
                            needv = max(tok.get(("h_v", gp_count - 2), 0),
                                        tok.get(("sttb", gb), 0),
                                        tok.get(("sttb", pbk), 0),
                                        tok.get("psum_vfree", 0) if gp_count - tb_gp0 < 2 else 0)
                            w.wait("tensor", "vS", needv)
                            w.wait("tensor", "sS", tok.get(("relu_s", gp_count - 2),
                                                           tok.get("psum_sfree", 0)))
                            w.wait("tensor", "wg%d" % gs, 16 * wgu)
                            for c in range(8):
                                w.do("tensor", lambda t, c_=c, tb_=tb, tt2_=tt2, gs_=gs, b_=gb: t.matmul(
                                    bank(b_)[:, :], S["wgbuf"].ap()[:, gs_, c_, :],
                                    S["xnT"].ap()[:, c_, tb_ * TB + tt2_ * 512: tb_ * TB + (tt2_ + 1) * 512],
                                    start=(c_ == 0), stop=(c_ == 7)),
                                    inc="tS" if c == 7 else None)
                            tok[("g_t", gp_count)] = w.tok("tS")
                            w.wait("tensor", "wp%d" % gs, 16 * wpu)
                            for c in range(8):
                                w.do("tensor", lambda t, c_=c, tb_=tb, tt2_=tt2, gs_=gs, b_=pbk: t.matmul(
                                    bank(b_)[:, :], S["wpbuf"].ap()[:, gs_, c_, :],
                                    S["xnT"].ap()[:, c_, tb_ * TB + tt2_ * 512: tb_ * TB + (tt2_ + 1) * 512],
                                    start=(c_ == 0), stop=(c_ == 7)),
                                    inc="tS" if c == 7 else None)
                            tok[("p_t", gp_count)] = w.tok("tS")
                            if tt2 == TB // 512 - 1:
                                tok[("wg_slot_prev", gs)] = tok[("g_t", gp_count)]
                                tok[("wp_slot_prev", gs)] = tok[("p_t", gp_count)]
                            w.wait("scalar", "tS", tok[("g_t", gp_count)])
                            w.wait("scalar", "vS", tok.get(("h_v", gp_count - 2), 0))
                            w.do("scalar", lambda s, b_=gb, sl_=gp_count % 2: s.activation(
                                out=Ebuf[:, sl_, :], in_=bank(b_)[:, :], func=AF.Relu), inc="sS")
                            tok[("relu_s", gp_count)] = w.tok("sS")
                            w.wait("vector", "sS", tok[("relu_s", gp_count)])
                            w.wait("vector", "tS", tok[("p_t", gp_count)])
                            if gp_count - tb_gp0 < 2 and tok.get("area_a_tfree"):
                                w.wait("vector", "tS", tok["area_a_tfree"])
                            w.do("vector", lambda v, b_=pbk, sl_=gp_count % 2, ft_=ft, tt2_=tt2: v.tensor_tensor(
                                hbuf[:, ft_, tt2_ * 512:(tt2_ + 1) * 512],
                                Ebuf[:, sl_, :], bank(b_)[:, :], ALU.mult), inc="vS")
                            tok[("h_v", gp_count)] = w.tok("vS")
                            tok[("h_ft_v", tb, e, ft)] = w.tok("vS")
                            tok[("gpbank_v", pbk)] = w.tok("vS")
                            tok[("gpbank_s", gb)] = tok[("relu_s", gp_count)]
                            gp_count += 1
                        if filler:
                            filler.pop(0)()
                    tok[("gp_done_v", tb, e)] = w.tok("vS")

                    for half in range(NHALF):
                        for f in range(16):
                            if half == 0 and f < 2:
                                if e + 1 < E:
                                    emit_wgp(tb, e + 1, f)
                                elif tb + 1 < NTB:
                                    emit_wgp(tb + 1, 0, f)
                            ws = f % 8
                            prevuse = max(tok.get(("wo_slot_prev", ws), 0),
                                          tok.get("areaC_tfree", 0))
                            w.wait("gpsimd", "tS", prevuse)
                            w.do("gpsimd", lambda q, e_=e, f_=f, ws_=ws: q.dma_start(
                                out=wobuf[:, ws_, :],
                                in_=dw["wexp"][e_, f_ * 128:(f_ + 1) * 128, :]),
                                inc="wo%d" % ws, k=16)
                            wo_uses[ws] += 1
                            w.wait("tensor", "wo%d" % ws, 16 * wo_uses[ws])
                            w.wait("tensor", "vS", tok.get(("h_ft_v", tb, e, f), 0))
                            if f == 0 and tb == 0 and e == 0:
                                w.wait("tensor", "vS", tok["w_v"])
                            dh_order = (0, 1) if f == 15 else (1, 0)
                            for dh in dh_order:
                                for ttl in range(4):
                                    bk = dh * 4 + ttl
                                    if f == 0:
                                        w.wait("tensor", "vS", tok.get(("sttb", bk), 0))
                                        if dh == 0:
                                            w.wait("tensor", "vS", tok.get(("gpbank_v", bk), 0))
                                            w.wait("tensor", "sS", tok.get(("gpbank_s", bk), 0))
                                    tt = half * 4 + ttl
                                    w.do("tensor", lambda t, f_=f, dh_=dh, tt_=tt, bk_=bk, ws_=ws, tb_=tb: t.matmul(
                                        bank(bk_)[:, :],
                                        hbuf[:, f_, tt_ * 128:(tt_ + 1) * 128],
                                        wobuf[:, ws_, dh_ * 512:(dh_ + 1) * 512],
                                        start=(f_ == 0), stop=(f_ == 15)),
                                        inc="tS" if (f == 15 and ttl == 3) else (
                                            "tS" if (f != 15 and dh == 0 and ttl == 3) else None))
                                if f == 15 and dh == 0:
                                    tok[("wo_mid_t", tb, e, half)] = w.tok("tS")
                            tok[("wo_slot_prev", ws)] = w.tok("tS")
                        tok[("woacc_t", tb, e, half)] = w.tok("tS")
                        w.wait("vector", "tS", tok[("wo_mid_t", tb, e, half)])
                        if e == 0 and tb == 0 and half == 0:
                            w.wait("vector", "vS", tok["w_v"])
                            w.wait("vector", "sS", tok[("resid_v", NT - 1)])
                        for phase2 in range(2):
                            if phase2 == 1:
                                w.wait("vector", "tS", tok[("woacc_t", tb, e, half)])
                            for ttl in range(4):
                                dh = phase2
                                bk = dh * 4 + ttl
                                gt_ = tb * NTT + half * 4 + ttl
                                w.do("vector", lambda v, bk_=bk, gt__=gt_, dh_=dh, e_=e: v.scalar_tensor_tensor(
                                    S["resid"].ap()[:, gt__, dh_ * 512:(dh_ + 1) * 512],
                                    bank(bk_)[:, :],
                                    S["w_tm"][:, gt__ * 8 + e_: gt__ * 8 + e_ + 1],
                                    S["resid"].ap()[:, gt__, dh_ * 512:(dh_ + 1) * 512],
                                    ALU.mult, ALU.add), inc="vS")
                                tok[("sttb", bk)] = w.tok("vS")
                        tok[("stt_v", tb, e, half)] = w.tok("vS")
                    tok["stt_prev_v"] = tok[("stt_v", tb, e, NHALF - 1)]
                    tok["bankfree_v"] = tok["stt_prev_v"]
                gp_state["gp"] = gp_count

            tok["psum_vfree"] = tok["laev_last"]
            tok["psum_sfree"] = tok["r_done_s"]
            do_ffn_tb(0, filler=list(gate_steps))

            w.wait("gpsimd", "tS", max(tok.get(("wo_slot_prev", s_), 0) for s_ in range(8)))
            w.do("gpsimd", lambda q: q.dma_start(
                out=WO, in_=dw["wo_attn"].rearrange("(c p) d -> p c d", p=128)),
                inc="qk", k=16)
            tok["wo_attn_qk"] = w.tok("qk")

            def flush_przb():
                pend = tok.get("pend")
                if pend is None:
                    return
                h_, qb_, qg_, ob_, rectok = pend
                w.wait("tensor", "vS", rectok)
                w.wait("tensor", "vS", tok.get("przb_cp"))
                w.do("tensor", lambda t: t.matmul(bank(4)[:, :], S["onesr"][:], rzv,
                                                  start=True, stop=True), inc="tS")
                tkp = w.tok("tS")
                w.wait("vector", "tS", tkp)
                w.do("vector", lambda v: v.tensor_copy(rzbv, bank(4)[:, :]), inc="vS")
                tok["przb_cp"] = w.tok("vS")
                w.do("vector", lambda v, h2=h_, qb2=qb_, ob2=ob_: v.tensor_tensor(
                    oT[:, h2, qb2 * 512:(qb2 + 1) * 512], bank(ob2)[:, :], rzbv,
                    ALU.mult), inc="vS")
                tok[("oev_v", qg_)] = w.tok("vS")
                tok["oev_last"] = w.tok("vS")
                tok["pend"] = None

            for h in range(H):
                pidx = 0
                w.wait("tensor", "sS", tok["r_done_s"])
                flushed = tok.get("pend") is not None and tok["pend"][2] % 2 == 0
                if flushed:
                    flush_przb()
                w.wait("tensor", "vS", max(tok.get(("oev_v", h * N5 - 2), 0),
                                           tok.get("stt_prev_v", 0),
                                           tok.get("oev_last", 0) if flushed else 0))

                def projchain(lhs_name, lhs_off, lhs_p, rhs_name, rhs_rows, bias_kind, bias_sl,
                              dst_fn, ev_eng, P_out):
                    nonlocal pidx
                    for tn in range(N5):
                        b = 2 + pidx % 2
                        prev = tok.get(("pj", h, pidx - 2))
                        w.wait("tensor", prev[0] if prev else "tS", prev[1] if prev else 0)
                        w.do("tensor", lambda t, tn_=tn, b_=b: t.matmul(
                            pb[0:P_out, b_ * 512:(b_ + 1) * 512],
                            S[lhs_name][:, lhs_off:lhs_off + lhs_p],
                            S[rhs_name][:, tn_ * 512:(tn_ + 1) * 512],
                            start=True, stop=True), inc="tS")
                        tk_ = w.tok("tS")
                        w.wait(ev_eng, "tS", tk_)
                        dst_fn(tn, b)
                        semn = "sS" if ev_eng == "scalar" else "vS"
                        tok[("pj", h, pidx)] = (semn, w.tok(semn))
                        pidx += 1

                def ev_kc(tn, b):
                    w.do("scalar", lambda s: s.activation(
                        out=kc_h[:, tn * 512:(tn + 1) * 512],
                        in_=pb[0:128, b * 512:(b + 1) * 512],
                        func=AF.Identity, bias=S["buk"][:, h:h + 1]), inc="sS")
                projchain("wuk", h * DK, DK, "ckv", DC, None, None, ev_kc, "scalar", 128)

                def ev_qc(tn, b):
                    w.do("scalar", lambda s: s.activation(
                        out=qc_h[:, tn * 512:(tn + 1) * 512],
                        in_=pb[0:128, b * 512:(b + 1) * 512],
                        func=AF.Identity, bias=S["buq"][:, h:h + 1]), inc="sS")
                projchain("wuq", h * DK, DK, "cq", DC, None, None, ev_qc, "scalar", 128)

                def ev_qr(tn, b):
                    w.do("scalar", lambda s: s.activation(
                        out=qr_h[:, tn * 512:(tn + 1) * 512],
                        in_=pb[0:DHR, b * 512:(b + 1) * 512],
                        func=AF.Identity, bias=S["bqr"][:, h:h + 1]), inc="sS")
                projchain("wqr", h * DHR, DHR, "cq", DC, None, None, ev_qr, "scalar", DHR)

                for tt in range(NT):
                    b = 2 + pidx % 2
                    prev = tok.get(("pj", h, pidx - 2))
                    w.wait("tensor", prev[0] if prev else "tS", prev[1] if prev else 0)
                    w.do("tensor", lambda t, tt_=tt, b_=b: t.matmul(
                        pb[:, b_ * 512:b_ * 512 + DK],
                        S["ckv"][:, tt_ * 128:(tt_ + 1) * 128],
                        S["wuv"][:, h * DK:(h + 1) * DK],
                        start=True, stop=True), inc="tS")
                    tk = w.tok("tS")
                    w.wait("vector", "tS", tk)
                    w.wait("vector", "vS", tok["bcasts_v"])
                    w.do("vector", lambda v, tt_=tt, b_=b: v.tensor_tensor(
                        v_h[:, tt_, :], pb[:, b_ * 512:b_ * 512 + DK],
                        S["bUVb"][:, h * DK:(h + 1) * DK], ALU.add), inc="vS")
                    tok[("pj", h, pidx)] = ("vS", w.tok("vS"))
                    pidx += 1
                tok[("proj_done", h)] = ("vS", w.tok("vS"), "sS", w.tok("sS"))

                for qb in range(N5):
                    qg = h * N5 + qb
                    p2 = qg % 2
                    ob = 2 if p2 == 0 else 6
                    zb = 3 if p2 == 0 else 7
                    w.wait("tensor", "vS", tok[("proj_done", h)][1])
                    w.wait("tensor", "sS", tok[("proj_done", h)][3])
                    for kt in range(NT + 2):
                        ko = kt - 2
                        if ko >= 0:
                            w.wait("tensor", "sS", tok[("exp_s", h, qb, ko)])
                        if kt == 0:
                            w.wait("tensor", "vS", tok.get(("oev_v", qg - 2),
                                                            tok.get("stt_prev_v", 0)))
                            w.wait("tensor", "sS", tok.get("exp_prev_last"))
                        if ko >= 0:
                            w.do("tensor", lambda t, kt_=ko, ob_=ob: t.matmul(
                                bank(ob_)[:, :], v_h[:, kt_, :], Ebuf[:, kt_ % 3, :],
                                start=(kt_ == 0), stop=(kt_ == NT - 1)),
                                inc="tS" if ko == NT - 1 else None)
                            w.do("tensor", lambda t, kt_=ko, zb_=zb: t.matmul(
                                bank(zb_)[:, :], S["onesm"][:], Ebuf[:, kt_ % 3, :],
                                start=(kt_ == 0), stop=(kt_ == NT - 1)),
                                inc="tS" if ko == NT - 1 else None)
                        if kt < NT:
                            b = (0, 1, 5)[kt % 3]
                            w.do("tensor", lambda t, kt_=kt, b_=b, qb_=qb: t.matmul(
                                bank(b_)[:, :], S["kr"][:, kt_ * 128:(kt_ + 1) * 128],
                                qr_h[:, qb_ * 512:(qb_ + 1) * 512],
                                start=True, stop=False))
                            w.do("tensor", lambda t, kt_=kt, b_=b, qb_=qb: t.matmul(
                                bank(b_)[:, :], kc_h[:, kt_ * 128:(kt_ + 1) * 128],
                                qc_h[:, qb_ * 512:(qb_ + 1) * 512],
                                start=False, stop=True), inc="tS")
                            tk = w.tok("tS")
                            w.wait("scalar", "tS", tk)
                            w.do("scalar", lambda s, kt_=kt, b_=b: s.activation(
                                out=Ebuf[:, kt_ % 3, :], in_=bank(b_)[:, :],
                                func=AF.Exp, scale=SC), inc="sS")
                            tok[("exp_s", h, qb, kt)] = w.tok("sS")
                        if kt == 5:
                            flush_przb()
                    tok[("oz_t", qg)] = w.tok("tS")
                    tok["exp_prev_last"] = tok[("exp_s", h, qb, NT - 1)]
                    w.wait("vector", "tS", tok[("oz_t", qg)])
                    w.do("vector", lambda v, zb_=zb: v.reciprocal(
                        rzv, pb[0:1, zb_ * 512:(zb_ + 1) * 512]), inc="vS")
                    tok["pend"] = (h, qb, qg, ob, w.tok("vS"))
            flush_przb()
            tok["attn_ov"] = w.tok("vS")
            tok["exp_all_s"] = w.tok("sS")

            widx = 0
            w.wait("tensor", "qk", tok["wo_attn_qk"])
            w.wait("vector", "sS", tok[("resid_v", NT - 1)])
            for qt in range(NT):
                for dh in range(2):
                    b = 5 + widx % 2
                    prev = tok.get(("woev_v", widx - 2))
                    w.wait("tensor", "vS", prev if prev else tok["attn_ov"])
                    for c in range(8):
                        w.do("tensor", lambda t, qt_=qt, dh_=dh, c_=c, b_=b: t.matmul(
                            bank(b_)[:, :], oT[:, c_, qt_ * 128:(qt_ + 1) * 128],
                            WO[:, c_, dh_ * 512:(dh_ + 1) * 512],
                            start=(c_ == 0), stop=False))
                    w.do("tensor", lambda t, dh_=dh, b_=b: t.matmul(
                        bank(b_)[:, :], S["onesm"][0:1, :],
                        S["borow"][0:1, dh_ * 512:(dh_ + 1) * 512],
                        start=False, stop=True), inc="tS")
                    tk = w.tok("tS")
                    w.wait("vector", "tS", tk)
                    w.do("vector", lambda v, qt_=qt, dh_=dh, b_=b: v.tensor_tensor(
                        S["resid"].ap()[:, qt_, dh_ * 512:(dh_ + 1) * 512],
                        S["resid"].ap()[:, qt_, dh_ * 512:(dh_ + 1) * 512],
                        bank(b_)[:, :], ALU.add), inc="vS")
                    tok[("woev_v", widx)] = w.tok("vS")
                    if widx == 2 * NTT - 1:
                        tok["woev_tb0_v"] = w.tok("vS")
                    widx += 1
            tok["WO_t"] = w.tok("tS")
            tok["WO_v"] = w.tok("vS")


            tok["psum_vfree"] = tok["WO_v"]
            tok["psum_sfree"] = tok["exp_all_s"]
            tok["area_a_tfree"] = tok["WO_t"]
            tok["areaC_tfree"] = tok["WO_t"]
            for tb2 in range(1, NTB):
                do_ffn_tb(tb2)
            w.wait("sync", "vS", max(tok["woev_tb0_v"], tok[("stt_v", 0, E - 1, NHALF - 1)]))
            w.do("sync", lambda q: q.dma_start(
                out=dy.rearrange("(i p) d -> p i d", p=128)[:, 0:NTT, :],
                in_=S["resid"].ap()[:, 0:NTT, :]),
                inc="qo", k=16)
            for tb2 in range(1, NTB):
                for hf in range(NHALF):
                    w.wait("sync", "vS", max(tok[("stt_v", tb2, E - 1, hf)], tok["WO_v"]))
                    w.do("sync", lambda q, tb_=tb2, hf_=hf: q.dma_start(
                        out=dy.rearrange("(i p) d -> p i d", p=128)
                            [:, tb_ * NTT + hf_ * 4:tb_ * NTT + hf_ * 4 + 4, :],
                        in_=S["resid"].ap()[:, tb_ * NTT + hf_ * 4:tb_ * NTT + hf_ * 4 + 4, :]),
                        inc="qo", k=16)
            w.wait("sync", "qo", 16 * (1 + (NTB - 1) * NHALF))

        finals = {}
        for name in ["sync", "gpsimd", "tensor", "vector", "scalar"]:
            deco = getattr(block, name)

            def mk(nm):
                def fn(px):
                    ww = Walk(nm, px, sems)
                    walk(ww)
                    finals[nm] = dict(ww.c)
                return fn
            deco(mk(name))
        ref_c = finals["sync"]
        for nm, c in finals.items():
            assert c == ref_c, (nm, {k: (c[k], ref_c[k]) for k in c if c[k] != ref_c[k]})

    return nc


def make_in_map(x_core, inputs):
    f32 = np.float32
    m = {"x": np.ascontiguousarray(np.asarray(x_core, f32))}
    key = {
        "wdq": "L0_attn_W_DQ", "bdq": "L0_attn_b_DQ", "wuq": "L0_attn_W_UQ", "buq": "L0_attn_b_UQ",
        "wdkv": "L0_attn_W_DKV", "bdkv": "L0_attn_b_DKV", "wuk": "L0_attn_W_UK", "buk": "L0_attn_b_UK",
        "wuv": "L0_attn_W_UV", "buv": "L0_attn_b_UV", "wqr": "L0_attn_W_QR", "bqr": "L0_attn_b_QR",
        "wkr": "L0_attn_W_KR", "bkr": "L0_attn_b_KR", "wo_attn": "L0_attn_W_O", "bo": "L0_attn_b_O",
        "rw": "L0_ffn_router_w", "rb": "L0_ffn_router_b", "temp": "L0_ffn_temperature",
        "wg": "L0_ffn_wg", "wp": "L0_ffn_wp", "wexp": "L0_ffn_wo",
    }
    for k, src in key.items():
        m[k] = np.ascontiguousarray(np.asarray(inputs[src], f32))
    m["eyeb"] = np.eye(128, dtype=BF)
    m["eyef"] = np.eye(8, dtype=f32)
    m["onesr"] = np.ones((1, 128), f32)
    m["onescb"] = np.ones((128, 1), BF)
    m["onescf"] = np.ones((128, 1), f32)
    m["onesm"] = np.ones((128, 128), BF)
    return m


def kernel(**inputs):
    from concourse.bass_utils import run_bass_kernel_spmd
    x = np.asarray(inputs["L0_attn_h_t"], np.float32)
    B, T, _ = x.shape
    nc = build_nc(T)
    in_maps = [make_in_map(x[b], inputs) for b in range(B)]
    res = run_bass_kernel_spmd(nc, in_maps, core_ids=list(range(B)))
    return np.stack([np.asarray(r["y"], np.float32) for r in res.results], axis=0)


# version 27
# speedup vs baseline: 1.0896x; 1.0215x over previous
import math
import contextlib
import numpy as np
import ml_dtypes

import concourse.bass as bass
import concourse.mybir as mybir

F32 = mybir.dt.float32
BF16 = mybir.dt.bfloat16
AF = mybir.ActivationFunctionType
ALU = mybir.AluOpType
AX = mybir.AxisListType

D = 1024
DC = 32
DHR = 32
H = 8
DK = 128
E = 8
FF = 2048
EPS = 1e-5
SC = 1.0 / math.sqrt(DK + DHR)

BF = ml_dtypes.bfloat16

SEM_NAMES = ["qc", "qk", "x0", "x1", "qo",
             "wg0", "wg1", "wg2", "wp0", "wp1", "wp2",
             "wo0", "wo1", "wo2", "wo3", "wo4", "wo5", "wo6", "wo7",
             "tS", "vS", "sS"]


class Walk:

    def __init__(self, en, px, sems):
        self.en = en
        self.px = px
        self.sems = sems
        self.c = {k: 0 for k in sems}
        self.seen = {}

    _SELF = {"vector": "vS", "scalar": "sS"}

    def do(self, eng, fn, inc=None, k=1):
        if eng == self.en:
            sw = self._SELF.get(eng)
            if sw is not None and self.c[sw] > 0:
                self.px.wait_ge(self.sems[sw], self.c[sw])
            inst = fn(self.px)
            if inc is not None:
                inst.then_inc(self.sems[inc], k)
        if inc is not None:
            self.c[inc] += k
            return self.c[inc]
        return None

    def tok(self, sem):
        return self.c[sem]

    def wait(self, eng, sem, v):
        if v is None or v <= 0:
            return
        if eng == self.en:
            key = sem
            if self.seen.get(key, -1) >= v:
                return
            self.seen[key] = v
            self.px.wait_ge(self.sems[sem], v)


def build_nc(T):
    assert T % 512 == 0
    NT = T // 128
    NTB = max(1, T // 1024)
    TB = T // NTB
    NTT = TB // 128
    NHALF = NTT // 4
    assert NTT % 4 == 0
    N5 = T // 512

    nc = bass.Bass("TRN2", target_bir_lowering=False)

    dx = nc.dram_tensor("x", [T, D], F32, kind="ExternalInput").ap()
    dy = nc.dram_tensor("y", [T, D], F32, kind="ExternalOutput").ap()
    dw = {}
    for nm, shp in [("wdq", [D, DC]), ("wdkv", [D, DC]), ("wkr", [D, DHR]),
                    ("wuq", [DC, H * DK]), ("wuk", [DC, H * DK]), ("wuv", [DC, H * DK]),
                    ("wqr", [DC, H * DHR]), ("wo_attn", [H * DK, D]),
                    ("rw", [E, D]), ("wg", [E, D, FF]), ("wp", [E, D, FF]), ("wexp", [E, FF, D])]:
        dw[nm] = nc.dram_tensor(nm, shp, F32, kind="ExternalInput").ap()
    for nm, shp in [("bdq", [DC]), ("bdkv", [DC]), ("bkr", [DHR]),
                    ("buq", [H * DK]), ("buk", [H * DK]), ("buv", [H * DK]),
                    ("bqr", [H * DHR]), ("bo", [D]), ("rb", [E]), ("temp", [1])]:
        dw[nm] = nc.dram_tensor(nm, shp, F32, kind="ExternalInput").ap()
    for nm, shp, dt in [("eyeb", [128, 128], BF16), ("eyef", [8, 8], F32),
                        ("onesr", [1, 128], F32), ("onescb", [128, 1], BF16),
                        ("onescf", [128, 1], F32), ("onesm", [128, 128], BF16)]:
        dw[nm] = nc.dram_tensor(nm, shp, dt, kind="ExternalInput").ap()

    with contextlib.ExitStack() as stack:
        en = stack.enter_context
        S = {}

        def sb(nm, shp, dt):
            S[nm] = en(nc.sbuf_tensor("s_" + nm, shp, dt))
            return S[nm]

        sb("eyeb", [128, 128], BF16); sb("eyef8", [8, 8], F32)
        sb("onesr", [1, 128], F32); sb("onescb", [128, 1], BF16); sb("onescf", [128, 1], F32)
        sb("onesm", [128, 128], BF16)
        sb("wcat", [128, 8, 96], BF16)
        sb("bdq", [DC, 1], F32); sb("bdkv", [DC, 1], F32); sb("bkr", [DHR, 1], F32)
        sb("wuq", [DC, H * DK], BF16); sb("wuk", [DC, H * DK], BF16)
        sb("wuv", [DC, H * DK], BF16); sb("wqr", [DC, H * DHR], BF16)
        sb("buq", [128, H], F32); sb("buk", [128, H], F32); sb("bqr", [DHR, H], F32)
        sb("bUVb", [128, D], BF16); sb("borow", [1, D], BF16)
        sb("rwT", [128, 8, E], BF16)
        sb("rbc", [E, 1], F32); sb("tempc", [1, 4], F32)
        sb("rcol", [E, 1], F32); sb("rbs", [E, 1], F32)
        sb("resid", [128, NT, D], F32)
        sb("xnT", [128, 8, T], BF16)
        sb("areaA", [128, max(8 * T, 16 * TB)], BF16)
        sb("areaC", [128, 8 * 1024], BF16)
        sb("wgbuf", [128, 3, 8, 128], BF16)
        sb("wpbuf", [128, 3, 8, 128], BF16)
        sb("areaB", [128, max(7168, 4 * T + 1536)], BF16)
        sb("statc", [128, 2, 4, 7], F32)
        sb("cq", [DC, T], BF16); sb("ckv", [DC, T], BF16); sb("kr", [DHR, T], BF16)
        sb("r_sb", [E, 2, 512], F32)
        sb("la", [128, NT * 8], F32)
        sb("gex", [128, NT * 8], F32)
        sb("gt", [128, NT * 8], F32)
        sb("gt2", [128, NT * 8], F32)
        sb("w_tm", [128, NT * 8], F32)
        sb("grow", [1, NT * 8], F32)
        sb("lse8", [1, 2 * E], F32)
        sb("epsc", [128, 1], F32)
        gl = nc.sbuf_bytes_remaining
        psum8 = en(nc.psum_tensor("psum8", [128, 4096], F32))
        sems = {s: en(nc.semaphore("sem_" + s)) for s in SEM_NAMES}
        block = en(nc.Block())

        oT = S["areaA"].ap()[:, 0:8 * T].rearrange("p (c t) -> p c t", c=8)
        hbuf = S["areaA"].ap()[:, 0:16 * TB].rearrange("p (f t) -> p f t", f=16)
        WO = S["areaC"].ap().rearrange("p (c d) -> p c d", c=8)
        wobuf = S["areaC"].ap().rearrange("p (s d) -> p s d", s=8)
        pb = psum8.ap()
        pbf = psum8.ap().bitcast(BF16)

        def bank(b, w=512):
            return pb[:, b * 512: b * 512 + w]

        def bankb(b, w=1024):
            return pbf[:, b * 1024: b * 1024 + w]

        rzv = S["r_sb"].ap()[0:1, 0, :]
        rzbv = S["wcat"].ap().rearrange("p a b -> p (a b)")[:, 0:512]
        aB = S["areaB"].ap()
        rowbuf = aB[0:1, 2048:4096].bitcast(F32)
        xnbf = aB[:, 4096:6144].rearrange("p (s d) -> p s d", s=2)
        sqout = aB[:, 6144:7168]
        qc_h = aB[:, 0:T]
        kc_h = aB[:, T:2 * T]
        v_h = aB[:, 2 * T:3 * T].rearrange("p (a b) -> p a b", b=DK)
        qr_h = aB[0:DHR, 3 * T:4 * T]
        Ebuf = aB[:, 4 * T:4 * T + 1536].rearrange("p (s d) -> p s d", s=3)
        rw8 = aB[0:E, 0:1024]
        lav = S["la"].ap().rearrange("p (a b) -> p a b", b=8)
        gexv = S["gex"].ap().rearrange("p (a b) -> p a b", b=8)
        gtv = S["gt"].ap().rearrange("p (a b) -> p a b", b=8)
        wv = S["w_tm"].ap().rearrange("p (a b) -> p a b", b=8)

        def walk(w):
            tok = {}

            for i0 in range(2):
                w.do("sync", lambda q, i_=i0: q.dma_start(
                    out=S["resid"].ap()[:, i_, :], in_=dx[i_ * 128:(i_ + 1) * 128, :]),
                    inc="x0" if i0 % 2 == 0 else "x1", k=16)
            nconst = 0
            for dst, src in [
                (S["eyeb"][:], dw["eyeb"]), (S["eyef8"][:], dw["eyef"]),
                (S["onesr"][:], dw["onesr"]), (S["onescb"][:], dw["onescb"]),
                (S["onescf"][:], dw["onescf"]), (S["onesm"][:], dw["onesm"]),
                (rowbuf, dw["buv"].rearrange("(o d) -> o d", o=1)),
            ]:
                w.do("sync", lambda q, d_=dst, s_=src: q.dma_start(out=d_, in_=s_), inc="qc", k=16)
                nconst += 1
            for dst, src in [
                (S["bdq"][:], dw["bdq"].rearrange("(p o) -> p o", o=1)),
                (S["bdkv"][:], dw["bdkv"].rearrange("(p o) -> p o", o=1)),
                (S["bkr"][:], dw["bkr"].rearrange("(p o) -> p o", o=1)),
                (S["rbc"][:], dw["rb"].rearrange("(p o) -> p o", o=1)),
                (S["tempc"][:, 0:1], dw["temp"].rearrange("(p o) -> p o", o=1)),
                (S["buq"][:], dw["buq"].rearrange("(h p) -> p h", p=128)),
                (S["buk"][:], dw["buk"].rearrange("(h p) -> p h", p=128)),
                (S["bqr"][:], dw["bqr"].rearrange("(h p) -> p h", p=DHR)),
            ]:
                w.do("sync", lambda q, d_=dst, s_=src: q.dma_start(
                    out=d_, in_=s_, allow_slow_non_contiguous=True), inc="qc", k=16)
                nconst += 1
            tok["consts"] = w.tok("qc")

            ncast = 0
            cast_loads = [
                (S["wcat"].ap()[:, :, 0:32], dw["wdq"].rearrange("(c p) m -> p c m", p=128)),
                (S["wcat"].ap()[:, :, 32:64], dw["wdkv"].rearrange("(c p) m -> p c m", p=128)),
                (S["wcat"].ap()[:, :, 64:96], dw["wkr"].rearrange("(c p) m -> p c m", p=128)),
                (S["wuq"][:], dw["wuq"]), (S["wuk"][:], dw["wuk"]),
                (S["wuv"][:], dw["wuv"]), (S["wqr"][:], dw["wqr"]),
                (rw8, dw["rw"]),
                (S["borow"][:], dw["bo"].rearrange("(o d) -> o d", o=1)),
            ]
            for dst, src in cast_loads:
                w.do("gpsimd", lambda q, d_=dst, s_=src: q.dma_start(out=d_, in_=s_),
                     inc="qk", k=16)
                ncast += 1
            tok["casts"] = w.tok("qk")

            w.wait("tensor", "qc", tok["consts"])
            for dh in range(2):
                b_use = 6 + dh % 2
                w.do("tensor", lambda t, d_=dh, b_=b_use: t.matmul(
                    bank(b_)[:, :], S["onesr"][:], rowbuf[:, d_ * 512:(d_ + 1) * 512],
                    start=True, stop=True), inc="tS")
                tk = w.tok("tS")
                w.wait("vector", "tS", tk)
                w.do("vector", lambda v, d_=dh, b_=b_use: v.tensor_copy(
                    S["bUVb"][:, d_ * 512:(d_ + 1) * 512], bank(b_)[:, :]), inc="vS")
                tok[("bc_v", b_use)] = w.tok("vS")
            tok["buvb_t"] = w.tok("tS")
            tok["bcasts_v"] = w.tok("vS")

            w.do("vector", lambda v: v.memset(S["epsc"][:], EPS), inc="vS")
            tok["eps_v"] = w.tok("vS")
            w.wait("vector", "qc", tok["consts"])
            w.do("vector", lambda v: v.tensor_scalar_max(S["tempc"][:, 1:2], S["tempc"][:, 0:1], 0.1), inc="vS")
            w.do("vector", lambda v: v.reciprocal(S["tempc"][:, 2:3], S["tempc"][:, 1:2]), inc="vS")
            tok["temp_v"] = w.tok("vS")
            w.wait("tensor", "vS", tok["temp_v"])
            w.do("tensor", lambda t: t.matmul(pb[0:E, 5 * 512:5 * 512 + 1],
                                              S["onesr"][0:1, 0:E], S["tempc"][0:1, 2:3],
                                              start=True, stop=True), inc="tS")
            tok["rcol_t"] = w.tok("tS")
            w.wait("vector", "tS", tok["rcol_t"])
            w.do("vector", lambda v: v.tensor_copy(S["rcol"][:], pb[0:E, 5 * 512:5 * 512 + 1]), inc="vS")
            w.do("vector", lambda v: v.tensor_tensor(S["rbs"][:], S["rbc"][:], S["rcol"][:], ALU.mult), inc="vS")
            tok["rbs_v"] = w.tok("vS")

            w.wait("tensor", "qk", tok["casts"])
            for c in range(8):
                b = c % 2
                if c >= 2:
                    w.wait("tensor", "vS", tok[("rwT_v", c - 2)])
                w.do("tensor", lambda t, c_=c, b_=b: t.transpose(
                    bankb(b_)[:, 0:E], rw8[:, c_ * 128:(c_ + 1) * 128], S["eyeb"][0:E, 0:E]),
                    inc="tS")
                tk = w.tok("tS")
                w.wait("vector", "tS", tk)
                w.do("vector", lambda v, c_=c, b_=b: v.tensor_copy(
                    S["rwT"].ap()[:, c_, :], bankb(b_)[:, 0:E]), inc="vS")
                tok[("rwT_v", c)] = w.tok("vS")
            tok["rwT_done"] = w.tok("vS")

            bank01 = {0: 0, 1: 0}
            gidx = 0
            NB = NT // 4
            stw = S["statc"].ap()
            for B in range(NB):
                bp = B % 2
                if B >= 2:
                    w.wait("vector", "sS", tok[("resid_v", (B - 2) * 4 + 3)])
                for j in range(4):
                    i = B * 4 + j
                    sl = i % 2
                    xs = "x0" if sl == 0 else "x1"
                    if i >= 2:
                        w.wait("sync", "vS", tok[("s1_v", i - 2)])
                        w.do("sync", lambda q, i_=i: q.dma_start(
                            out=S["resid"].ap()[:, i_, :], in_=dx[i_ * 128:(i_ + 1) * 128, :]),
                            inc=xs, k=16)
                    xtile = S["resid"].ap()[:, i, :]
                    w.wait("vector", xs, 16 * (i // 2 + 1))
                    w.do("vector", lambda v, x_=xtile, j_=j: v.reduce_sum(
                        stw[:, bp, j_, 0:1], x_, axis=AX.X), inc="vS")
                    tok[("s1_v", i)] = w.tok("vS")
                    w.wait("scalar", xs, 16 * (i // 2 + 1))
                    if B >= 2 and j == 0:
                        w.wait("scalar", "vS", tok[("s1_v", i)])
                    w.do("scalar", lambda s, x_=xtile, j_=j: s.activation(
                        out=sqout, in_=x_, func=AF.Square, accum_out=stw[:, bp, j_, 1:2]), inc="sS")
                tok[("sq_s", B)] = w.tok("sS")
                w.wait("vector", "sS", tok[("sq_s", B)])
                w.do("vector", lambda v: v.tensor_scalar_mul(stw[:, bp, :, 2:3], stw[:, bp, :, 0:1], 1.0 / D), inc="vS")
                w.do("vector", lambda v: v.tensor_scalar_mul(stw[:, bp, :, 3:4], stw[:, bp, :, 1:2], 1.0 / D), inc="vS")
                w.do("vector", lambda v: v.tensor_tensor(stw[:, bp, :, 4:5], stw[:, bp, :, 2:3], stw[:, bp, :, 2:3], ALU.mult), inc="vS")
                w.do("vector", lambda v: v.tensor_tensor(stw[:, bp, :, 3:4], stw[:, bp, :, 3:4], stw[:, bp, :, 4:5], ALU.subtract), inc="vS")
                tok[("var_v", B)] = w.tok("vS")
                w.wait("scalar", "vS", tok[("var_v", B)])
                w.do("scalar", lambda s: s.activation(
                    out=stw[:, bp, :, 4:5], in_=stw[:, bp, :, 3:4], func=AF.Sqrt,
                    bias=S["epsc"][:]), inc="sS")
                tok[("sqrt_s", B)] = w.tok("sS")
                w.wait("vector", "sS", tok[("sqrt_s", B)])
                w.do("vector", lambda v: v.reciprocal(stw[:, bp, :, 5:6], stw[:, bp, :, 4:5]), inc="vS")
                w.do("vector", lambda v: v.tensor_tensor(stw[:, bp, :, 6:7], stw[:, bp, :, 2:3], stw[:, bp, :, 5:6], ALU.mult), inc="vS")
                w.do("vector", lambda v: v.tensor_scalar_mul(stw[:, bp, :, 6:7], stw[:, bp, :, 6:7], -1.0), inc="vS")
                w.do("vector", lambda v: v.tensor_scalar_add(stw[:, bp, :, 0:1], stw[:, bp, :, 5:6], 1.0), inc="vS")
                tok[("cols_v", B)] = w.tok("vS")
                w.wait("scalar", "vS", tok[("cols_v", B)])
                for j in range(4):
                    i = B * 4 + j
                    sl = i % 2
                    xtile = S["resid"].ap()[:, i, :]
                    if i >= 2:
                        w.wait("scalar", "tS", tok[("tr_t", i - 2)])
                    w.do("scalar", lambda s, x_=xtile, j_=j, sl_=sl: s.activation(
                        out=xnbf[:, sl_, :], in_=x_, func=AF.Identity,
                        bias=stw[:, bp, j_, 6:7], scale=stw[:, bp, j_, 5:6]), inc="sS")
                    tok[("xn_s", i)] = w.tok("sS")
                    if B < 2:
                        if j < 2:
                            w.wait("vector", "sS", tok[("xn_s", i)])
                            w.do("vector", lambda v, x_=xtile, j_=j, i_=i: v.tensor_scalar(
                                S["resid"].ap()[:, i_, :], x_, stw[:, bp, j_, 0:1],
                                stw[:, bp, j_, 6:7], ALU.mult, ALU.add), inc="vS")
                            tok[("resid_v", i)] = w.tok("vS")
                        else:
                            w.do("scalar", lambda s, x_=xtile, j_=j, i_=i: s.activation(
                                out=S["resid"].ap()[:, i_, :], in_=x_, func=AF.Identity,
                                bias=stw[:, bp, j_, 6:7], scale=stw[:, bp, j_, 0:1]), inc="sS")
                            tok[("resid_v", i)] = w.tok("sS")
                    w.wait("tensor", "sS", tok[("xn_s", i)])
                    for c in range(8):
                        b = (i * 8 + c) % 2
                        w.wait("tensor", "vS", bank01[b])
                        w.do("tensor", lambda t, c_=c, b_=b, sl_=sl: t.transpose(
                            bankb(b_)[:, 0:128], xnbf[:, sl_, c_ * 128:(c_ + 1) * 128],
                            S["eyeb"][:]), inc="tS")
                        tk = w.tok("tS")
                        w.wait("vector", "tS", tk)
                        w.do("vector", lambda v, c_=c, b_=b, i_=i: v.tensor_copy(
                            S["xnT"].ap()[:, c_, i_ * 128:(i_ + 1) * 128], bankb(b_)[:, 0:128]), inc="vS")
                        bank01[b] = w.tok("vS")
                        tok[("trev_v", i * 8 + c)] = w.tok("vS")
                    tok[("tr_t", i)] = w.tok("tS")
                i = B * 4 + 3
                tn = B
                xready = tok[("trev_v", i * 8 + 7)]
                if tn == 0:
                    w.wait("tensor", "qk", tok["casts"])
                    w.wait("tensor", "vS", tok["rwT_done"])
                w.wait("tensor", "vS", xready)
                for kind, off, bias, dst in [("cq", 0, "bdq", "cq"), ("ckv", 32, "bdkv", "ckv"),
                                             ("kr", 64, "bkr", "kr")]:
                    P = DC if kind != "kr" else DHR
                    b = 2 + gidx % 2
                    prev = tok.get(("cev_s", gidx - 2))
                    w.wait("tensor", "sS", prev)
                    for c in range(8):
                        w.do("tensor", lambda t, c_=c, b_=b, o_=off, tn_=tn, P_=P: t.matmul(
                            pb[0:P_, b_ * 512:(b_ + 1) * 512],
                            S["wcat"].ap()[:, c_, o_:o_ + P_],
                            S["xnT"].ap()[:, c_, tn_ * 512:(tn_ + 1) * 512],
                            start=(c_ == 0), stop=(c_ == 7)),
                            inc="tS" if c == 7 else None)
                    tk = w.tok("tS")
                    w.wait("scalar", "tS", tk)
                    w.do("scalar", lambda s, b_=b, tn_=tn, P_=P, bi=bias, d_=dst: s.activation(
                        out=S[d_][:, tn_ * 512:(tn_ + 1) * 512],
                        in_=pb[0:P_, b_ * 512:(b_ + 1) * 512],
                        func=AF.Identity, bias=S[bi][:]), inc="sS")
                    tok[("cev_s", gidx)] = w.tok("sS")
                    gidx += 1
                b = 4 + tn % 2
                prev = tok.get(("rev_s", tn - 2))
                w.wait("tensor", "sS", prev)
                if tn < 2:
                    w.wait("tensor", "vS", tok["rbs_v"])
                for c in range(8):
                    w.do("tensor", lambda t, c_=c, b_=b, tn_=tn: t.matmul(
                        pb[0:E, b_ * 512:(b_ + 1) * 512],
                        S["rwT"].ap()[:, c_, :],
                        S["xnT"].ap()[:, c_, tn_ * 512:(tn_ + 1) * 512],
                        start=(c_ == 0), stop=(c_ == 7)),
                        inc="tS" if c == 7 else None)
                tk = w.tok("tS")
                w.wait("scalar", "tS", tk)
                w.wait("scalar", "vS", tok["rbs_v"])
                w.do("scalar", lambda s, b_=b, tn_=tn: s.activation(
                    out=S["r_sb"].ap()[:, tn_ % 2, :],
                    in_=pb[0:E, b_ * 512:(b_ + 1) * 512],
                    func=AF.Identity, bias=S["rbs"][:], scale=S["rcol"][:]), inc="sS")
                tok[("rev_s", tn)] = w.tok("sS")
                w.wait("tensor", "sS", tok[("rev_s", tn)])
                for j in range(4):
                    ii = tn * 4 + j
                    b2 = ii % 2
                    w.wait("tensor", "vS", bank01[b2])
                    w.do("tensor", lambda t, tn_=tn, j_=j, b_=b2: t.transpose(
                        bank(b_, 8)[:, :], S["r_sb"].ap()[:, tn_ % 2, j_ * 128:(j_ + 1) * 128],
                        S["eyef8"][:]), inc="tS")
                    tk2 = w.tok("tS")
                    w.wait("vector", "tS", tk2)
                    w.do("vector", lambda v, i_=ii, b_=b2: v.tensor_copy(
                        lav[:, i_, :], bank(b_, 8)[:, :]), inc="vS")
                    bank01[b2] = w.tok("vS")
                    tok[("laev_v", ii)] = w.tok("vS")
                tok[("rtr_t", tn)] = w.tok("tS")
            tok["xnT_v"] = w.tok("vS")
            tok["c_done_s"] = w.tok("sS")
            tok["r_done_s"] = w.tok("sS")
            tok["laev_last"] = w.tok("vS")

            NE = NT * 8
            gate_steps = []
            def _gs(fn):
                gate_steps.append(fn)
            for it in range(3):
                def stepA(it_=it):
                    w.wait("scalar", "vS", w.tok("vS"))
                    w.do("scalar", lambda s: s.activation(out=S["gex"][:], in_=S["la"][:], func=AF.Exp), inc="sS")
                    sk = w.tok("sS")
                    w.wait("vector", "sS", sk)
                    w.do("vector", lambda v: v.reduce_sum(gtv[:, :, 0:1], gexv, axis=AX.X), inc="vS")
                    vk = w.tok("vS")
                    w.wait("scalar", "vS", vk)
                    w.do("scalar", lambda s: s.activation(out=S["gt2"][:, 0:NT], in_=gtv[:, :, 0:1].squeeze(2),
                                                          func=AF.Ln), inc="sS")
                    sk = w.tok("sS")
                    w.wait("vector", "sS", sk)
                    w.do("vector", lambda v: v.tensor_tensor(
                        lav, lav, S["gt2"].ap()[:, 0:NT].unsqueeze(2).broadcast_to([128, NT, 8]),
                        ALU.subtract), inc="vS")
                    vk = w.tok("vS")
                    w.wait("scalar", "vS", vk)
                    w.do("scalar", lambda s: s.activation(out=S["gex"][:], in_=S["la"][:], func=AF.Exp), inc="sS")
                    tok[("sink_exp2", it_)] = w.tok("sS")
                _gs(stepA)

                def stepB(it_=it):
                    w.wait("tensor", "sS", tok[("sink_exp2", it_)])
                    w.wait("tensor", "vS", tok.get(("lse_v", it_ - 1)))
                    w.do("tensor", lambda t: t.matmul(pb[0:1, 6 * 512:6 * 512 + NE],
                                                      S["onescf"][:], S["gex"][:],
                                                      start=True, stop=True), inc="tS")
                    tk = w.tok("tS")
                    w.wait("vector", "tS", tk)
                    w.do("vector", lambda v: v.reduce_sum(
                        S["lse8"][0:1, 0:E],
                        pb[0:1, 6 * 512:6 * 512 + NE].rearrange("o (a b) -> o b a", b=8),
                        axis=AX.X), inc="vS")
                    vk = w.tok("vS")
                    w.wait("scalar", "vS", vk)
                    w.do("scalar", lambda s: s.activation(out=S["lse8"][0:1, E:2 * E], in_=S["lse8"][0:1, 0:E], func=AF.Ln), inc="sS")
                    sk = w.tok("sS")
                    w.wait("vector", "sS", sk)
                    w.do("vector", lambda v: v.tensor_copy(
                        S["grow"].ap().rearrange("o (a b) -> o a b", b=8),
                        S["lse8"][0:1, E:2 * E].unsqueeze(1).broadcast_to([1, NT, 8])), inc="vS")
                    tok[("sink_rep", it_)] = w.tok("vS")
                _gs(stepB)

                def stepC(it_=it):
                    w.wait("tensor", "vS", tok[("sink_rep", it_)])
                    w.do("tensor", lambda t: t.matmul(pb[:, 7 * 512:7 * 512 + NE],
                                                      S["onesr"][:], S["grow"][:],
                                                      start=True, stop=True), inc="tS")
                    tk = w.tok("tS")
                    w.wait("vector", "tS", tk)
                    w.do("vector", lambda v: v.tensor_tensor(
                        S["la"][:], S["la"][:], pb[:, 7 * 512:7 * 512 + NE], ALU.subtract), inc="vS")
                    tok[("lse_v", it_)] = w.tok("vS")
                _gs(stepC)

            def stepFin():
              w.wait("scalar", "vS", w.tok("vS"))
              w.do("scalar", lambda s: s.activation(out=S["gex"][:], in_=S["la"][:], func=AF.Exp), inc="sS")
              sk = w.tok("sS")
              w.wait("vector", "sS", sk)
              w.do("vector", lambda v: v.reduce_sum(gtv[:, :, 0:1], gexv, axis=AX.X), inc="vS")
              w.do("vector", lambda v: v.tensor_scalar_add(gtv[:, :, 1:2], gtv[:, :, 0:1], 1e-8), inc="vS")
              w.do("vector", lambda v: v.reciprocal(gtv[:, :, 2:3], gtv[:, :, 1:2]), inc="vS")
              w.do("vector", lambda v: v.tensor_tensor(
                  gexv, gexv, gtv[:, :, 2:3].broadcast_to([128, NT, 8]), ALU.mult), inc="vS")
              g4 = S["gex"].ap().rearrange("p (a b c) -> p a b c", b=4, c=2)
              hi = S["gt"].ap().rearrange("p (a b) -> p a b", b=8)[:, :, 0:4]
              lo = S["gt"].ap().rearrange("p (a b) -> p a b", b=8)[:, :, 4:8]
              w.do("vector", lambda v: v.reduce_max(hi, g4, axis=AX.X), inc="vS")
              w.do("vector", lambda v: v.tensor_reduce(lo, g4, axis=AX.X, op=ALU.min), inc="vS")
              t2 = S["gt2"].ap().rearrange("p (a b) -> p a b", b=8)
              w.do("vector", lambda v: v.tensor_tensor(t2[:, :, 0:2], hi[:, :, 0:4:2], hi[:, :, 1:4:2], ALU.max), inc="vS")
              w.do("vector", lambda v: v.tensor_tensor(t2[:, :, 2:4], lo[:, :, 0:4:2], hi[:, :, 1:4:2], ALU.max), inc="vS")
              w.do("vector", lambda v: v.tensor_tensor(t2[:, :, 4:6], lo[:, :, 1:4:2], hi[:, :, 0:4:2], ALU.max), inc="vS")
              w.do("vector", lambda v: v.tensor_tensor(t2[:, :, 6:8], t2[:, :, 2:4], t2[:, :, 4:6], ALU.min), inc="vS")
              w.do("vector", lambda v: v.tensor_tensor(gtv[:, :, 0:1], t2[:, :, 0:1], t2[:, :, 1:2], ALU.max), inc="vS")
              w.do("vector", lambda v: v.tensor_tensor(gtv[:, :, 3:4], t2[:, :, 6:7], t2[:, :, 1:2], ALU.max), inc="vS")
              w.do("vector", lambda v: v.tensor_tensor(gtv[:, :, 4:5], t2[:, :, 7:8], t2[:, :, 0:1], ALU.max), inc="vS")
              w.do("vector", lambda v: v.tensor_tensor(gtv[:, :, 1:2], gtv[:, :, 3:4], gtv[:, :, 4:5], ALU.min), inc="vS")
              w.do("vector", lambda v: v.tensor_tensor(gtv[:, :, 5:6], gtv[:, :, 0:1], gtv[:, :, 1:2], ALU.add), inc="vS")
              w.do("vector", lambda v: v.tensor_scalar_add(gtv[:, :, 5:6], gtv[:, :, 5:6], 1e-8), inc="vS")
              w.do("vector", lambda v: v.reciprocal(gtv[:, :, 6:7], gtv[:, :, 5:6]), inc="vS")
              w.do("vector", lambda v: v.tensor_tensor(
                  wv, gexv, gtv[:, :, 1:2].broadcast_to([128, NT, 8]), ALU.is_ge), inc="vS")
              w.do("vector", lambda v: v.tensor_tensor(wv, wv, gexv, ALU.mult), inc="vS")
              w.do("vector", lambda v: v.tensor_tensor(
                  wv, wv, gtv[:, :, 6:7].broadcast_to([128, NT, 8]), ALU.mult), inc="vS")
              tok["w_v"] = w.tok("vS")
            _gs(stepFin)


            wg_uses = {0: 0, 1: 0, 2: 0}
            wp_uses = {0: 0, 1: 0, 2: 0}
            wgp_ci = {"i": 0}
            wgp_emitted = set()

            def emit_wgp(tb_, e_, ft_):
                if (tb_, e_, ft_) in wgp_emitted:
                    return
                wgp_emitted.add((tb_, e_, ft_))
                gs = wgp_ci["i"] % 3
                wgp_ci["i"] += 1
                w.wait("gpsimd", "tS", tok.get(("wg_slot_prev", gs)))
                w.do("gpsimd", lambda q, e2=e_, ft2=ft_, gs2=gs: q.dma_start(
                    out=S["wgbuf"].ap()[:, gs2, :, :],
                    in_=dw["wg"][e2, :, :].rearrange("(c p) f -> p c f", p=128)
                        [:, :, ft2 * 128:(ft2 + 1) * 128]),
                    inc="wg%d" % gs, k=16)
                wg_uses[gs] += 1
                w.wait("gpsimd", "tS", tok.get(("wp_slot_prev", gs)))
                w.do("gpsimd", lambda q, e2=e_, ft2=ft_, gs2=gs: q.dma_start(
                    out=S["wpbuf"].ap()[:, gs2, :, :],
                    in_=dw["wp"][e2, :, :].rearrange("(c p) f -> p c f", p=128)
                        [:, :, ft2 * 128:(ft2 + 1) * 128]),
                    inc="wp%d" % gs, k=16)
                wp_uses[gs] += 1
                tok[("wgp_slot_of", tb_, e_, ft_)] = (gs, wg_uses[gs], wp_uses[gs])
            wo_uses = {s: 0 for s in range(8)}
            gp_state = {"gp": 0}

            def do_ffn_tb(tb, filler=None):
                gp_count = gp_state["gp"]
                tb_gp0 = gp_count
                for e in range(E):
                    for ft in range(16):
                        emit_wgp(tb, e, ft)
                        gs, wgu, wpu = tok[("wgp_slot_of", tb, e, ft)]
                        for tt2 in range(TB // 512):
                            par = gp_count % 2
                            gb, pbk = 2 * par, 2 * par + 1
                            needv = max(tok.get(("h_v", gp_count - 2), 0),
                                        tok.get(("sttb", gb), 0),
                                        tok.get(("sttb", pbk), 0),
                                        tok.get("psum_vfree", 0) if gp_count - tb_gp0 < 2 else 0)
                            w.wait("tensor", "vS", needv)
                            w.wait("tensor", "sS", tok.get(("relu_s", gp_count - 2),
                                                           tok.get("psum_sfree", 0)))
                            w.wait("tensor", "wg%d" % gs, 16 * wgu)
                            for c in range(8):
                                w.do("tensor", lambda t, c_=c, tb_=tb, tt2_=tt2, gs_=gs, b_=gb: t.matmul(
                                    bank(b_)[:, :], S["wgbuf"].ap()[:, gs_, c_, :],
                                    S["xnT"].ap()[:, c_, tb_ * TB + tt2_ * 512: tb_ * TB + (tt2_ + 1) * 512],
                                    start=(c_ == 0), stop=(c_ == 7)),
                                    inc="tS" if c == 7 else None)
                            tok[("g_t", gp_count)] = w.tok("tS")
                            w.wait("tensor", "wp%d" % gs, 16 * wpu)
                            for c in range(8):
                                w.do("tensor", lambda t, c_=c, tb_=tb, tt2_=tt2, gs_=gs, b_=pbk: t.matmul(
                                    bank(b_)[:, :], S["wpbuf"].ap()[:, gs_, c_, :],
                                    S["xnT"].ap()[:, c_, tb_ * TB + tt2_ * 512: tb_ * TB + (tt2_ + 1) * 512],
                                    start=(c_ == 0), stop=(c_ == 7)),
                                    inc="tS" if c == 7 else None)
                            tok[("p_t", gp_count)] = w.tok("tS")
                            if tt2 == TB // 512 - 1:
                                tok[("wg_slot_prev", gs)] = tok[("g_t", gp_count)]
                                tok[("wp_slot_prev", gs)] = tok[("p_t", gp_count)]
                            w.wait("scalar", "tS", tok[("g_t", gp_count)])
                            w.wait("scalar", "vS", tok.get(("h_v", gp_count - 2), 0))
                            w.do("scalar", lambda s, b_=gb, sl_=gp_count % 2: s.activation(
                                out=Ebuf[:, sl_, :], in_=bank(b_)[:, :], func=AF.Relu), inc="sS")
                            tok[("relu_s", gp_count)] = w.tok("sS")
                            w.wait("vector", "sS", tok[("relu_s", gp_count)])
                            w.wait("vector", "tS", tok[("p_t", gp_count)])
                            if gp_count - tb_gp0 < 2 and tok.get("area_a_tfree"):
                                w.wait("vector", "tS", tok["area_a_tfree"])
                            w.do("vector", lambda v, b_=pbk, sl_=gp_count % 2, ft_=ft, tt2_=tt2: v.tensor_tensor(
                                hbuf[:, ft_, tt2_ * 512:(tt2_ + 1) * 512],
                                Ebuf[:, sl_, :], bank(b_)[:, :], ALU.mult), inc="vS")
                            tok[("h_v", gp_count)] = w.tok("vS")
                            tok[("h_ft_v", tb, e, ft)] = w.tok("vS")
                            tok[("gpbank_v", pbk)] = w.tok("vS")
                            tok[("gpbank_s", gb)] = tok[("relu_s", gp_count)]
                            gp_count += 1
                        if filler:
                            filler.pop(0)()
                        if tb == 0 and e == 0 and 8 <= ft < NT:
                            i_t, bp_, j_ = ft, (ft // 4) % 2, ft % 4
                            if j_ < 2:
                                w.wait("vector", "sS", tok[("xn_s", i_t)])
                                w.do("vector", lambda v, i2=i_t, b2=bp_, j2=j_: v.tensor_scalar(
                                    S["resid"].ap()[:, i2, :], S["resid"].ap()[:, i2, :],
                                    stw[:, b2, j2, 0:1], stw[:, b2, j2, 6:7],
                                    ALU.mult, ALU.add), inc="vS")
                                tok[("resid_v", i_t)] = w.tok("vS")
                            else:
                                w.wait("scalar", "vS", tok[("cols_v", ft // 4)])
                                w.do("scalar", lambda s, i2=i_t, b2=bp_, j2=j_: s.activation(
                                    out=S["resid"].ap()[:, i2, :], in_=S["resid"].ap()[:, i2, :],
                                    func=AF.Identity, bias=stw[:, b2, j2, 6:7],
                                    scale=stw[:, b2, j2, 0:1]), inc="sS")
                                tok[("resid_v", i_t)] = w.tok("sS")
                    tok[("gp_done_v", tb, e)] = w.tok("vS")

                    for half in range(NHALF):
                        for f in range(16):
                            if half == 0 and f < 2:
                                if e + 1 < E:
                                    emit_wgp(tb, e + 1, f)
                                elif tb + 1 < NTB:
                                    emit_wgp(tb + 1, 0, f)
                            ws = f % 8
                            prevuse = max(tok.get(("wo_slot_prev", ws), 0),
                                          tok.get("areaC_tfree", 0))
                            w.wait("gpsimd", "tS", prevuse)
                            w.do("gpsimd", lambda q, e_=e, f_=f, ws_=ws: q.dma_start(
                                out=wobuf[:, ws_, :],
                                in_=dw["wexp"][e_, f_ * 128:(f_ + 1) * 128, :]),
                                inc="wo%d" % ws, k=16)
                            wo_uses[ws] += 1
                            w.wait("tensor", "wo%d" % ws, 16 * wo_uses[ws])
                            w.wait("tensor", "vS", tok.get(("h_ft_v", tb, e, f), 0))
                            if f == 0 and tb == 0 and e == 0:
                                w.wait("tensor", "vS", tok["w_v"])
                            dh_order = (0, 1) if f == 15 else (1, 0)
                            for dh in dh_order:
                                for ttl in range(4):
                                    bk = dh * 4 + ttl
                                    if f == 0:
                                        w.wait("tensor", "vS", tok.get(("sttb", bk), 0))
                                        if dh == 0:
                                            w.wait("tensor", "vS", tok.get(("gpbank_v", bk), 0))
                                            w.wait("tensor", "sS", tok.get(("gpbank_s", bk), 0))
                                    tt = half * 4 + ttl
                                    w.do("tensor", lambda t, f_=f, dh_=dh, tt_=tt, bk_=bk, ws_=ws, tb_=tb: t.matmul(
                                        bank(bk_)[:, :],
                                        hbuf[:, f_, tt_ * 128:(tt_ + 1) * 128],
                                        wobuf[:, ws_, dh_ * 512:(dh_ + 1) * 512],
                                        start=(f_ == 0), stop=(f_ == 15)),
                                        inc="tS" if (f == 15 and ttl == 3) else (
                                            "tS" if (f != 15 and dh == 0 and ttl == 3) else None))
                                if f == 15 and dh == 0:
                                    tok[("wo_mid_t", tb, e, half)] = w.tok("tS")
                            tok[("wo_slot_prev", ws)] = w.tok("tS")
                        tok[("woacc_t", tb, e, half)] = w.tok("tS")
                        w.wait("vector", "tS", tok[("wo_mid_t", tb, e, half)])
                        if e == 0 and tb == 0 and half == 0:
                            w.wait("vector", "vS", tok["w_v"])
                            w.wait("vector", "sS", tok[("resid_v", NT - 1)])
                        for phase2 in range(2):
                            if phase2 == 1:
                                w.wait("vector", "tS", tok[("woacc_t", tb, e, half)])
                            for ttl in range(4):
                                dh = phase2
                                bk = dh * 4 + ttl
                                gt_ = tb * NTT + half * 4 + ttl
                                w.do("vector", lambda v, bk_=bk, gt__=gt_, dh_=dh, e_=e: v.scalar_tensor_tensor(
                                    S["resid"].ap()[:, gt__, dh_ * 512:(dh_ + 1) * 512],
                                    bank(bk_)[:, :],
                                    S["w_tm"][:, gt__ * 8 + e_: gt__ * 8 + e_ + 1],
                                    S["resid"].ap()[:, gt__, dh_ * 512:(dh_ + 1) * 512],
                                    ALU.mult, ALU.add), inc="vS")
                                tok[("sttb", bk)] = w.tok("vS")
                        tok[("stt_v", tb, e, half)] = w.tok("vS")
                    tok["stt_prev_v"] = tok[("stt_v", tb, e, NHALF - 1)]
                    tok["bankfree_v"] = tok["stt_prev_v"]
                gp_state["gp"] = gp_count

            tok["psum_vfree"] = tok["laev_last"]
            tok["psum_sfree"] = tok["r_done_s"]
            do_ffn_tb(0, filler=list(gate_steps))

            w.wait("gpsimd", "tS", max(tok.get(("wo_slot_prev", s_), 0) for s_ in range(8)))
            w.do("gpsimd", lambda q: q.dma_start(
                out=WO, in_=dw["wo_attn"].rearrange("(c p) d -> p c d", p=128)),
                inc="qk", k=16)
            tok["wo_attn_qk"] = w.tok("qk")

            def flush_przb():
                pend = tok.get("pend")
                if pend is None:
                    return
                h_, qb_, qg_, ob_, rectok = pend
                w.wait("tensor", "vS", rectok)
                w.wait("tensor", "vS", tok.get("przb_cp"))
                w.do("tensor", lambda t: t.matmul(bank(4)[:, :], S["onesr"][:], rzv,
                                                  start=True, stop=True), inc="tS")
                tkp = w.tok("tS")
                w.wait("vector", "tS", tkp)
                w.do("vector", lambda v: v.tensor_copy(rzbv, bank(4)[:, :]), inc="vS")
                tok["przb_cp"] = w.tok("vS")
                w.do("vector", lambda v, h2=h_, qb2=qb_, ob2=ob_: v.tensor_tensor(
                    oT[:, h2, qb2 * 512:(qb2 + 1) * 512], bank(ob2)[:, :], rzbv,
                    ALU.mult), inc="vS")
                tok[("oev_v", qg_)] = w.tok("vS")
                tok["oev_last"] = w.tok("vS")
                tok["pend"] = None

            for h in range(H):
                pidx = 0
                w.wait("tensor", "sS", tok["r_done_s"])
                flushed = tok.get("pend") is not None and tok["pend"][2] % 2 == 0
                if flushed:
                    flush_przb()
                w.wait("tensor", "vS", max(tok.get(("oev_v", h * N5 - 2), 0),
                                           tok.get("stt_prev_v", 0),
                                           tok.get("oev_last", 0) if flushed else 0))

                def projchain(lhs_name, lhs_off, lhs_p, rhs_name, rhs_rows, bias_kind, bias_sl,
                              dst_fn, ev_eng, P_out):
                    nonlocal pidx
                    for tn in range(N5):
                        b = 2 + pidx % 2
                        prev = tok.get(("pj", h, pidx - 2))
                        w.wait("tensor", prev[0] if prev else "tS", prev[1] if prev else 0)
                        w.do("tensor", lambda t, tn_=tn, b_=b: t.matmul(
                            pb[0:P_out, b_ * 512:(b_ + 1) * 512],
                            S[lhs_name][:, lhs_off:lhs_off + lhs_p],
                            S[rhs_name][:, tn_ * 512:(tn_ + 1) * 512],
                            start=True, stop=True), inc="tS")
                        tk_ = w.tok("tS")
                        w.wait(ev_eng, "tS", tk_)
                        dst_fn(tn, b)
                        semn = "sS" if ev_eng == "scalar" else "vS"
                        tok[("pj", h, pidx)] = (semn, w.tok(semn))
                        pidx += 1

                def ev_kc(tn, b):
                    w.do("scalar", lambda s: s.activation(
                        out=kc_h[:, tn * 512:(tn + 1) * 512],
                        in_=pb[0:128, b * 512:(b + 1) * 512],
                        func=AF.Identity, bias=S["buk"][:, h:h + 1]), inc="sS")
                projchain("wuk", h * DK, DK, "ckv", DC, None, None, ev_kc, "scalar", 128)

                def ev_qc(tn, b):
                    w.do("scalar", lambda s: s.activation(
                        out=qc_h[:, tn * 512:(tn + 1) * 512],
                        in_=pb[0:128, b * 512:(b + 1) * 512],
                        func=AF.Identity, bias=S["buq"][:, h:h + 1]), inc="sS")
                projchain("wuq", h * DK, DK, "cq", DC, None, None, ev_qc, "scalar", 128)

                def ev_qr(tn, b):
                    w.do("scalar", lambda s: s.activation(
                        out=qr_h[:, tn * 512:(tn + 1) * 512],
                        in_=pb[0:DHR, b * 512:(b + 1) * 512],
                        func=AF.Identity, bias=S["bqr"][:, h:h + 1]), inc="sS")
                projchain("wqr", h * DHR, DHR, "cq", DC, None, None, ev_qr, "scalar", DHR)

                for tt in range(NT):
                    b = 2 + pidx % 2
                    prev = tok.get(("pj", h, pidx - 2))
                    w.wait("tensor", prev[0] if prev else "tS", prev[1] if prev else 0)
                    w.do("tensor", lambda t, tt_=tt, b_=b: t.matmul(
                        pb[:, b_ * 512:b_ * 512 + DK],
                        S["ckv"][:, tt_ * 128:(tt_ + 1) * 128],
                        S["wuv"][:, h * DK:(h + 1) * DK],
                        start=True, stop=True), inc="tS")
                    tk = w.tok("tS")
                    w.wait("vector", "tS", tk)
                    w.wait("vector", "vS", tok["bcasts_v"])
                    w.do("vector", lambda v, tt_=tt, b_=b: v.tensor_tensor(
                        v_h[:, tt_, :], pb[:, b_ * 512:b_ * 512 + DK],
                        S["bUVb"][:, h * DK:(h + 1) * DK], ALU.add), inc="vS")
                    tok[("pj", h, pidx)] = ("vS", w.tok("vS"))
                    pidx += 1
                tok[("proj_done", h)] = ("vS", w.tok("vS"), "sS", w.tok("sS"))

                for qb in range(N5):
                    qg = h * N5 + qb
                    p2 = qg % 2
                    ob = 2 if p2 == 0 else 6
                    zb = 3 if p2 == 0 else 7
                    w.wait("tensor", "vS", tok[("proj_done", h)][1])
                    w.wait("tensor", "sS", tok[("proj_done", h)][3])
                    for kt in range(NT + 2):
                        ko = kt - 2
                        if ko >= 0:
                            w.wait("tensor", "sS", tok[("exp_s", h, qb, ko)])
                        if kt == 0:
                            w.wait("tensor", "vS", tok.get(("oev_v", qg - 2),
                                                            tok.get("stt_prev_v", 0)))
                            w.wait("tensor", "sS", tok.get("exp_prev_last"))
                        if ko >= 0:
                            w.do("tensor", lambda t, kt_=ko, ob_=ob: t.matmul(
                                bank(ob_)[:, :], v_h[:, kt_, :], Ebuf[:, kt_ % 3, :],
                                start=(kt_ == 0), stop=(kt_ == NT - 1)),
                                inc="tS" if ko == NT - 1 else None)
                            w.do("tensor", lambda t, kt_=ko, zb_=zb: t.matmul(
                                bank(zb_)[:, :], S["onesm"][:], Ebuf[:, kt_ % 3, :],
                                start=(kt_ == 0), stop=(kt_ == NT - 1)),
                                inc="tS" if ko == NT - 1 else None)
                        if kt < NT:
                            b = (0, 1, 5)[kt % 3]
                            w.do("tensor", lambda t, kt_=kt, b_=b, qb_=qb: t.matmul(
                                bank(b_)[:, :], S["kr"][:, kt_ * 128:(kt_ + 1) * 128],
                                qr_h[:, qb_ * 512:(qb_ + 1) * 512],
                                start=True, stop=False))
                            w.do("tensor", lambda t, kt_=kt, b_=b, qb_=qb: t.matmul(
                                bank(b_)[:, :], kc_h[:, kt_ * 128:(kt_ + 1) * 128],
                                qc_h[:, qb_ * 512:(qb_ + 1) * 512],
                                start=False, stop=True), inc="tS")
                            tk = w.tok("tS")
                            w.wait("scalar", "tS", tk)
                            w.do("scalar", lambda s, kt_=kt, b_=b: s.activation(
                                out=Ebuf[:, kt_ % 3, :], in_=bank(b_)[:, :],
                                func=AF.Exp, scale=SC), inc="sS")
                            tok[("exp_s", h, qb, kt)] = w.tok("sS")
                        if kt == 5:
                            flush_przb()
                    tok[("oz_t", qg)] = w.tok("tS")
                    tok["exp_prev_last"] = tok[("exp_s", h, qb, NT - 1)]
                    w.wait("vector", "tS", tok[("oz_t", qg)])
                    w.do("vector", lambda v, zb_=zb: v.reciprocal(
                        rzv, pb[0:1, zb_ * 512:(zb_ + 1) * 512]), inc="vS")
                    tok["pend"] = (h, qb, qg, ob, w.tok("vS"))
            flush_przb()
            tok["attn_ov"] = w.tok("vS")
            tok["exp_all_s"] = w.tok("sS")

            widx = 0
            w.wait("tensor", "qk", tok["wo_attn_qk"])
            w.wait("vector", "sS", tok[("resid_v", NT - 1)])
            for qt in range(NT):
                for dh in range(2):
                    b = 5 + widx % 2
                    prev = tok.get(("woev_v", widx - 2))
                    w.wait("tensor", "vS", prev if prev else tok["attn_ov"])
                    for c in range(8):
                        w.do("tensor", lambda t, qt_=qt, dh_=dh, c_=c, b_=b: t.matmul(
                            bank(b_)[:, :], oT[:, c_, qt_ * 128:(qt_ + 1) * 128],
                            WO[:, c_, dh_ * 512:(dh_ + 1) * 512],
                            start=(c_ == 0), stop=False))
                    w.do("tensor", lambda t, dh_=dh, b_=b: t.matmul(
                        bank(b_)[:, :], S["onesm"][0:1, :],
                        S["borow"][0:1, dh_ * 512:(dh_ + 1) * 512],
                        start=False, stop=True), inc="tS")
                    tk = w.tok("tS")
                    w.wait("vector", "tS", tk)
                    w.do("vector", lambda v, qt_=qt, dh_=dh, b_=b: v.tensor_tensor(
                        S["resid"].ap()[:, qt_, dh_ * 512:(dh_ + 1) * 512],
                        S["resid"].ap()[:, qt_, dh_ * 512:(dh_ + 1) * 512],
                        bank(b_)[:, :], ALU.add), inc="vS")
                    tok[("woev_v", widx)] = w.tok("vS")
                    if widx == 2 * NTT - 1:
                        tok["woev_tb0_v"] = w.tok("vS")
                    widx += 1
            tok["WO_t"] = w.tok("tS")
            tok["WO_v"] = w.tok("vS")


            tok["psum_vfree"] = tok["WO_v"]
            tok["psum_sfree"] = tok["exp_all_s"]
            tok["area_a_tfree"] = tok["WO_t"]
            tok["areaC_tfree"] = tok["WO_t"]
            for tb2 in range(1, NTB):
                do_ffn_tb(tb2)
            w.wait("sync", "vS", max(tok["woev_tb0_v"], tok[("stt_v", 0, E - 1, NHALF - 1)]))
            w.do("sync", lambda q: q.dma_start(
                out=dy.rearrange("(i p) d -> p i d", p=128)[:, 0:NTT, :],
                in_=S["resid"].ap()[:, 0:NTT, :]),
                inc="qo", k=16)
            for tb2 in range(1, NTB):
                for hf in range(NHALF):
                    w.wait("sync", "vS", max(tok[("stt_v", tb2, E - 1, hf)], tok["WO_v"]))
                    w.do("sync", lambda q, tb_=tb2, hf_=hf: q.dma_start(
                        out=dy.rearrange("(i p) d -> p i d", p=128)
                            [:, tb_ * NTT + hf_ * 4:tb_ * NTT + hf_ * 4 + 4, :],
                        in_=S["resid"].ap()[:, tb_ * NTT + hf_ * 4:tb_ * NTT + hf_ * 4 + 4, :]),
                        inc="qo", k=16)
            w.wait("sync", "qo", 16 * (1 + (NTB - 1) * NHALF))

        finals = {}
        for name in ["sync", "gpsimd", "tensor", "vector", "scalar"]:
            deco = getattr(block, name)

            def mk(nm):
                def fn(px):
                    ww = Walk(nm, px, sems)
                    walk(ww)
                    finals[nm] = dict(ww.c)
                return fn
            deco(mk(name))
        ref_c = finals["sync"]
        for nm, c in finals.items():
            assert c == ref_c, (nm, {k: (c[k], ref_c[k]) for k in c if c[k] != ref_c[k]})

    return nc


def make_in_map(x_core, inputs):
    f32 = np.float32
    m = {"x": np.ascontiguousarray(np.asarray(x_core, f32))}
    key = {
        "wdq": "L0_attn_W_DQ", "bdq": "L0_attn_b_DQ", "wuq": "L0_attn_W_UQ", "buq": "L0_attn_b_UQ",
        "wdkv": "L0_attn_W_DKV", "bdkv": "L0_attn_b_DKV", "wuk": "L0_attn_W_UK", "buk": "L0_attn_b_UK",
        "wuv": "L0_attn_W_UV", "buv": "L0_attn_b_UV", "wqr": "L0_attn_W_QR", "bqr": "L0_attn_b_QR",
        "wkr": "L0_attn_W_KR", "bkr": "L0_attn_b_KR", "wo_attn": "L0_attn_W_O", "bo": "L0_attn_b_O",
        "rw": "L0_ffn_router_w", "rb": "L0_ffn_router_b", "temp": "L0_ffn_temperature",
        "wg": "L0_ffn_wg", "wp": "L0_ffn_wp", "wexp": "L0_ffn_wo",
    }
    for k, src in key.items():
        m[k] = np.ascontiguousarray(np.asarray(inputs[src], f32))
    m["eyeb"] = np.eye(128, dtype=BF)
    m["eyef"] = np.eye(8, dtype=f32)
    m["onesr"] = np.ones((1, 128), f32)
    m["onescb"] = np.ones((128, 1), BF)
    m["onescf"] = np.ones((128, 1), f32)
    m["onesm"] = np.ones((128, 128), BF)
    return m


def kernel(**inputs):
    from concourse.bass_utils import run_bass_kernel_spmd
    x = np.asarray(inputs["L0_attn_h_t"], np.float32)
    B, T, _ = x.shape
    nc = build_nc(T)
    in_maps = [make_in_map(x[b], inputs) for b in range(B)]
    res = run_bass_kernel_spmd(nc, in_maps, core_ids=list(range(B)))
    return np.stack([np.asarray(r["y"], np.float32) for r in res.results], axis=0)


# version 28
# speedup vs baseline: 1.1038x; 1.0130x over previous
import math
import contextlib
import numpy as np
import ml_dtypes

import concourse.bass as bass
import concourse.mybir as mybir

F32 = mybir.dt.float32
BF16 = mybir.dt.bfloat16
AF = mybir.ActivationFunctionType
ALU = mybir.AluOpType
AX = mybir.AxisListType

D = 1024
DC = 32
DHR = 32
H = 8
DK = 128
E = 8
FF = 2048
EPS = 1e-5
SC = 1.0 / math.sqrt(DK + DHR)

BF = ml_dtypes.bfloat16

SEM_NAMES = ["qc", "qk", "x0", "x1", "qo",
             "wg0", "wg1", "wg2", "wp0", "wp1", "wp2",
             "wo0", "wo1", "wo2", "wo3", "wo4", "wo5", "wo6", "wo7",
             "tS", "vS", "sS"]


class Walk:

    def __init__(self, en, px, sems):
        self.en = en
        self.px = px
        self.sems = sems
        self.c = {k: 0 for k in sems}
        self.seen = {}

    _SELF = {"vector": "vS", "scalar": "sS"}

    def do(self, eng, fn, inc=None, k=1):
        if eng == self.en:
            sw = self._SELF.get(eng)
            if sw is not None and self.c[sw] > 0:
                self.px.wait_ge(self.sems[sw], self.c[sw])
            inst = fn(self.px)
            if inc is not None:
                inst.then_inc(self.sems[inc], k)
        if inc is not None:
            self.c[inc] += k
            return self.c[inc]
        return None

    def tok(self, sem):
        return self.c[sem]

    def wait(self, eng, sem, v):
        if v is None or v <= 0:
            return
        if eng == self.en:
            key = sem
            if self.seen.get(key, -1) >= v:
                return
            self.seen[key] = v
            self.px.wait_ge(self.sems[sem], v)


def build_nc(T):
    assert T % 512 == 0
    NT = T // 128
    NTB = max(1, T // 1024)
    TB = T // NTB
    NTT = TB // 128
    NHALF = NTT // 4
    assert NTT % 4 == 0
    N5 = T // 512

    nc = bass.Bass("TRN2", target_bir_lowering=False)

    dx = nc.dram_tensor("x", [T, D], F32, kind="ExternalInput").ap()
    dy = nc.dram_tensor("y", [T, D], F32, kind="ExternalOutput").ap()
    dw = {}
    for nm, shp in [("wdq", [D, DC]), ("wdkv", [D, DC]), ("wkr", [D, DHR]),
                    ("wuq", [DC, H * DK]), ("wuk", [DC, H * DK]), ("wuv", [DC, H * DK]),
                    ("wqr", [DC, H * DHR]), ("wo_attn", [H * DK, D]),
                    ("rw", [E, D]), ("wg", [E, D, FF]), ("wp", [E, D, FF]), ("wexp", [E, FF, D])]:
        dw[nm] = nc.dram_tensor(nm, shp, F32, kind="ExternalInput").ap()
    for nm, shp in [("bdq", [DC]), ("bdkv", [DC]), ("bkr", [DHR]),
                    ("buq", [H * DK]), ("buk", [H * DK]), ("buv", [H * DK]),
                    ("bqr", [H * DHR]), ("bo", [D]), ("rb", [E]), ("temp", [1])]:
        dw[nm] = nc.dram_tensor(nm, shp, F32, kind="ExternalInput").ap()
    for nm, shp, dt in [("eyeb", [128, 128], BF16), ("eyef", [8, 8], F32),
                        ("onesr", [1, 128], F32), ("onescb", [128, 1], BF16),
                        ("onescf", [128, 1], F32), ("onesm", [128, 128], BF16)]:
        dw[nm] = nc.dram_tensor(nm, shp, dt, kind="ExternalInput").ap()

    with contextlib.ExitStack() as stack:
        en = stack.enter_context
        S = {}

        def sb(nm, shp, dt):
            S[nm] = en(nc.sbuf_tensor("s_" + nm, shp, dt))
            return S[nm]

        sb("eyeb", [128, 128], BF16); sb("eyef8", [8, 8], F32)
        sb("onesr", [1, 128], F32); sb("onescb", [128, 1], BF16); sb("onescf", [128, 1], F32)
        sb("onesm", [128, 128], BF16)
        sb("wcat", [128, 8, 96], BF16)
        sb("bdq", [DC, 1], F32); sb("bdkv", [DC, 1], F32); sb("bkr", [DHR, 1], F32)
        sb("wuq", [DC, H * DK], BF16); sb("wuk", [DC, H * DK], BF16)
        sb("wuv", [DC, H * DK], BF16); sb("wqr", [DC, H * DHR], BF16)
        sb("buq", [128, H], F32); sb("buk", [128, H], F32); sb("bqr", [DHR, H], F32)
        sb("bUVb", [128, D], BF16); sb("borow", [1, D], BF16)
        sb("rwT", [128, 8, E], BF16)
        sb("rbc", [E, 1], F32); sb("tempc", [1, 4], F32)
        sb("rcol", [E, 1], F32); sb("rbs", [E, 1], F32)
        sb("resid", [128, NT, D], F32)
        sb("xnT", [128, 8, T], BF16)
        sb("areaA", [128, max(8 * T, 16 * TB)], BF16)
        sb("areaC", [128, 8 * 1024], BF16)
        sb("wgbuf", [128, 3, 8, 128], BF16)
        sb("wpbuf", [128, 3, 8, 128], BF16)
        sb("areaB", [128, max(7168, 4 * T + 1536)], BF16)
        sb("statc", [128, 2, 4, 7], F32)
        sb("cq", [DC, T], BF16); sb("ckv", [DC, T], BF16); sb("kr", [DHR, T], BF16)
        sb("r_sb", [E, 2, 512], F32)
        sb("la", [128, NT * 8], F32)
        sb("gex", [128, NT * 8], F32)
        sb("gt", [128, NT * 8], F32)
        sb("gt2", [128, NT * 8], F32)
        sb("w_tm", [128, NT * 8], F32)
        sb("grow", [1, NT * 8], F32)
        sb("lse8", [1, 2 * E], F32)
        sb("epsc", [128, 1], F32)
        gl = nc.sbuf_bytes_remaining
        psum8 = en(nc.psum_tensor("psum8", [128, 4096], F32))
        sems = {s: en(nc.semaphore("sem_" + s)) for s in SEM_NAMES}
        block = en(nc.Block())

        oT = S["areaA"].ap()[:, 0:8 * T].rearrange("p (c t) -> p c t", c=8)
        hbuf = S["areaA"].ap()[:, 0:16 * TB].rearrange("p (f t) -> p f t", f=16)
        WO = S["areaC"].ap().rearrange("p (c d) -> p c d", c=8)
        wobuf = S["areaC"].ap().rearrange("p (s d) -> p s d", s=8)
        pb = psum8.ap()
        pbf = psum8.ap().bitcast(BF16)

        def bank(b, w=512):
            return pb[:, b * 512: b * 512 + w]

        def bankb(b, w=1024):
            return pbf[:, b * 1024: b * 1024 + w]

        rzv = S["r_sb"].ap()[0:1, 0, :]
        rzbv = S["wcat"].ap().rearrange("p a b -> p (a b)")[:, 0:512]
        aB = S["areaB"].ap()
        rowbuf = aB[0:1, 2048:4096].bitcast(F32)
        xnbf = aB[:, 4096:6144].rearrange("p (s d) -> p s d", s=2)
        sqout = aB[:, 6144:7168]
        qc_h = aB[:, 0:T]
        kc_h = aB[:, T:2 * T]
        v_h = aB[:, 2 * T:3 * T].rearrange("p (a b) -> p a b", b=DK)
        qr_h = aB[0:DHR, 3 * T:4 * T]
        Ebuf = aB[:, 4 * T:4 * T + 1536].rearrange("p (s d) -> p s d", s=3)
        rw8 = aB[0:E, 0:1024]
        lav = S["la"].ap().rearrange("p (a b) -> p a b", b=8)
        gexv = S["gex"].ap().rearrange("p (a b) -> p a b", b=8)
        gtv = S["gt"].ap().rearrange("p (a b) -> p a b", b=8)
        wv = S["w_tm"].ap().rearrange("p (a b) -> p a b", b=8)

        def walk(w):
            tok = {}

            for i0 in range(2):
                w.do("sync", lambda q, i_=i0: q.dma_start(
                    out=S["resid"].ap()[:, i_, :], in_=dx[i_ * 128:(i_ + 1) * 128, :]),
                    inc="x0" if i0 % 2 == 0 else "x1", k=16)
            nconst = 0
            for dst, src in [
                (S["eyeb"][:], dw["eyeb"]), (S["eyef8"][:], dw["eyef"]),
                (S["onesr"][:], dw["onesr"]), (S["onescb"][:], dw["onescb"]),
                (S["onescf"][:], dw["onescf"]), (S["onesm"][:], dw["onesm"]),
                (rowbuf, dw["buv"].rearrange("(o d) -> o d", o=1)),
            ]:
                w.do("sync", lambda q, d_=dst, s_=src: q.dma_start(out=d_, in_=s_), inc="qc", k=16)
                nconst += 1
            for dst, src in [
                (S["bdq"][:], dw["bdq"].rearrange("(p o) -> p o", o=1)),
                (S["bdkv"][:], dw["bdkv"].rearrange("(p o) -> p o", o=1)),
                (S["bkr"][:], dw["bkr"].rearrange("(p o) -> p o", o=1)),
                (S["rbc"][:], dw["rb"].rearrange("(p o) -> p o", o=1)),
                (S["tempc"][:, 0:1], dw["temp"].rearrange("(p o) -> p o", o=1)),
                (S["buq"][:], dw["buq"].rearrange("(h p) -> p h", p=128)),
                (S["buk"][:], dw["buk"].rearrange("(h p) -> p h", p=128)),
                (S["bqr"][:], dw["bqr"].rearrange("(h p) -> p h", p=DHR)),
            ]:
                w.do("sync", lambda q, d_=dst, s_=src: q.dma_start(
                    out=d_, in_=s_, allow_slow_non_contiguous=True), inc="qc", k=16)
                nconst += 1
            tok["consts"] = w.tok("qc")

            ncast = 0
            cast_loads = [
                (S["wcat"].ap()[:, :, 0:32], dw["wdq"].rearrange("(c p) m -> p c m", p=128)),
                (S["wcat"].ap()[:, :, 32:64], dw["wdkv"].rearrange("(c p) m -> p c m", p=128)),
                (S["wcat"].ap()[:, :, 64:96], dw["wkr"].rearrange("(c p) m -> p c m", p=128)),
                (S["wuq"][:], dw["wuq"]), (S["wuk"][:], dw["wuk"]),
                (S["wuv"][:], dw["wuv"]), (S["wqr"][:], dw["wqr"]),
                (rw8, dw["rw"]),
                (S["borow"][:], dw["bo"].rearrange("(o d) -> o d", o=1)),
            ]
            for dst, src in cast_loads:
                w.do("gpsimd", lambda q, d_=dst, s_=src: q.dma_start(out=d_, in_=s_),
                     inc="qk", k=16)
                ncast += 1
            tok["casts"] = w.tok("qk")

            w.wait("tensor", "qc", tok["consts"])
            for dh in range(2):
                b_use = 6 + dh % 2
                w.do("tensor", lambda t, d_=dh, b_=b_use: t.matmul(
                    bank(b_)[:, :], S["onesr"][:], rowbuf[:, d_ * 512:(d_ + 1) * 512],
                    start=True, stop=True), inc="tS")
                tk = w.tok("tS")
                w.wait("vector", "tS", tk)
                w.do("vector", lambda v, d_=dh, b_=b_use: v.tensor_copy(
                    S["bUVb"][:, d_ * 512:(d_ + 1) * 512], bank(b_)[:, :]), inc="vS")
                tok[("bc_v", b_use)] = w.tok("vS")
            tok["buvb_t"] = w.tok("tS")
            tok["bcasts_v"] = w.tok("vS")

            w.do("vector", lambda v: v.memset(S["epsc"][:], EPS), inc="vS")
            tok["eps_v"] = w.tok("vS")
            w.wait("vector", "qc", tok["consts"])
            w.do("vector", lambda v: v.tensor_scalar_max(S["tempc"][:, 1:2], S["tempc"][:, 0:1], 0.1), inc="vS")
            w.do("vector", lambda v: v.reciprocal(S["tempc"][:, 2:3], S["tempc"][:, 1:2]), inc="vS")
            tok["temp_v"] = w.tok("vS")
            w.wait("tensor", "vS", tok["temp_v"])
            w.do("tensor", lambda t: t.matmul(pb[0:E, 5 * 512:5 * 512 + 1],
                                              S["onesr"][0:1, 0:E], S["tempc"][0:1, 2:3],
                                              start=True, stop=True), inc="tS")
            tok["rcol_t"] = w.tok("tS")
            w.wait("vector", "tS", tok["rcol_t"])
            w.do("vector", lambda v: v.tensor_copy(S["rcol"][:], pb[0:E, 5 * 512:5 * 512 + 1]), inc="vS")
            w.do("vector", lambda v: v.tensor_tensor(S["rbs"][:], S["rbc"][:], S["rcol"][:], ALU.mult), inc="vS")
            tok["rbs_v"] = w.tok("vS")

            w.wait("tensor", "qk", tok["casts"])
            for c in range(8):
                b = c % 2
                if c >= 2:
                    w.wait("tensor", "vS", tok[("rwT_v", c - 2)])
                w.do("tensor", lambda t, c_=c, b_=b: t.transpose(
                    bankb(b_)[:, 0:E], rw8[:, c_ * 128:(c_ + 1) * 128], S["eyeb"][0:E, 0:E]),
                    inc="tS")
                tk = w.tok("tS")
                w.wait("vector", "tS", tk)
                w.do("vector", lambda v, c_=c, b_=b: v.tensor_copy(
                    S["rwT"].ap()[:, c_, :], bankb(b_)[:, 0:E]), inc="vS")
                tok[("rwT_v", c)] = w.tok("vS")
            tok["rwT_done"] = w.tok("vS")

            bank01 = {0: 0, 1: 0}
            gidx = 0
            NB = NT // 4
            stw = S["statc"].ap()
            for B in range(NB):
                bp = B % 2
                if B >= 2:
                    w.wait("vector", "sS", tok[("resid_v", (B - 2) * 4 + 3)])
                for j in range(4):
                    i = B * 4 + j
                    sl = i % 2
                    xs = "x0" if sl == 0 else "x1"
                    if i >= 2:
                        w.wait("sync", "vS", tok[("s1_v", i - 2)])
                        w.do("sync", lambda q, i_=i: q.dma_start(
                            out=S["resid"].ap()[:, i_, :], in_=dx[i_ * 128:(i_ + 1) * 128, :]),
                            inc=xs, k=16)
                    xtile = S["resid"].ap()[:, i, :]
                    w.wait("vector", xs, 16 * (i // 2 + 1))
                    w.do("vector", lambda v, x_=xtile, j_=j: v.reduce_sum(
                        stw[:, bp, j_, 0:1], x_, axis=AX.X), inc="vS")
                    tok[("s1_v", i)] = w.tok("vS")
                    w.wait("scalar", xs, 16 * (i // 2 + 1))
                    if B >= 2 and j == 0:
                        w.wait("scalar", "vS", tok[("s1_v", i)])
                    w.do("scalar", lambda s, x_=xtile, j_=j: s.activation(
                        out=sqout, in_=x_, func=AF.Square, accum_out=stw[:, bp, j_, 1:2]), inc="sS")
                tok[("sq_s", B)] = w.tok("sS")
                w.wait("vector", "sS", tok[("sq_s", B)])
                w.do("vector", lambda v: v.tensor_scalar_mul(stw[:, bp, :, 2:3], stw[:, bp, :, 0:1], 1.0 / D), inc="vS")
                w.do("vector", lambda v: v.tensor_scalar_mul(stw[:, bp, :, 3:4], stw[:, bp, :, 1:2], 1.0 / D), inc="vS")
                w.do("vector", lambda v: v.tensor_tensor(stw[:, bp, :, 4:5], stw[:, bp, :, 2:3], stw[:, bp, :, 2:3], ALU.mult), inc="vS")
                w.do("vector", lambda v: v.tensor_tensor(stw[:, bp, :, 3:4], stw[:, bp, :, 3:4], stw[:, bp, :, 4:5], ALU.subtract), inc="vS")
                tok[("var_v", B)] = w.tok("vS")
                w.wait("scalar", "vS", tok[("var_v", B)])
                w.do("scalar", lambda s: s.activation(
                    out=stw[:, bp, :, 4:5], in_=stw[:, bp, :, 3:4], func=AF.Sqrt,
                    bias=S["epsc"][:]), inc="sS")
                tok[("sqrt_s", B)] = w.tok("sS")
                w.wait("vector", "sS", tok[("sqrt_s", B)])
                w.do("vector", lambda v: v.reciprocal(stw[:, bp, :, 5:6], stw[:, bp, :, 4:5]), inc="vS")
                w.do("vector", lambda v: v.tensor_tensor(stw[:, bp, :, 6:7], stw[:, bp, :, 2:3], stw[:, bp, :, 5:6], ALU.mult), inc="vS")
                w.do("vector", lambda v: v.tensor_scalar_mul(stw[:, bp, :, 6:7], stw[:, bp, :, 6:7], -1.0), inc="vS")
                w.do("vector", lambda v: v.tensor_scalar_add(stw[:, bp, :, 0:1], stw[:, bp, :, 5:6], 1.0), inc="vS")
                tok[("cols_v", B)] = w.tok("vS")
                w.wait("scalar", "vS", tok[("cols_v", B)])
                for j in range(4):
                    i = B * 4 + j
                    sl = i % 2
                    xtile = S["resid"].ap()[:, i, :]
                    if i >= 2:
                        w.wait("scalar", "tS", tok[("tr_t", i - 2)])
                    w.do("scalar", lambda s, x_=xtile, j_=j, sl_=sl: s.activation(
                        out=xnbf[:, sl_, :], in_=x_, func=AF.Identity,
                        bias=stw[:, bp, j_, 6:7], scale=stw[:, bp, j_, 5:6]), inc="sS")
                    tok[("xn_s", i)] = w.tok("sS")
                    if B < 2:
                        if j < 2:
                            w.wait("vector", "sS", tok[("xn_s", i)])
                            w.do("vector", lambda v, x_=xtile, j_=j, i_=i: v.tensor_scalar(
                                S["resid"].ap()[:, i_, :], x_, stw[:, bp, j_, 0:1],
                                stw[:, bp, j_, 6:7], ALU.mult, ALU.add), inc="vS")
                            tok[("resid_v", i)] = w.tok("vS")
                        else:
                            w.do("scalar", lambda s, x_=xtile, j_=j, i_=i: s.activation(
                                out=S["resid"].ap()[:, i_, :], in_=x_, func=AF.Identity,
                                bias=stw[:, bp, j_, 6:7], scale=stw[:, bp, j_, 0:1]), inc="sS")
                            tok[("resid_v", i)] = w.tok("sS")
                    w.wait("tensor", "sS", tok[("xn_s", i)])
                    for c in range(8):
                        b = (i * 8 + c) % 2
                        w.wait("tensor", "vS", bank01[b])
                        w.do("tensor", lambda t, c_=c, b_=b, sl_=sl: t.transpose(
                            bankb(b_)[:, 0:128], xnbf[:, sl_, c_ * 128:(c_ + 1) * 128],
                            S["eyeb"][:]), inc="tS")
                        tk = w.tok("tS")
                        w.wait("vector", "tS", tk)
                        w.do("vector", lambda v, c_=c, b_=b, i_=i: v.tensor_copy(
                            S["xnT"].ap()[:, c_, i_ * 128:(i_ + 1) * 128], bankb(b_)[:, 0:128]), inc="vS")
                        bank01[b] = w.tok("vS")
                        tok[("trev_v", i * 8 + c)] = w.tok("vS")
                    tok[("tr_t", i)] = w.tok("tS")
                i = B * 4 + 3
                tn = B
                xready = tok[("trev_v", i * 8 + 7)]
                if tn == 0:
                    w.wait("tensor", "qk", tok["casts"])
                    w.wait("tensor", "vS", tok["rwT_done"])
                w.wait("tensor", "vS", xready)
                for kind, off, bias, dst in [("cq", 0, "bdq", "cq"), ("ckv", 32, "bdkv", "ckv"),
                                             ("kr", 64, "bkr", "kr")]:
                    P = DC if kind != "kr" else DHR
                    b = 2 + gidx % 2
                    prev = tok.get(("cev_s", gidx - 2))
                    w.wait("tensor", "sS", prev)
                    for c in range(8):
                        w.do("tensor", lambda t, c_=c, b_=b, o_=off, tn_=tn, P_=P: t.matmul(
                            pb[0:P_, b_ * 512:(b_ + 1) * 512],
                            S["wcat"].ap()[:, c_, o_:o_ + P_],
                            S["xnT"].ap()[:, c_, tn_ * 512:(tn_ + 1) * 512],
                            start=(c_ == 0), stop=(c_ == 7)),
                            inc="tS" if c == 7 else None)
                    tk = w.tok("tS")
                    w.wait("scalar", "tS", tk)
                    w.do("scalar", lambda s, b_=b, tn_=tn, P_=P, bi=bias, d_=dst: s.activation(
                        out=S[d_][:, tn_ * 512:(tn_ + 1) * 512],
                        in_=pb[0:P_, b_ * 512:(b_ + 1) * 512],
                        func=AF.Identity, bias=S[bi][:]), inc="sS")
                    tok[("cev_s", gidx)] = w.tok("sS")
                    gidx += 1
                b = 4 + tn % 2
                prev = tok.get(("rev_s", tn - 2))
                w.wait("tensor", "sS", prev)
                if tn < 2:
                    w.wait("tensor", "vS", tok["rbs_v"])
                for c in range(8):
                    w.do("tensor", lambda t, c_=c, b_=b, tn_=tn: t.matmul(
                        pb[0:E, b_ * 512:(b_ + 1) * 512],
                        S["rwT"].ap()[:, c_, :],
                        S["xnT"].ap()[:, c_, tn_ * 512:(tn_ + 1) * 512],
                        start=(c_ == 0), stop=(c_ == 7)),
                        inc="tS" if c == 7 else None)
                tk = w.tok("tS")
                w.wait("scalar", "tS", tk)
                w.wait("scalar", "vS", tok["rbs_v"])
                w.do("scalar", lambda s, b_=b, tn_=tn: s.activation(
                    out=S["r_sb"].ap()[:, tn_ % 2, :],
                    in_=pb[0:E, b_ * 512:(b_ + 1) * 512],
                    func=AF.Identity, bias=S["rbs"][:], scale=S["rcol"][:]), inc="sS")
                tok[("rev_s", tn)] = w.tok("sS")
                w.wait("tensor", "sS", tok[("rev_s", tn)])
                for j in range(4):
                    ii = tn * 4 + j
                    b2 = ii % 2
                    w.wait("tensor", "vS", bank01[b2])
                    w.do("tensor", lambda t, tn_=tn, j_=j, b_=b2: t.transpose(
                        bank(b_, 8)[:, :], S["r_sb"].ap()[:, tn_ % 2, j_ * 128:(j_ + 1) * 128],
                        S["eyef8"][:]), inc="tS")
                    tk2 = w.tok("tS")
                    w.wait("vector", "tS", tk2)
                    w.do("vector", lambda v, i_=ii, b_=b2: v.tensor_copy(
                        lav[:, i_, :], bank(b_, 8)[:, :]), inc="vS")
                    bank01[b2] = w.tok("vS")
                    tok[("laev_v", ii)] = w.tok("vS")
                tok[("rtr_t", tn)] = w.tok("tS")
            tok["xnT_v"] = w.tok("vS")
            tok["c_done_s"] = w.tok("sS")
            tok["r_done_s"] = w.tok("sS")
            tok["laev_last"] = w.tok("vS")

            NE = NT * 8
            gate_steps = []
            def _gs(fn):
                gate_steps.append(fn)
            for it in range(3):
                def stepA(it_=it):
                    w.wait("scalar", "vS", w.tok("vS"))
                    w.do("scalar", lambda s: s.activation(out=S["gex"][:], in_=S["la"][:], func=AF.Exp), inc="sS")
                    sk = w.tok("sS")
                    w.wait("vector", "sS", sk)
                    w.do("vector", lambda v: v.reduce_sum(gtv[:, :, 0:1], gexv, axis=AX.X), inc="vS")
                    vk = w.tok("vS")
                    w.wait("scalar", "vS", vk)
                    w.do("scalar", lambda s: s.activation(out=S["gt2"][:, 0:NT], in_=gtv[:, :, 0:1].squeeze(2),
                                                          func=AF.Ln), inc="sS")
                    sk = w.tok("sS")
                    w.wait("vector", "sS", sk)
                    w.do("vector", lambda v: v.tensor_tensor(
                        lav, lav, S["gt2"].ap()[:, 0:NT].unsqueeze(2).broadcast_to([128, NT, 8]),
                        ALU.subtract), inc="vS")
                    vk = w.tok("vS")
                    w.wait("scalar", "vS", vk)
                    w.do("scalar", lambda s: s.activation(out=S["gex"][:], in_=S["la"][:], func=AF.Exp), inc="sS")
                    tok[("sink_exp2", it_)] = w.tok("sS")
                _gs(stepA)

                def stepB(it_=it):
                    w.wait("tensor", "sS", tok[("sink_exp2", it_)])
                    w.wait("tensor", "vS", tok.get(("lse_v", it_ - 1)))
                    w.do("tensor", lambda t: t.matmul(pb[0:1, 6 * 512:6 * 512 + NE],
                                                      S["onescf"][:], S["gex"][:],
                                                      start=True, stop=True), inc="tS")
                    tk = w.tok("tS")
                    w.wait("vector", "tS", tk)
                    w.do("vector", lambda v: v.reduce_sum(
                        S["lse8"][0:1, 0:E],
                        pb[0:1, 6 * 512:6 * 512 + NE].rearrange("o (a b) -> o b a", b=8),
                        axis=AX.X), inc="vS")
                    vk = w.tok("vS")
                    w.wait("scalar", "vS", vk)
                    w.do("scalar", lambda s: s.activation(out=S["lse8"][0:1, E:2 * E], in_=S["lse8"][0:1, 0:E], func=AF.Ln), inc="sS")
                    sk = w.tok("sS")
                    w.wait("vector", "sS", sk)
                    w.do("vector", lambda v: v.tensor_copy(
                        S["grow"].ap().rearrange("o (a b) -> o a b", b=8),
                        S["lse8"][0:1, E:2 * E].unsqueeze(1).broadcast_to([1, NT, 8])), inc="vS")
                    tok[("sink_rep", it_)] = w.tok("vS")
                _gs(stepB)

                def stepC(it_=it):
                    w.wait("tensor", "vS", tok[("sink_rep", it_)])
                    w.do("tensor", lambda t: t.matmul(pb[:, 7 * 512:7 * 512 + NE],
                                                      S["onesr"][:], S["grow"][:],
                                                      start=True, stop=True), inc="tS")
                    tk = w.tok("tS")
                    w.wait("vector", "tS", tk)
                    w.do("vector", lambda v: v.tensor_tensor(
                        S["la"][:], S["la"][:], pb[:, 7 * 512:7 * 512 + NE], ALU.subtract), inc="vS")
                    tok[("lse_v", it_)] = w.tok("vS")
                _gs(stepC)

            def stepFin():
              w.wait("scalar", "vS", w.tok("vS"))
              w.do("scalar", lambda s: s.activation(out=S["gex"][:], in_=S["la"][:], func=AF.Exp), inc="sS")
              sk = w.tok("sS")
              w.wait("vector", "sS", sk)
              w.do("vector", lambda v: v.reduce_sum(gtv[:, :, 0:1], gexv, axis=AX.X), inc="vS")
              w.do("vector", lambda v: v.tensor_scalar_add(gtv[:, :, 1:2], gtv[:, :, 0:1], 1e-8), inc="vS")
              w.do("vector", lambda v: v.reciprocal(gtv[:, :, 2:3], gtv[:, :, 1:2]), inc="vS")
              w.do("vector", lambda v: v.tensor_tensor(
                  gexv, gexv, gtv[:, :, 2:3].broadcast_to([128, NT, 8]), ALU.mult), inc="vS")
              g4 = S["gex"].ap().rearrange("p (a b c) -> p a b c", b=4, c=2)
              hi = S["gt"].ap().rearrange("p (a b) -> p a b", b=8)[:, :, 0:4]
              lo = S["gt"].ap().rearrange("p (a b) -> p a b", b=8)[:, :, 4:8]
              w.do("vector", lambda v: v.reduce_max(hi, g4, axis=AX.X), inc="vS")
              w.do("vector", lambda v: v.tensor_reduce(lo, g4, axis=AX.X, op=ALU.min), inc="vS")
              t2 = S["gt2"].ap().rearrange("p (a b) -> p a b", b=8)
              w.do("vector", lambda v: v.tensor_tensor(t2[:, :, 0:2], hi[:, :, 0:4:2], hi[:, :, 1:4:2], ALU.max), inc="vS")
              w.do("vector", lambda v: v.tensor_tensor(t2[:, :, 2:4], lo[:, :, 0:4:2], hi[:, :, 1:4:2], ALU.max), inc="vS")
              w.do("vector", lambda v: v.tensor_tensor(t2[:, :, 4:6], lo[:, :, 1:4:2], hi[:, :, 0:4:2], ALU.max), inc="vS")
              w.do("vector", lambda v: v.tensor_tensor(t2[:, :, 6:8], t2[:, :, 2:4], t2[:, :, 4:6], ALU.min), inc="vS")
              w.do("vector", lambda v: v.tensor_tensor(gtv[:, :, 0:1], t2[:, :, 0:1], t2[:, :, 1:2], ALU.max), inc="vS")
              w.do("vector", lambda v: v.tensor_tensor(gtv[:, :, 3:4], t2[:, :, 6:7], t2[:, :, 1:2], ALU.max), inc="vS")
              w.do("vector", lambda v: v.tensor_tensor(gtv[:, :, 4:5], t2[:, :, 7:8], t2[:, :, 0:1], ALU.max), inc="vS")
              w.do("vector", lambda v: v.tensor_tensor(gtv[:, :, 1:2], gtv[:, :, 3:4], gtv[:, :, 4:5], ALU.min), inc="vS")
              w.do("vector", lambda v: v.tensor_tensor(gtv[:, :, 5:6], gtv[:, :, 0:1], gtv[:, :, 1:2], ALU.add), inc="vS")
              w.do("vector", lambda v: v.tensor_scalar_add(gtv[:, :, 5:6], gtv[:, :, 5:6], 1e-8), inc="vS")
              w.do("vector", lambda v: v.reciprocal(gtv[:, :, 6:7], gtv[:, :, 5:6]), inc="vS")
              w.do("vector", lambda v: v.tensor_tensor(
                  wv, gexv, gtv[:, :, 1:2].broadcast_to([128, NT, 8]), ALU.is_ge), inc="vS")
              w.do("vector", lambda v: v.tensor_tensor(wv, wv, gexv, ALU.mult), inc="vS")
              w.do("vector", lambda v: v.tensor_tensor(
                  wv, wv, gtv[:, :, 6:7].broadcast_to([128, NT, 8]), ALU.mult), inc="vS")
              tok["w_v"] = w.tok("vS")
            _gs(stepFin)


            wg_uses = {0: 0, 1: 0, 2: 0}
            wp_uses = {0: 0, 1: 0, 2: 0}
            wgp_ci = {"i": 0}
            wgp_emitted = set()

            def emit_wgp(tb_, e_, ft_):
                if (tb_, e_, ft_) in wgp_emitted:
                    return
                wgp_emitted.add((tb_, e_, ft_))
                gs = wgp_ci["i"] % 3
                wgp_ci["i"] += 1
                w.wait("gpsimd", "tS", tok.get(("wg_slot_prev", gs)))
                w.do("gpsimd", lambda q, e2=e_, ft2=ft_, gs2=gs: q.dma_start(
                    out=S["wgbuf"].ap()[:, gs2, :, :],
                    in_=dw["wg"][e2, :, :].rearrange("(c p) f -> p c f", p=128)
                        [:, :, ft2 * 128:(ft2 + 1) * 128]),
                    inc="wg%d" % gs, k=16)
                wg_uses[gs] += 1
                w.wait("gpsimd", "tS", tok.get(("wp_slot_prev", gs)))
                w.do("gpsimd", lambda q, e2=e_, ft2=ft_, gs2=gs: q.dma_start(
                    out=S["wpbuf"].ap()[:, gs2, :, :],
                    in_=dw["wp"][e2, :, :].rearrange("(c p) f -> p c f", p=128)
                        [:, :, ft2 * 128:(ft2 + 1) * 128]),
                    inc="wp%d" % gs, k=16)
                wp_uses[gs] += 1
                tok[("wgp_slot_of", tb_, e_, ft_)] = (gs, wg_uses[gs], wp_uses[gs])
            wo_uses = {s: 0 for s in range(8)}
            gp_state = {"gp": 0}

            def do_ffn_tb(tb, filler=None):
                gp_count = gp_state["gp"]
                tb_gp0 = gp_count
                for e in range(E):
                    for ft in range(16):
                        emit_wgp(tb, e, ft)
                        gs, wgu, wpu = tok[("wgp_slot_of", tb, e, ft)]
                        for tt2 in range(TB // 512):
                            par = gp_count % 2
                            gb, pbk = 2 * par, 2 * par + 1
                            needv = max(tok.get(("h_v", gp_count - 2), 0),
                                        tok.get(("sttb", gb), 0),
                                        tok.get(("sttb", pbk), 0),
                                        tok.get("psum_vfree", 0) if gp_count - tb_gp0 < 2 else 0)
                            w.wait("tensor", "vS", needv)
                            w.wait("tensor", "sS", tok.get(("relu_s", gp_count - 2),
                                                           tok.get("psum_sfree", 0)))
                            w.wait("tensor", "wg%d" % gs, 16 * wgu)
                            for c in range(8):
                                w.do("tensor", lambda t, c_=c, tb_=tb, tt2_=tt2, gs_=gs, b_=gb: t.matmul(
                                    bank(b_)[:, :], S["wgbuf"].ap()[:, gs_, c_, :],
                                    S["xnT"].ap()[:, c_, tb_ * TB + tt2_ * 512: tb_ * TB + (tt2_ + 1) * 512],
                                    start=(c_ == 0), stop=(c_ == 7)),
                                    inc="tS" if c == 7 else None)
                            tok[("g_t", gp_count)] = w.tok("tS")
                            w.wait("tensor", "wp%d" % gs, 16 * wpu)
                            for c in range(8):
                                w.do("tensor", lambda t, c_=c, tb_=tb, tt2_=tt2, gs_=gs, b_=pbk: t.matmul(
                                    bank(b_)[:, :], S["wpbuf"].ap()[:, gs_, c_, :],
                                    S["xnT"].ap()[:, c_, tb_ * TB + tt2_ * 512: tb_ * TB + (tt2_ + 1) * 512],
                                    start=(c_ == 0), stop=(c_ == 7)),
                                    inc="tS" if c == 7 else None)
                            tok[("p_t", gp_count)] = w.tok("tS")
                            if tt2 == TB // 512 - 1:
                                tok[("wg_slot_prev", gs)] = tok[("g_t", gp_count)]
                                tok[("wp_slot_prev", gs)] = tok[("p_t", gp_count)]
                            w.wait("scalar", "tS", tok[("g_t", gp_count)])
                            w.wait("scalar", "vS", tok.get(("h_v", gp_count - 2), 0))
                            w.do("scalar", lambda s, b_=gb, sl_=gp_count % 2: s.activation(
                                out=Ebuf[:, sl_, :], in_=bank(b_)[:, :], func=AF.Relu), inc="sS")
                            tok[("relu_s", gp_count)] = w.tok("sS")
                            w.wait("vector", "sS", tok[("relu_s", gp_count)])
                            w.wait("vector", "tS", tok[("p_t", gp_count)])
                            if gp_count - tb_gp0 < 2 and tok.get("area_a_tfree"):
                                w.wait("vector", "tS", tok["area_a_tfree"])
                            w.do("vector", lambda v, b_=pbk, sl_=gp_count % 2, ft_=ft, tt2_=tt2: v.tensor_tensor(
                                hbuf[:, ft_, tt2_ * 512:(tt2_ + 1) * 512],
                                Ebuf[:, sl_, :], bank(b_)[:, :], ALU.mult), inc="vS")
                            tok[("h_v", gp_count)] = w.tok("vS")
                            tok[("h_ft_v", tb, e, ft)] = w.tok("vS")
                            tok[("gpbank_v", pbk)] = w.tok("vS")
                            tok[("gpbank_s", gb)] = tok[("relu_s", gp_count)]
                            gp_count += 1
                        if filler:
                            filler.pop(0)()
                    tok[("gp_done_v", tb, e)] = w.tok("vS")

                    for half in range(NHALF):
                        for f in range(16):
                            if half == 0 and f < 2:
                                if e + 1 < E:
                                    emit_wgp(tb, e + 1, f)
                                elif tb + 1 < NTB:
                                    emit_wgp(tb + 1, 0, f)
                            ws = f % 8
                            prevuse = max(tok.get(("wo_slot_prev", ws), 0),
                                          tok.get("areaC_tfree", 0))
                            w.wait("gpsimd", "tS", prevuse)
                            w.do("gpsimd", lambda q, e_=e, f_=f, ws_=ws: q.dma_start(
                                out=wobuf[:, ws_, :],
                                in_=dw["wexp"][e_, f_ * 128:(f_ + 1) * 128, :]),
                                inc="wo%d" % ws, k=16)
                            wo_uses[ws] += 1
                            w.wait("tensor", "wo%d" % ws, 16 * wo_uses[ws])
                            w.wait("tensor", "vS", tok.get(("h_ft_v", tb, e, f), 0))
                            if f == 0 and tb == 0 and e == 0:
                                w.wait("tensor", "vS", tok["w_v"])
                            dh_order = (0, 1) if f == 15 else (1, 0)
                            for dh in dh_order:
                                for ttl in range(4):
                                    bk = dh * 4 + ttl
                                    if f == 0:
                                        w.wait("tensor", "vS", tok.get(("sttb", bk), 0))
                                        if dh == 0:
                                            w.wait("tensor", "vS", tok.get(("gpbank_v", bk), 0))
                                            w.wait("tensor", "sS", tok.get(("gpbank_s", bk), 0))
                                    tt = half * 4 + ttl
                                    w.do("tensor", lambda t, f_=f, dh_=dh, tt_=tt, bk_=bk, ws_=ws, tb_=tb: t.matmul(
                                        bank(bk_)[:, :],
                                        hbuf[:, f_, tt_ * 128:(tt_ + 1) * 128],
                                        wobuf[:, ws_, dh_ * 512:(dh_ + 1) * 512],
                                        start=(f_ == 0), stop=(f_ == 15)),
                                        inc="tS" if (f == 15 and ttl == 3) else (
                                            "tS" if (f != 15 and dh == 0 and ttl == 3) else None))
                                if f == 15 and dh == 0:
                                    tok[("wo_mid_t", tb, e, half)] = w.tok("tS")
                            tok[("wo_slot_prev", ws)] = w.tok("tS")
                        tok[("woacc_t", tb, e, half)] = w.tok("tS")
                        w.wait("vector", "tS", tok[("wo_mid_t", tb, e, half)])
                        if e == 0 and tb == 0 and half == 0:
                            w.wait("vector", "vS", tok["w_v"])
                            w.wait("vector", "sS", tok[("resid_v", min(7, NT - 1))])
                        for phase2 in range(2):
                            if phase2 == 1:
                                w.wait("vector", "tS", tok[("woacc_t", tb, e, half)])
                            for ttl in range(4):
                                dh = phase2
                                bk = dh * 4 + ttl
                                gt_ = tb * NTT + half * 4 + ttl
                                w.do("vector", lambda v, bk_=bk, gt__=gt_, dh_=dh, e_=e: v.scalar_tensor_tensor(
                                    S["resid"].ap()[:, gt__, dh_ * 512:(dh_ + 1) * 512],
                                    bank(bk_)[:, :],
                                    S["w_tm"][:, gt__ * 8 + e_: gt__ * 8 + e_ + 1],
                                    S["resid"].ap()[:, gt__, dh_ * 512:(dh_ + 1) * 512],
                                    ALU.mult, ALU.add), inc="vS")
                                tok[("sttb", bk)] = w.tok("vS")
                        tok[("stt_v", tb, e, half)] = w.tok("vS")
                    tok["stt_prev_v"] = tok[("stt_v", tb, e, NHALF - 1)]
                    tok["bankfree_v"] = tok["stt_prev_v"]
                gp_state["gp"] = gp_count

            tok["psum_vfree"] = tok["laev_last"]
            tok["psum_sfree"] = tok["r_done_s"]
            do_ffn_tb(0, filler=list(gate_steps))

            w.wait("gpsimd", "tS", max(tok.get(("wo_slot_prev", s_), 0) for s_ in range(8)))
            w.do("gpsimd", lambda q: q.dma_start(
                out=WO, in_=dw["wo_attn"].rearrange("(c p) d -> p c d", p=128)),
                inc="qk", k=16)
            tok["wo_attn_qk"] = w.tok("qk")

            def flush_przb():
                pend = tok.get("pend")
                if pend is None:
                    return
                h_, qb_, qg_, ob_, rectok = pend
                w.wait("tensor", "vS", rectok)
                w.wait("tensor", "vS", tok.get("przb_cp"))
                w.do("tensor", lambda t: t.matmul(bank(4)[:, :], S["onesr"][:], rzv,
                                                  start=True, stop=True), inc="tS")
                tkp = w.tok("tS")
                w.wait("vector", "tS", tkp)
                w.do("vector", lambda v: v.tensor_copy(rzbv, bank(4)[:, :]), inc="vS")
                tok["przb_cp"] = w.tok("vS")
                w.do("vector", lambda v, h2=h_, qb2=qb_, ob2=ob_: v.tensor_tensor(
                    oT[:, h2, qb2 * 512:(qb2 + 1) * 512], bank(ob2)[:, :], rzbv,
                    ALU.mult), inc="vS")
                tok[("oev_v", qg_)] = w.tok("vS")
                tok["oev_last"] = w.tok("vS")
                tok["pend"] = None

            for h in range(H):
                pidx = 0
                w.wait("tensor", "sS", tok["r_done_s"])
                flushed = tok.get("pend") is not None and tok["pend"][2] % 2 == 0
                if flushed:
                    flush_przb()
                w.wait("tensor", "vS", max(tok.get(("oev_v", h * N5 - 2), 0),
                                           tok.get("stt_prev_v", 0),
                                           tok.get("oev_last", 0) if flushed else 0))

                def projchain(lhs_name, lhs_off, lhs_p, rhs_name, rhs_rows, bias_kind, bias_sl,
                              dst_fn, ev_eng, P_out):
                    nonlocal pidx
                    for tn in range(N5):
                        b = 2 + pidx % 2
                        prev = tok.get(("pj", h, pidx - 2))
                        w.wait("tensor", prev[0] if prev else "tS", prev[1] if prev else 0)
                        w.do("tensor", lambda t, tn_=tn, b_=b: t.matmul(
                            pb[0:P_out, b_ * 512:(b_ + 1) * 512],
                            S[lhs_name][:, lhs_off:lhs_off + lhs_p],
                            S[rhs_name][:, tn_ * 512:(tn_ + 1) * 512],
                            start=True, stop=True), inc="tS")
                        tk_ = w.tok("tS")
                        w.wait(ev_eng, "tS", tk_)
                        dst_fn(tn, b)
                        semn = "sS" if ev_eng == "scalar" else "vS"
                        tok[("pj", h, pidx)] = (semn, w.tok(semn))
                        pidx += 1

                def ev_kc(tn, b):
                    w.do("scalar", lambda s: s.activation(
                        out=kc_h[:, tn * 512:(tn + 1) * 512],
                        in_=pb[0:128, b * 512:(b + 1) * 512],
                        func=AF.Identity, bias=S["buk"][:, h:h + 1]), inc="sS")
                projchain("wuk", h * DK, DK, "ckv", DC, None, None, ev_kc, "scalar", 128)

                def ev_qc(tn, b):
                    w.do("scalar", lambda s: s.activation(
                        out=qc_h[:, tn * 512:(tn + 1) * 512],
                        in_=pb[0:128, b * 512:(b + 1) * 512],
                        func=AF.Identity, bias=S["buq"][:, h:h + 1]), inc="sS")
                projchain("wuq", h * DK, DK, "cq", DC, None, None, ev_qc, "scalar", 128)

                def ev_qr(tn, b):
                    w.do("scalar", lambda s: s.activation(
                        out=qr_h[:, tn * 512:(tn + 1) * 512],
                        in_=pb[0:DHR, b * 512:(b + 1) * 512],
                        func=AF.Identity, bias=S["bqr"][:, h:h + 1]), inc="sS")
                projchain("wqr", h * DHR, DHR, "cq", DC, None, None, ev_qr, "scalar", DHR)

                for tt in range(NT):
                    b = 2 + pidx % 2
                    prev = tok.get(("pj", h, pidx - 2))
                    w.wait("tensor", prev[0] if prev else "tS", prev[1] if prev else 0)
                    w.do("tensor", lambda t, tt_=tt, b_=b: t.matmul(
                        pb[:, b_ * 512:b_ * 512 + DK],
                        S["ckv"][:, tt_ * 128:(tt_ + 1) * 128],
                        S["wuv"][:, h * DK:(h + 1) * DK],
                        start=True, stop=True), inc="tS")
                    tk = w.tok("tS")
                    w.wait("vector", "tS", tk)
                    w.wait("vector", "vS", tok["bcasts_v"])
                    w.do("vector", lambda v, tt_=tt, b_=b: v.tensor_tensor(
                        v_h[:, tt_, :], pb[:, b_ * 512:b_ * 512 + DK],
                        S["bUVb"][:, h * DK:(h + 1) * DK], ALU.add), inc="vS")
                    tok[("pj", h, pidx)] = ("vS", w.tok("vS"))
                    pidx += 1
                tok[("proj_done", h)] = ("vS", w.tok("vS"), "sS", w.tok("sS"))

                for qb in range(N5):
                    qg = h * N5 + qb
                    p2 = qg % 2
                    ob = 2 if p2 == 0 else 6
                    zb = 3 if p2 == 0 else 7
                    w.wait("tensor", "vS", tok[("proj_done", h)][1])
                    w.wait("tensor", "sS", tok[("proj_done", h)][3])
                    for kt in range(NT + 2):
                        ko = kt - 2
                        if ko >= 0:
                            w.wait("tensor", "sS", tok[("exp_s", h, qb, ko)])
                        if kt == 0:
                            w.wait("tensor", "vS", tok.get(("oev_v", qg - 2),
                                                            tok.get("stt_prev_v", 0)))
                            w.wait("tensor", "sS", tok.get("exp_prev_last"))
                        if ko >= 0:
                            w.do("tensor", lambda t, kt_=ko, ob_=ob: t.matmul(
                                bank(ob_)[:, :], v_h[:, kt_, :], Ebuf[:, kt_ % 3, :],
                                start=(kt_ == 0), stop=(kt_ == NT - 1)),
                                inc="tS" if ko == NT - 1 else None)
                            w.do("tensor", lambda t, kt_=ko, zb_=zb: t.matmul(
                                bank(zb_)[:, :], S["onesm"][:], Ebuf[:, kt_ % 3, :],
                                start=(kt_ == 0), stop=(kt_ == NT - 1)),
                                inc="tS" if ko == NT - 1 else None)
                        if kt < NT:
                            b = (0, 1, 5)[kt % 3]
                            w.do("tensor", lambda t, kt_=kt, b_=b, qb_=qb: t.matmul(
                                bank(b_)[:, :], S["kr"][:, kt_ * 128:(kt_ + 1) * 128],
                                qr_h[:, qb_ * 512:(qb_ + 1) * 512],
                                start=True, stop=False))
                            w.do("tensor", lambda t, kt_=kt, b_=b, qb_=qb: t.matmul(
                                bank(b_)[:, :], kc_h[:, kt_ * 128:(kt_ + 1) * 128],
                                qc_h[:, qb_ * 512:(qb_ + 1) * 512],
                                start=False, stop=True), inc="tS")
                            tk = w.tok("tS")
                            w.wait("scalar", "tS", tk)
                            w.do("scalar", lambda s, kt_=kt, b_=b: s.activation(
                                out=Ebuf[:, kt_ % 3, :], in_=bank(b_)[:, :],
                                func=AF.Exp, scale=SC), inc="sS")
                            tok[("exp_s", h, qb, kt)] = w.tok("sS")
                        if kt == 5:
                            flush_przb()
                    tok[("oz_t", qg)] = w.tok("tS")
                    tok["exp_prev_last"] = tok[("exp_s", h, qb, NT - 1)]
                    w.wait("vector", "tS", tok[("oz_t", qg)])
                    w.do("vector", lambda v, zb_=zb: v.reciprocal(
                        rzv, pb[0:1, zb_ * 512:(zb_ + 1) * 512]), inc="vS")
                    tok["pend"] = (h, qb, qg, ob, w.tok("vS"))
            flush_przb()
            tok["attn_ov"] = w.tok("vS")
            tok["exp_all_s"] = w.tok("sS")

            widx = 0
            w.wait("tensor", "qk", tok["wo_attn_qk"])
            w.wait("vector", "sS", tok[("resid_v", min(7, NT - 1))])
            for qt in range(NT):
                for dh in range(2):
                    b = 5 + widx % 2
                    prev = tok.get(("woev_v", widx - 2))
                    w.wait("tensor", "vS", prev if prev else tok["attn_ov"])
                    if NT > 8 and qt == 8 and dh == 0:
                        w.wait("vector", "sS", tok[("resid_v", NT - 1)])
                    for c in range(8):
                        w.do("tensor", lambda t, qt_=qt, dh_=dh, c_=c, b_=b: t.matmul(
                            bank(b_)[:, :], oT[:, c_, qt_ * 128:(qt_ + 1) * 128],
                            WO[:, c_, dh_ * 512:(dh_ + 1) * 512],
                            start=(c_ == 0), stop=False))
                    w.do("tensor", lambda t, dh_=dh, b_=b: t.matmul(
                        bank(b_)[:, :], S["onesm"][0:1, :],
                        S["borow"][0:1, dh_ * 512:(dh_ + 1) * 512],
                        start=False, stop=True), inc="tS")
                    tk = w.tok("tS")
                    w.wait("vector", "tS", tk)
                    w.do("vector", lambda v, qt_=qt, dh_=dh, b_=b: v.tensor_tensor(
                        S["resid"].ap()[:, qt_, dh_ * 512:(dh_ + 1) * 512],
                        S["resid"].ap()[:, qt_, dh_ * 512:(dh_ + 1) * 512],
                        bank(b_)[:, :], ALU.add), inc="vS")
                    tok[("woev_v", widx)] = w.tok("vS")
                    if widx == 2 * NTT - 1:
                        tok["woev_tb0_v"] = w.tok("vS")
                    if NT > 8 and widx < 8:
                        i_t = 8 + widx
                        bp_, j_ = (i_t // 4) % 2, i_t % 4
                        if j_ < 2:
                            w.wait("vector", "sS", tok[("xn_s", i_t)])
                            w.do("vector", lambda v, i2=i_t, b2=bp_, j2=j_: v.tensor_scalar(
                                S["resid"].ap()[:, i2, :], S["resid"].ap()[:, i2, :],
                                stw[:, b2, j2, 0:1], stw[:, b2, j2, 6:7],
                                ALU.mult, ALU.add), inc="vS")
                            tok[("resid_v", i_t)] = w.tok("vS")
                        else:
                            w.wait("scalar", "vS", tok[("cols_v", i_t // 4)])
                            w.do("scalar", lambda s, i2=i_t, b2=bp_, j2=j_: s.activation(
                                out=S["resid"].ap()[:, i2, :], in_=S["resid"].ap()[:, i2, :],
                                func=AF.Identity, bias=stw[:, b2, j2, 6:7],
                                scale=stw[:, b2, j2, 0:1]), inc="sS")
                            tok[("resid_v", i_t)] = w.tok("sS")
                    widx += 1
            tok["WO_t"] = w.tok("tS")
            tok["WO_v"] = w.tok("vS")


            tok["psum_vfree"] = tok["WO_v"]
            tok["psum_sfree"] = tok["exp_all_s"]
            tok["area_a_tfree"] = tok["WO_t"]
            tok["areaC_tfree"] = tok["WO_t"]
            for tb2 in range(1, NTB):
                do_ffn_tb(tb2)
            w.wait("sync", "vS", max(tok["woev_tb0_v"], tok[("stt_v", 0, E - 1, NHALF - 1)]))
            w.do("sync", lambda q: q.dma_start(
                out=dy.rearrange("(i p) d -> p i d", p=128)[:, 0:NTT, :],
                in_=S["resid"].ap()[:, 0:NTT, :]),
                inc="qo", k=16)
            for tb2 in range(1, NTB):
                for hf in range(NHALF):
                    w.wait("sync", "vS", max(tok[("stt_v", tb2, E - 1, hf)], tok["WO_v"]))
                    w.do("sync", lambda q, tb_=tb2, hf_=hf: q.dma_start(
                        out=dy.rearrange("(i p) d -> p i d", p=128)
                            [:, tb_ * NTT + hf_ * 4:tb_ * NTT + hf_ * 4 + 4, :],
                        in_=S["resid"].ap()[:, tb_ * NTT + hf_ * 4:tb_ * NTT + hf_ * 4 + 4, :]),
                        inc="qo", k=16)
            w.wait("sync", "qo", 16 * (1 + (NTB - 1) * NHALF))

        finals = {}
        for name in ["sync", "gpsimd", "tensor", "vector", "scalar"]:
            deco = getattr(block, name)

            def mk(nm):
                def fn(px):
                    ww = Walk(nm, px, sems)
                    walk(ww)
                    finals[nm] = dict(ww.c)
                return fn
            deco(mk(name))
        ref_c = finals["sync"]
        for nm, c in finals.items():
            assert c == ref_c, (nm, {k: (c[k], ref_c[k]) for k in c if c[k] != ref_c[k]})

    return nc


def make_in_map(x_core, inputs):
    f32 = np.float32
    m = {"x": np.ascontiguousarray(np.asarray(x_core, f32))}
    key = {
        "wdq": "L0_attn_W_DQ", "bdq": "L0_attn_b_DQ", "wuq": "L0_attn_W_UQ", "buq": "L0_attn_b_UQ",
        "wdkv": "L0_attn_W_DKV", "bdkv": "L0_attn_b_DKV", "wuk": "L0_attn_W_UK", "buk": "L0_attn_b_UK",
        "wuv": "L0_attn_W_UV", "buv": "L0_attn_b_UV", "wqr": "L0_attn_W_QR", "bqr": "L0_attn_b_QR",
        "wkr": "L0_attn_W_KR", "bkr": "L0_attn_b_KR", "wo_attn": "L0_attn_W_O", "bo": "L0_attn_b_O",
        "rw": "L0_ffn_router_w", "rb": "L0_ffn_router_b", "temp": "L0_ffn_temperature",
        "wg": "L0_ffn_wg", "wp": "L0_ffn_wp", "wexp": "L0_ffn_wo",
    }
    for k, src in key.items():
        m[k] = np.ascontiguousarray(np.asarray(inputs[src], f32))
    m["eyeb"] = np.eye(128, dtype=BF)
    m["eyef"] = np.eye(8, dtype=f32)
    m["onesr"] = np.ones((1, 128), f32)
    m["onescb"] = np.ones((128, 1), BF)
    m["onescf"] = np.ones((128, 1), f32)
    m["onesm"] = np.ones((128, 128), BF)
    return m


def kernel(**inputs):
    from concourse.bass_utils import run_bass_kernel_spmd
    x = np.asarray(inputs["L0_attn_h_t"], np.float32)
    B, T, _ = x.shape
    nc = build_nc(T)
    in_maps = [make_in_map(x[b], inputs) for b in range(B)]
    res = run_bass_kernel_spmd(nc, in_maps, core_ids=list(range(B)))
    return np.stack([np.asarray(r["y"], np.float32) for r in res.results], axis=0)
